# version 31
# baseline (speedup 1.0000x reference)
import numpy as np
from contextlib import ExitStack, contextmanager

import concourse.bass as bass
import concourse.mybir as mybir
from concourse.bass_utils import run_bass_kernel_spmd

F32 = mybir.dt.float32
BF16 = mybir.dt.bfloat16
F16 = mybir.dt.float16
I32 = mybir.dt.int32
I8 = mybir.dt.int8
AF = mybir.ActivationFunctionType
ALU = mybir.AluOpType
AX = mybir.AxisListType

P = 128
T = 2048
D = 1024
NF = D // P
NCORES = 8
S = 256
NCH = T // S
LN_EPS = 1e-5
RMS_EPS = 1e-6
LOG1EM7 = -16.11809565095832


class K:
    STREAMS = ("pe", "dv", "ac", "pl", "qs")

    def __init__(self):
        self.nc = bass.Bass("TRN2", target_bir_lowering=False)
        self.es = ExitStack()
        self.plan = {e: [] for e in self.STREAMS}
        self.cnt = {s: 0 for s in self.STREAMS}
        self.sem = {}
        self.hw = {e: {} for e in self.STREAMS}
        self._scopes = []

    def open(self):
        for s in self.STREAMS:
            self.sem[s] = self.es.enter_context(self.nc.semaphore("sem_" + s))

    def dma_sem(self, key):
        if key not in self.sem:
            self.sem[key] = self.es.enter_context(self.nc.semaphore(key))
            self.cnt[key] = 0
        return self.sem[key]

    def _owner(self):
        return self._scopes[-1] if self._scopes else self.es

    def sb(self, name, shape, dt):
        return self._owner().enter_context(self.nc.sbuf_tensor(name, shape, dt))

    def ps(self, name, shape, dt=F32):
        return self._owner().enter_context(self.nc.psum_tensor(name, shape, dt))

    @contextmanager
    def scope(self):
        s = ExitStack()
        self._scopes.append(s)
        try:
            yield s
        finally:
            self._scopes.pop()
            s.close()

    def wait(self, eng, *tokens):
        best = {}
        for tok in tokens:
            if tok is None:
                continue
            s, v = tok
            if v > best.get(s, 0):
                best[s] = v
        for s, v in best.items():
            if v <= 0 or self.hw[eng].get(s, 0) >= v:
                continue
            self.hw[eng][s] = v
            sem = self.sem[s]

            def run(h, sem=sem, v=v):
                h.wait_ge(sem, v)

            self.plan[eng].append(run)

    def op(self, eng, fn, after=()):
        self.wait(eng, *after)
        self.cnt[eng] += 1
        n = self.cnt[eng]
        sem = self.sem[eng]

        def run(h, fn=fn, sem=sem):
            fn(h).then_inc(sem, 1)

        self.plan[eng].append(run)
        return (eng, n)

    def dma(self, queue, key, out, in_, after=(), **kw):
        stream = "qs" if queue == "qs" else "pl"
        sem = self.dma_sem(key)
        self.wait(stream, *after)
        self.cnt[key] += 16
        n = self.cnt[key]

        def run(h, out=out, in_=in_, sem=sem, kw=kw):
            h.dma_start(out=out, in_=in_, **kw).then_inc(sem, 16)

        self.plan[stream].append(run)
        return (key, n)

    def snap(self):
        return [(s, v) for s, v in self.cnt.items() if v > 0]

    def join(self):
        toks = self.snap()
        for e in self.STREAMS:
            self.wait(e, *toks)

    def finish(self, final_tokens):
        self.wait("qs", *final_tokens)

    def replay(self):
        blk = self.es.enter_context(self.nc.Block())
        plans = self.plan

        @blk.tensor
        def _(h):
            for f in plans["pe"]:
                f(h)

        @blk.vector
        def _(h):
            for f in plans["dv"]:
                f(h)

        @blk.scalar
        def _(h):
            for f in plans["ac"]:
                f(h)

        @blk.gpsimd
        def _(h):
            for f in plans["pl"]:
                f(h)

        @blk.sync
        def _(h):
            for f in plans["qs"]:
                f(h)


def row1(ap):
    return ap.rearrange("(a d) -> a d", a=1)



def emit_consts(k):
    c = {}
    c["ones_row_bf"] = k.sb("ones_row_bf", [1, 512], BF16)
    c["ones_col_bf"] = k.sb("ones_col_bf", [128, 1], BF16)
    c["ones_row_f"] = k.sb("ones_row_f", [1, 128], F32)
    c["ident_f"] = k.sb("ident_f", [128, 128], F32)
    t1 = k.op("pl", lambda h: h.memset(c["ones_row_bf"][:], 1.0))
    t2 = k.op("pl", lambda h: h.memset(c["ones_col_bf"][:], 1.0))
    t3 = k.op("pl", lambda h: h.memset(c["ones_row_f"][:], 1.0))
    tz = k.op("pl", lambda h: h.memset(c["ident_f"][:], 0.0))
    t4 = k.op("pl", lambda h: h.affine_select(
        out=c["ident_f"][:], in_=c["ident_f"][:], compare_op=ALU.not_equal,
        fill=1.0, base=0, pattern=[[-1, 128]], channel_multiplier=1), after=[tz])
    c["cpool"] = k.sb("cpool", [1, 4], F32)
    c["ccols"] = k.sb("ccols", [P, 4], F32)
    t5 = k.op("pl", lambda h: h.memset(c["cpool"][0:1, 0:1], RMS_EPS))
    t6 = k.op("pl", lambda h: h.memset(c["cpool"][0:1, 1:2], LN_EPS))
    t7 = k.op("pl", lambda h: h.memset(c["cpool"][0:1, 2:3], 1.0))
    t8 = k.op("pl", lambda h: h.memset(c["ccols"][:, 0:1], 1e-6))
    t9 = k.op("pl", lambda h: h.memset(c["ccols"][:, 1:2], 1.0))
    t10 = k.op("pl", lambda h: h.memset(c["ccols"][:, 2:3], LN_EPS))
    c["tok"] = [t1, t2, t3, t4, t5, t6, t7, t8, t9, t10]
    return c


def vec_to_cols(k, c, row_ap, ncols, psbank, dst, dst_col0, after, guard):
    mm_toks = []
    for f in range(ncols):
        mm_toks.append(k.op("pe", lambda h, f=f: h.matmul(
            psbank[:, f:f + 1], row_ap[0:1, f * 128:(f + 1) * 128],
            c["ones_row_f"][0:1, 0:1], start=(f == 0), stop=(f == ncols - 1),
            skip_group_check=True),
            after=list(after) + c["tok"] + ([guard] if f == 0 else [])))
    return k.op("dv", lambda h: h.tensor_copy(dst[:, dst_col0:dst_col0 + ncols],
                                              psbank[:, 0:ncols]), after=mm_toks)



def emit_load_stream_fm(k, c, src_ap, xfm):
    with k.scope():
        stag = [k.sb(f"ldstag{i}", [P, D], F32) for i in range(2)]
        psT = [k.ps(f"ldpsT{i}", [P, 512], F32) for i in range(2)]
        dma_tok, tr_tok, ev_tok = {}, {}, {}
        for tb in range(T // P):
            b = tb % 2
            after = [tr_tok.get((tb - 2, f)) for f in range(NF)]
            dma_tok[tb] = k.dma("qs", f"q_ld{b}", stag[b][:],
                                src_ap[tb * P:(tb + 1) * P, :], after=after)
            for f in range(NF):
                bank, slot = f // 4, f % 4
                tr_tok[(tb, f)] = k.op(
                    "pe", lambda h, b=b, f=f, bank=bank, slot=slot:
                    h.transpose(psT[bank][:, slot * 128:(slot + 1) * 128],
                                stag[b][:, f * 128:(f + 1) * 128], c["ident_f"][:]),
                    after=[dma_tok[tb]] + [ev_tok.get((tb - 1, bank * 4 + s))
                                           for s in range(4)] + c["tok"])
            for f in range(NF):
                bank, slot = f // 4, f % 4
                ev_tok[(tb, f)] = k.op(
                    "dv", lambda h, f=f, bank=bank, slot=slot, tb=tb:
                    h.tensor_copy(xfm[f][:, tb * P:(tb + 1) * P],
                                  psT[bank][:, slot * 128:(slot + 1) * 128]),
                    after=[tr_tok[(tb, bank * 4 + 3)]])
        k.join()


def emit_store_stream_fm(k, c, xfm, dst_ap):
    out_toks = []
    with k.scope():
        stag = [k.sb(f"ststag{i}", [P, D], F32) for i in range(2)]
        psT = [k.ps(f"stpsT{i}", [P, 512], F32) for i in range(2)]
        dma_tok, tr_tok, ev_tok = {}, {}, {}
        for tb in range(T // P):
            b = tb % 2
            for f in range(NF):
                bank, slot = f // 4, f % 4
                tr_tok[(tb, f)] = k.op(
                    "pe", lambda h, f=f, tb=tb, bank=bank, slot=slot:
                    h.transpose(psT[bank][:, slot * 128:(slot + 1) * 128],
                                xfm[f][:, tb * P:(tb + 1) * P], c["ident_f"][:]),
                    after=[ev_tok.get((tb - 1, bank * 4 + s)) for s in range(4)]
                    + [dma_tok.get(tb - 2)] + c["tok"])
            for f in range(NF):
                bank, slot = f // 4, f % 4
                ev_tok[(tb, f)] = k.op(
                    "dv", lambda h, b=b, f=f, bank=bank, slot=slot:
                    h.tensor_copy(stag[b][:, f * 128:(f + 1) * 128],
                                  psT[bank][:, slot * 128:(slot + 1) * 128]),
                    after=[tr_tok[(tb, bank * 4 + 3)]])
            dma_tok[tb] = k.dma("qs", f"q_st{b}", dst_ap[tb * P:(tb + 1) * P, :],
                                stag[b][:],
                                after=[ev_tok[(tb, f)] for f in range(NF)])
            out_toks.append(dma_tok[tb])
    return out_toks



def emit_rms_chunk(k, c, src_view, w_cols, wcol0, out_view, sq, psS, psBC,
                   row_scr, rstd_bc, guard, extra_after=()):
    sq_toks = []
    for f in range(NF):
        sq_toks.append(k.op("ac", lambda h, f=f: h.activation(
            sq[:, f * S:(f + 1) * S], src_view(f), AF.Square),
            after=list(extra_after)))
    mm = []
    for f in range(NF):
        mm.append(k.op("pe", lambda h, f=f: h.matmul(
            psS[0:1, 0:S], c["ones_col_bf"][:, 0:1], sq[:, f * S:(f + 1) * S],
            start=(f == 0), stop=(f == NF - 1), skip_group_check=True),
            after=[sq_toks[f]] + c["tok"] + ([guard.get("psS")] if f == 0 else [])))
    t = k.op("ac", lambda h: h.activation(row_scr[0:1, 0:S], psS[0:1, 0:S],
                                          AF.Sqrt, bias=c["cpool"][0:1, 0:1],
                                          scale=1.0 / D),
             after=[mm[-1]] + c["tok"])
    guard["psS"] = t
    t = k.op("dv", lambda h: h.reciprocal(row_scr[0:1, 0:S], row_scr[0:1, 0:S]),
             after=[t])
    t = k.op("pe", lambda h: h.matmul(psBC[:, 0:S], c["ones_row_f"][0:1, :],
                                      row_scr[0:1, 0:S], start=True, stop=True,
                                      skip_group_check=True),
             after=[t, guard.get("psBC")])
    t = k.op("dv", lambda h: h.tensor_copy(rstd_bc[:, 0:S], psBC[:, 0:S]), after=[t])
    guard["psBC"] = t
    out_toks = []
    for f in range(NF):
        out_toks.append(k.op("dv", lambda h, f=f: h.scalar_tensor_tensor(
            out_view(f), src_view(f), w_cols[:, wcol0 + f:wcol0 + f + 1],
            rstd_bc[:, 0:S], ALU.mult, ALU.mult),
            after=[t] + list(extra_after)))
    return out_toks


def emit_ln_chunk(k, c, xfm, cs, sq, ybf, psS, psBC, row_scr, ab_bc, tmp,
                  guard, in_toks=()):
    cast_toks, sq_toks = [], []
    for f in range(NF):
        ct = k.op("ac", lambda h, f=f: h.activation(
            ybf[:, f * S:(f + 1) * S], xfm[f][:, cs], AF.Copy), after=list(in_toks))
        cast_toks.append(ct)
        sq_toks.append(k.op("ac", lambda h, f=f: h.activation(
            sq[:, f * S:(f + 1) * S], ybf[:, f * S:(f + 1) * S], AF.Square),
            after=[ct]))
    mm = []
    nmm = 2 * NF
    i = 0
    for f in range(NF):
        mm.append(k.op("pe", lambda h, f=f, i=i: h.matmul(
            psS[0:1, 0:S], c["ones_col_bf"][:, 0:1], sq[:, f * S:(f + 1) * S],
            start=(i == 0), stop=(i == nmm - 1), skip_group_check=True),
            after=[sq_toks[f]] + c["tok"] + ([guard.get("psS")] if i == 0 else [])))
        i += 1
        mm.append(k.op("pe", lambda h, f=f, i=i: h.matmul(
            psS[0:1, S:2 * S], c["ones_col_bf"][:, 0:1], ybf[:, f * S:(f + 1) * S],
            start=(i == 0), stop=(i == nmm - 1), skip_group_check=True),
            after=[cast_toks[f]]))
        i += 1
    rR = row_scr[0:1, 0:256]
    rM = row_scr[0:1, 256:512]
    rV = row_scr[0:1, 512:768]
    rQ = row_scr[0:1, 768:1024]
    t = k.op("ac", lambda h: h.activation(rM[:, 0:S], psS[0:1, S:2 * S], AF.Copy,
                                          scale=1.0 / D), after=[mm[-1]])
    t = k.op("ac", lambda h: h.activation(rQ[:, 0:S], rM[:, 0:S], AF.Square),
             after=[t])
    t = k.op("dv", lambda h: h.scalar_tensor_tensor(
        rV[:, 0:S], psS[0:1, 0:S], 1.0 / D, rQ[:, 0:S], ALU.mult, ALU.subtract),
        after=[t])
    guard["psS"] = t
    t = k.op("ac", lambda h: h.activation(rR[:, 0:S], rV[:, 0:S], AF.Sqrt,
                                          bias=c["cpool"][0:1, 1:2]),
             after=[t] + c["tok"])
    t = k.op("dv", lambda h: h.reciprocal(rR[:, 0:S], rR[:, 0:S]), after=[t])
    tb = k.op("dv", lambda h: h.scalar_tensor_tensor(
        rM[:, 0:S], rM[:, 0:S], -1.0, rR[:, 0:S], ALU.mult, ALU.mult), after=[t])
    t1 = k.op("pe", lambda h: h.matmul(psBC[:, 0:S], c["ones_row_f"][0:1, :],
                                       rR[:, 0:S], start=True, stop=False,
                                       skip_group_check=True),
              after=[tb, guard.get("psBC")])
    t2 = k.op("pe", lambda h: h.matmul(psBC[:, S:2 * S], c["ones_row_f"][0:1, :],
                                       rM[:, 0:S], start=False, stop=True,
                                       skip_group_check=True), after=[tb])
    t = k.op("dv", lambda h: h.tensor_copy(ab_bc[:, 0:2 * S], psBC[:, 0:2 * S]),
             after=[t1, t2])
    guard["psBC"] = t
    out_toks = []
    prev2 = None
    for f in range(NF):
        tt = k.op("dv", lambda h, f=f: h.tensor_tensor(
            tmp[:, 0:S], xfm[f][:, cs], ab_bc[:, 0:S], ALU.mult),
            after=[t, prev2] + list(in_toks))
        prev2 = k.op("dv", lambda h, f=f: h.tensor_tensor(
            xfm[f][:, cs], tmp[:, 0:S], ab_bc[:, S:2 * S], ALU.add), after=[tt])
        out_toks.append(prev2)
    return out_toks



def emit_b0(k, c, xfm, w):
    import os
    B0_PHASE = int(os.environ.get("B0_PHASE", "99"))
    NCH_MIX = int(os.environ.get("B0_NCH", str(NCH)))
    NCH_FFN = int(os.environ.get("B0_FFN", str(NCH)))
    nc = k.nc
    cols = k.sb("b0_cols", [P, 32], F32)
    cw_cols = k.sb("b0_cwcols", [P, 56], F32)
    brow = k.sb("b0_brow", [1, 2048], BF16)
    row_scr = k.sb("b0_rowscr", [1, 1024], F32)
    rstd_bc = k.sb("b0_rstdbc", [P, 256], F32)
    carry = k.sb("b0_carry", [P, 16], F32)
    guard = {}

    with k.scope():
        psS = k.ps("b0_psS", [P, 512])
        psBC = k.ps("b0_psBC", [P, 512])
        vrow = k.sb("b0_vrow", [1, 3 * D], F32)
        srow = k.sb("b0_srow", [1, 2], F32)
        k.dma("qs", "q_b0c", vrow[0:1, 0:D], row1(w["norm1_w"]))
        k.dma("qs", "q_b0c", vrow[0:1, D:2 * D], row1(w["norm2_w"]))
        k.dma("qs", "q_b0c", vrow[0:1, 2 * D:3 * D], row1(w["bc"]))
        k.dma("qs", "q_b0c", srow[0:1, 0:1], row1(w["state_scale"]))
        k.dma("qs", "q_b0c", srow[0:1, 1:2], row1(w["mem_scale"]))
        t_bd = k.dma("qg", "q_b0cg", brow[0:1, 0:D], row1(w["bdecay"]))
        t_bg = k.dma("qg", "q_b0cg2", brow[0:1, D:2 * D], row1(w["bgate"]))
        t_cv = k.dma("qs", "q_b0c", cw_cols[:, :].rearrange("p (f j) -> p f j", j=7),
                     w["conv_w"].rearrange("(f p) j -> p f j", p=128))
        g1 = vec_to_cols(k, c, vrow[0:1, 0:D], 8, psS, cols, 0, [t_cv], None)
        g2 = vec_to_cols(k, c, vrow[0:1, D:2 * D], 8, psS, cols, 8, [t_cv], g1)
        g3 = vec_to_cols(k, c, vrow[0:1, 2 * D:3 * D], 8, psS, cols, 16, [t_cv], g2)
        guard["psS"] = g3
        t = k.op("pe", lambda h: h.matmul(psBC[:, 0:2], c["ones_row_f"][0:1, :],
                                          srow[0:1, 0:2], start=True, stop=True,
                                          skip_group_check=True),
                 after=[t_cv] + c["tok"])
        guard["psBC"] = k.op("dv", lambda h: h.tensor_copy(cols[:, 24:26], psBC[:, 0:2]),
                             after=[t])
        k.op("pl", lambda h: h.memset(carry[:], 0.0))
        k.join()

    with k.scope():
        h32 = k.sb("b0_h32", [P, NF * 264], BF16)
        cvt = k.sb("b0_cvt", [P, NF * S], BF16)
        sq = k.sb("b0_sq", [P, NF * S], BF16)
        SA = k.sb("b0_SA", [P, NF * S], F32)
        SB = k.sb("b0_SB", [P, NF * S], F32)
        SC = k.sb("b0_SC", [P, NF * S], F32)
        local_bf = k.sb("b0_local", [P, 2 * NF * S], BF16)
        state_bf = k.sb("b0_state", [P, NF * S], BF16)
        mem_bf = k.sb("b0_mem", [P, NF * S], BF16)
        k_bf = k.sb("b0_kbf", [P, 512], BF16)
        mo_bf = k.sb("b0_mobf", [P, 1024], BF16)
        wd_sb = k.sb("b0_wd", [P, 8192], BF16)
        wg_sb = k.sb("b0_wg", [P, 8192], BF16)
        wk_sb = k.sb("b0_wk", [P, 2048], BF16)
        mt_sb = k.sb("b0_mt", [P, 512], BF16)
        wmem_sb = k.sb("b0_wmem", [P, 2048], BF16)
        wc_sb = [k.sb(f"b0_wc{i}", [P, 8192], BF16) for i in range(2)]
        psA = [k.ps(f"b0_psA{i}", [P, 512]) for i in range(4)]
        psDG = [k.ps(f"b0_psDG{i}", [P, 512]) for i in range(2)]
        psB0 = k.ps("b0_psB0", [P, 512])
        psB1 = k.ps("b0_psB1", [P, 512])

        h3 = lambda off, w: h32[:].rearrange("p (f r) -> p f r", r=264)[:, :, off:off + w]
        SA3 = lambda: SA[:].rearrange("p (f s) -> p f s", s=S)
        SB3 = lambda: SB[:].rearrange("p (f s) -> p f s", s=S)
        SC3 = lambda: SC[:].rearrange("p (f s) -> p f s", s=S)
        cvt3 = lambda: cvt[:].rearrange("p (f s) -> p f s", s=S)
        car3 = lambda o: carry[:, o:o + 8].rearrange("p (f j) -> p f j", j=1)
        cw_bc = lambda j: cw_cols[:].rearrange("p (f j) -> p f j", j=7)[
            :, :, j:j + 1].broadcast_to([P, NF, S])

        def ld_kxm(key, dst, src2d, m):
            return k.dma("qg", key, dst[:].rearrange("p (a m) -> p a m", m=m),
                         src2d.rearrange("(a p) m -> p a m", p=128))

        t_wd = ld_kxm("q_bw0", wd_sb, w["Wdecay"], 1024)
        t_wg = ld_kxm("q_bw1", wg_sb, w["Wgate"], 1024)
        t_wk = ld_kxm("q_bw2", wk_sb, w["Wk"], 256)
        t_mt = ld_kxm("q_bw3", mt_sb, w["M_T"], 256)
        t_wm = ld_kxm("q_bw4", wmem_sb, w["Wmem"], 1024)

        wc_dma = {}
        wc_user = {}

        def load_wc(idx):
            ci, third = divmod(idx, 3)
            buf = idx % 2
            wc_dma[idx] = k.dma(
                "qg", f"q_wc{buf}", wc_sb[buf][:].rearrange("p (a m) -> p a m", m=1024),
                w["Wc"][third * 1024:(third + 1) * 1024, :].rearrange(
                    "(a p) m -> p a m", p=128),
                after=[wc_user.get(idx - 2)])

        load_wc(0)
        load_wc(1)

        t_tail0 = [k.op("pl", lambda h, f=f: h.memset(h32[:, f * 264 + 2:f * 264 + 8], 0.0))
                   for f in range(NF)]

        prev_h_toks = None
        prev_hreaders = []
        kmm_prev = None
        evA_prev = [None] * 4
        evB0_prev = None
        evB1_prev = None
        bf_users_prev = []
        dg_guard = [None, None]
        prev_state = None
        sc_guard = None
        mx_out_toks = []
        pend = None
        pend_d = {}

        def emit_memB(pc):
            nonlocal bf_users_prev
            d = pend_d[pc]
            po = d["po"]
            mem_last = {}
            for p in range(4):
                i = 0
                for half in range(2):
                    m = 2 * p + half
                    for kk in range(2):
                        aft = [d["t_mobf"], t_wm]
                        if i == 0:
                            aft.append(evA_prev[p])
                        tok = k.op("pe", lambda h, p=p, half=half, m=m, kk=kk,
                                   i=i, po=po: h.matmul(
                                       psA[p][:, half * 256:half * 256 + 256],
                                       wmem_sb[:, kk * 1024 + m * 128:kk * 1024 + (m + 1) * 128],
                                       mo_bf[:, po + kk * 256:po + (kk + 1) * 256],
                                       start=(i == 0), stop=(i == 3),
                                       skip_group_check=True), after=aft)
                        i += 1
                mem_last[p] = tok
            mem_toks = [k.op("ac", lambda h, p=p: h.activation(
                mem_bf[:, p * 512:(p + 1) * 512], psA[p][:], AF.Copy),
                after=[mem_last[p]] + bf_users_prev) for p in range(4)]

            srcs = [local_bf, state_bf, mem_bf]
            offs = [d["lp"], 0, 0]
            src_tok = [d["local"], d["state"], mem_toks]
            comb_last = {}
            for third in range(3):
                idx = pc * 3 + third
                buf = idx % 2
                dma_t = wc_dma[idx]
                for p in range(4):
                    for half in range(2):
                        m = 2 * p + half
                        for kk in range(8):
                            first = (third == 0 and half == 0 and kk == 0)
                            last = (third == 2 and half == 1 and kk == 7)
                            aft = [dma_t] if kk == 0 else []
                            if first:
                                aft = aft + [mem_toks[p]]
                            if kk == 0:
                                aft = aft + [src_tok[third][p]]
                            tok = k.op("pe", lambda h, buf=buf, p=p, half=half, m=m,
                                       kk=kk, third=third, first=first, last=last,
                                       off=offs[third]:
                                       h.matmul(
                                           psA[p][:, half * 256:half * 256 + 256],
                                           wc_sb[buf][:, kk * 1024 + m * 128:kk * 1024 + (m + 1) * 128],
                                           srcs[third][:, off + kk * S:off + (kk + 1) * S],
                                           start=first, stop=last,
                                           skip_group_check=True), after=aft)
                        comb_last[(third, m)] = tok
                wc_user[idx] = tok
                if idx + 2 < NCH * 3:
                    load_wc(idx + 2)
            d["comb_last"] = comb_last
            bf_users_prev = [comb_last[(2, 7)]]

        def emit_mixed(pc):
            nonlocal evA_prev, sc_guard
            d = pend_d.pop(pc)
            comb_last = d["comb_last"]
            cs = d["cs"]
            mixed_toks = {}
            evA = [None] * 4
            for m in range(NF):
                p, half = m // 2, m % 2
                tok = k.op("dv", lambda h, m=m, p=p, half=half, cs=cs:
                           h.scalar_tensor_tensor(
                               SB[:, m * S:(m + 1) * S],
                               psA[p][:, half * 256:half * 256 + 256],
                               cols[:, 16 + m:17 + m], xfm[m][:, cs],
                               ALU.add, ALU.add),
                           after=[comb_last[(2, 2 * p)], comb_last[(2, 2 * p + 1)],
                                  sc_guard])
                mixed_toks[m] = tok
                evA[p] = tok
            t_mxout = k.dma("qs", "q_mx", w["mx_scr"][:, :, cs],
                            SB[:].rearrange("p (f s) -> p f s", s=S),
                            after=list(mixed_toks.values()))
            mx_out_toks.append(t_mxout)
            evA_prev = evA
            sc_guard = t_mxout

        for ci in range(NCH_MIX):
            cs = slice(ci * S, (ci + 1) * S)
            po = (ci % 2) * 512
            if ci > 0:
                tail_toks = [k.op("dv", lambda h: h.tensor_copy(
                    h3(2, 6), h3(258, 6)),
                    after=list(prev_h_toks) + prev_hreaders)]
            else:
                tail_toks = t_tail0

            h_toks = emit_rms_chunk(
                k, c, lambda f, cs=cs: xfm[f][:, cs], cols, 0,
                lambda f: h32[:, f * 264 + 8:f * 264 + 264],
                sq, psB0, psB1, row_scr, rstd_bc, guard,
                extra_after=tail_toks + prev_hreaders)

            sigA, sigC = [], []
            for wsb, brow_off, dst, w_tok, sig_list in (
                    (wd_sb, 0, SA, t_wd, sigA), (wg_sb, D, SC, t_wg, sigC)):
                for p in range(4):
                    b = p % 2
                    i = 0
                    last_mm = None
                    for half in range(2):
                        m = 2 * p + half
                        for kk in range(8):
                            aft = []
                            if i == 0:
                                aft = list(h_toks) + [dg_guard[b], w_tok]
                            last_mm = k.op("pe", lambda h, b=b, half=half, m=m,
                                           kk=kk, wsb=wsb, i=i: h.matmul(
                                               psDG[b][:, half * 256:half * 256 + 256],
                                               wsb[:, kk * 1024 + m * 128:kk * 1024 + (m + 1) * 128],
                                               h32[:, kk * 264 + 8:kk * 264 + 264],
                                               start=(i == 0), stop=False,
                                               skip_group_check=True), after=aft)
                            i += 1
                        last_mm = k.op("pe", lambda h, b=b, half=half, m=m,
                                       brow_off=brow_off, i=i: h.matmul(
                                           psDG[b][:, half * 256:half * 256 + 256],
                                           brow[0:1, brow_off + m * 128:brow_off + (m + 1) * 128],
                                           c["ones_row_bf"][0:1, 0:256],
                                           start=False, stop=(i == 17),
                                           skip_group_check=True),
                                       after=[t_bd, t_bg])
                        i += 1
                    tok = k.op("ac", lambda h, b=b, p=p, dst=dst: h.activation(
                        dst[:, p * 512:(p + 1) * 512], psDG[b][:], AF.Sigmoid),
                        after=[last_mm, prev_state])
                    dg_guard[b] = tok
                    sig_list.append(tok)

            kmm = []
            i = 0
            for m2 in range(2):
                for kk in range(8):
                    aft = []
                    if i == 0:
                        aft = [t_wk, evB0_prev, guard.get("psS")] + list(h_toks)
                    kmm.append(k.op("pe", lambda h, m2=m2, kk=kk, i=i: h.matmul(
                        psB0[:, m2 * 256:m2 * 256 + 256],
                        wk_sb[:, kk * 256 + m2 * 128:kk * 256 + (m2 + 1) * 128],
                        h32[:, kk * 264 + 8:kk * 264 + 264],
                        start=(i == 0), stop=(i == 15), skip_group_check=True),
                        after=aft))
                    i += 1
            t_kbf = k.op("ac", lambda h: h.activation(k_bf[:], psB0[:], AF.Copy),
                         after=[kmm[-1]])
            guard["psS"] = t_kbf
            evB0_prev = t_kbf
            momm = []
            i = 0
            for m2 in range(2):
                for kk in range(2):
                    aft = [t_kbf, t_mt]
                    if i == 0:
                        aft.append(evB1_prev)
                        aft.append(guard.get("psBC"))
                    momm.append(k.op("pe", lambda h, m2=m2, kk=kk, i=i: h.matmul(
                        psB1[:, m2 * 256:m2 * 256 + 256],
                        mt_sb[:, kk * 256 + m2 * 128:kk * 256 + (m2 + 1) * 128],
                        k_bf[:, kk * 256:(kk + 1) * 256],
                        start=(i == 0), stop=(i == 3), skip_group_check=True),
                        after=aft))
                    i += 1

            t = k.op("dv", lambda h: h.tensor_tensor(
                SB3(), h3(2, S), cw_bc(0), ALU.mult),
                after=list(h_toks) + [sc_guard] + tail_toks + bf_users_prev)
            for j in range(1, 7):
                t1 = k.op("dv", lambda h, j=j: h.tensor_tensor(
                    cvt3(), h3(2 + j, S), cw_bc(j), ALU.mult),
                    after=list(h_toks) + [t])
                t = k.op("dv", lambda h: h.tensor_tensor(
                    SB3(), SB3(), cvt3(), ALU.add), after=[t, t1])
            conv_tok = t
            conv_toks = [conv_tok]
            t_mobf = k.op("dv", lambda h, po=po: h.tensor_scalar_mul(
                mo_bf[:, po:po + 512], psB1[:], cols[:, 25:26]), after=[momm[-1]])
            guard["psBC"] = t_mobf
            evB1_prev = t_mobf
            lp = (ci % 2) * NF * S
            local_tok = k.op("ac", lambda h, lp=lp: h.activation(
                local_bf[:, lp:lp + NF * S], SB[:], AF.Copy),
                after=[conv_tok] + bf_users_prev)
            if pend is not None:
                emit_memB(pend)

            clip_tok = k.op("dv", lambda h: h.tensor_scalar(
                SA[:], SA[:], 0.05, 0.95, ALU.max, ALU.min), after=sigA)
            ln_tok = k.op("ac", lambda h: h.activation(
                SA[:], SA[:], AF.Ln), after=[clip_tok])
            t0f = k.op("dv", lambda h: h.tensor_scalar_max(
                SB3()[:, :, 0:1], car3(0), LOG1EM7),
                after=[ln_tok, local_tok])
            scan_toks = []
            for m in range(NF):
                a0 = m * S
                scan_toks.append(k.op("dv", lambda h, m=m, a0=a0: h.tensor_tensor_scan(
                    SA[:, a0:a0 + S], SA[:, a0:a0 + S], SA[:, a0:a0 + S],
                    carry[:, m:m + 1], ALU.add, ALU.bypass),
                    after=[ln_tok, t0f]))
            t2f = k.op("dv", lambda h: h.tensor_scalar_max(
                SB3()[:, :, 1:S], SA3()[:, :, 0:S - 1], LOG1EM7),
                after=[scan_toks[-1], local_tok])
            t3f = k.op("dv", lambda h: h.tensor_copy(
                car3(0), SA3()[:, :, S - 1:S]), after=[scan_toks[-1]])
            einv_tok = k.op("ac", lambda h: h.activation(
                SB[:], SB[:], AF.Exp, scale=-1.0), after=[t0f, t2f])
            cd_tok = k.op("ac", lambda h: h.activation(
                SA[:], SA[:], AF.Exp), after=[t2f, t3f])
            upd_tok = k.op("dv", lambda h: h.tensor_tensor(
                SC3(), SC3(), h3(8, S), ALU.mult), after=sigC + list(h_toks))
            upd_toks = [upd_tok]
            wu_tok = k.op("dv", lambda h: h.tensor_tensor(
                SC[:], SC[:], SB[:], ALU.mult), after=[upd_tok, einv_tok])
            cwu_toks = []
            for m in range(NF):
                a0 = m * S
                cwu_toks.append(k.op("dv", lambda h, m=m, a0=a0: h.tensor_tensor_scan(
                    SC[:, a0:a0 + S], SC[:, a0:a0 + S], SC[:, a0:a0 + S],
                    carry[:, 8 + m:9 + m], ALU.add, ALU.bypass), after=[wu_tok]))
            k.op("dv", lambda h: h.tensor_copy(
                car3(8), SC3()[:, :, S - 1:S]), after=[cwu_toks[-1]])
            state_tok = k.op("dv", lambda h: h.scalar_tensor_tensor(
                state_bf[:], SC[:], cols[:, 24:25], SA[:], ALU.mult, ALU.mult),
                after=[cwu_toks[-1], cd_tok] + bf_users_prev)
            state_toks = [state_tok] * 4
            local_toks = [local_tok] * 4

            if pend is not None:
                emit_mixed(pend)
            pend = ci
            pend_d[ci] = {"cs": cs, "po": po, "t_mobf": t_mobf, "lp": lp,
                          "local": local_toks, "state": state_toks}
            prev_state = state_tok
            prev_h_toks = h_toks
            prev_hreaders = [conv_tok, upd_tok, kmm[-1]]
            kmm_prev = kmm[-1]
        emit_memB(pend)
        emit_mixed(pend)
        k.join()

    T2 = 4 * S
    with k.scope():
        h2all = k.sb("b0_h2all", [P, NF * T2], BF16)
        yacc = k.sb("b0_yacc", [P, NF * T2], F32)
        mx_in = [k.sb(f"b0_mxin{i}", [P, NF * S], F32) for i in range(2)]
        ab_bc = k.sb("b0_abbc", [P, 512], F32)
        tmp = k.sb("b0_tmp", [P, 256], F32)
        sq2 = k.sb("b0_sq2", [P, NF * S], BF16)
        silu_bf = k.sb("b0_silu", [P, 512], BF16)
        prod_bf = k.sb("b0_prod", [P, 1024], BF16)
        ybf = k.sb("b0_ybf", [P, NF * S], BF16)
        wgg = [k.sb(f"b0_wgg{i}", [P, 8 * 512], BF16) for i in range(2)]
        wuu = [k.sb(f"b0_wuu{i}", [P, 8 * 512], BF16) for i in range(2)]
        wdd = [k.sb(f"b0_wdd{i}", [P, 4 * 1024], BF16) for i in range(2)]
        psF = [k.ps(f"b0_psF{i}", [P, 512]) for i in range(4)]
        psG = k.ps("b0_psG", [P, 512])
        psU = k.ps("b0_psU", [P, 512])
        psS = k.ps("b0_psS2", [P, 512])
        psBC = k.ps("b0_psBC2", [P, 512])

        NB = 8
        wdma = {}
        wuser = {}

        def load_ffn_blk(idx):
            hi, bi = divmod(idx, NB)
            buf = idx % 2
            aft = [wuser.get(idx - 2)]
            ta = k.dma("qg", f"q_fba{buf}", wgg[buf][:].rearrange("p (a m) -> p a m", m=512),
                       w["Wg"][:, bi * 512:(bi + 1) * 512].rearrange(
                           "(a p) m -> p a m", p=128), after=aft)
            tb = k.dma("qg", f"q_fbb{buf}", wuu[buf][:].rearrange("p (a m) -> p a m", m=512),
                       w["Wup"][:, bi * 512:(bi + 1) * 512].rearrange(
                           "(a p) m -> p a m", p=128), after=aft)
            t3 = k.dma("qg", f"q_fbc{buf}", wdd[buf][:].rearrange("p (a m) -> p a m", m=1024),
                       w["Wdown"][bi * 512:(bi + 1) * 512, :].rearrange(
                           "(a p) m -> p a m", p=128), after=aft)
            wdma[idx] = (ta, tb, t3)

        load_ffn_blk(0)
        load_ffn_blk(1)

        yac3 = lambda p, tl: yacc[:].rearrange("p (f t) -> p f t", t=T2)[
            :, 2 * p:2 * p + 2, tl * S:tl * S + 256]
        psF3 = lambda p: psF[p][:].rearrange("p (m s) -> p m s", s=256)

        evF_prev = [None] * 4
        yacc_readers = []
        yacc_readers_prev = []
        mx_dma = {}
        mx_user = {}
        ph_pe = None
        silu_prev = None
        prod_prev = None
        dn_last = None
        pend = None
        ln_prev = []
        for hi in range(2):
            h2_tl = {}
            for tl in range(4):
                ti = hi * 4 + tl
                mb = ti % 2
                mx_dma[ti] = k.dma("qs", f"q_mxi{mb}",
                                   mx_in[mb][:].rearrange("p (f s) -> p f s", s=S),
                                   w["mx_scr"][:, :, ti * S:(ti + 1) * S],
                                   after=[mx_user.get(ti - 2)])
                h2_toks = emit_rms_chunk(
                    k, c, lambda f, mb=mb: mx_in[mb][:, f * S:(f + 1) * S], cols, 8,
                    lambda f, tl=tl: h2all[:, f * T2 + tl * S:f * T2 + tl * S + 256],
                    sq2, psS, psBC, row_scr, rstd_bc, guard,
                    extra_after=[mx_dma[ti], ph_pe])
                mx_user[ti] = h2_toks[-1]
                h2_tl[tl] = list(h2_toks)

            def emit_ph3(tl, ev, hi=hi):
                nonlocal ln_prev
                ti = hi * 4 + tl
                mb = ti % 2
                mx_dma[ti + 8] = k.dma(
                    "qs", f"q_mxi{mb}", mx_in[mb][:].rearrange("p (f s) -> p f s", s=S),
                    w["mx_scr"][:, :, ti * S:(ti + 1) * S],
                    after=[mx_user.get(ti + 2 if tl < 2 else ti + 6)])
                t_m = k.op("dv", lambda h, mb=mb, tl=tl: h.tensor_tensor(
                    mx_in[mb][:].rearrange("p (f s) -> p f s", s=S),
                    mx_in[mb][:].rearrange("p (f s) -> p f s", s=S),
                    yacc[:].rearrange("p (f t) -> p f t", t=T2)[:, :, tl * S:tl * S + 256],
                    ALU.add), after=[mx_dma[ti + 8]] + ev + ln_prev)
                yacc_readers.append(t_m)
                y1_toks = []
                for mo in range(NF):
                    tok = k.op("dv", lambda h, mo=mo, mb=mb, ti=ti: h.tensor_tensor(
                        xfm[mo][:, ti * S:(ti + 1) * S], xfm[mo][:, ti * S:(ti + 1) * S],
                        mx_in[mb][:, mo * S:(mo + 1) * S], ALU.add), after=[t_m])
                    y1_toks.append(tok)
                mx_user[ti + 8] = y1_toks[-1]
                ln_prev = emit_ln_chunk(k, c, xfm, slice(ti * S, (ti + 1) * S),
                                        sq2, ybf, psS, psBC, row_scr,
                                        ab_bc, tmp, guard, in_toks=y1_toks)

            def emit_dn(pend, hi=hi):
                nonlocal dn_last, evF_prev
                (p_prod, p_td, p_buf, p_bi, p_tl, p_sub, p_idx, p_first, p_last) = pend
                dn = []
                for half in range(2):
                    ht4 = (p_sub * 2 + half) % 4
                    for mo in range(NF):
                        p, mh = mo // 2, mo % 2
                        aft = [p_prod, p_td]
                        if p_first and half == 0 and mh == 0:
                            aft.append(evF_prev[p])
                        dn.append(k.op("pe", lambda h, ht4=ht4, mo=mo, p=p,
                                       mh=mh, buf=p_buf, half=half,
                                       fp=p_first, lp=p_last, ps=p_sub: h.matmul(
                                           psF[p][:, mh * 256:mh * 256 + 256],
                                           wdd[buf][:, ht4 * 1024 + mo * 128:ht4 * 1024 + (mo + 1) * 128],
                                           prod_bf[:, (ps % 2) * 512 + half * 256:(ps % 2) * 512 + half * 256 + 256],
                                           start=(fp and half == 0 and mh == 0),
                                           stop=(lp and half == 1 and mh == 1),
                                           skip_group_check=True), after=aft))
                dn_last = dn[-1]
                if p_sub == 1:
                    ev = []
                    for p in range(4):
                        if p_bi == 0:
                            ev.append(k.op("dv", lambda h, p=p, tl=p_tl:
                                           h.tensor_copy(yac3(p, tl), psF3(p)),
                                           after=[dn_last] + yacc_readers_prev))
                        else:
                            ev.append(k.op("dv", lambda h, p=p, tl=p_tl:
                                           h.tensor_tensor(yac3(p, tl), yac3(p, tl),
                                                           psF3(p), ALU.add),
                                           after=[dn_last]))
                    evF_prev = ev
                    if p_tl == 3:
                        wuser[p_idx] = dn_last
                        if p_idx + 2 < 2 * NB:
                            load_ffn_blk(p_idx + 2)
                    if p_bi == NB - 1:
                        emit_ph3(p_tl, ev)

            for bi in range(NB):
                idx = hi * NB + bi
                buf = idx % 2
                tg, tu, td = wdma[idx]
                for tl in range(4):
                    for sub in range(2):
                        gm = []
                        for half in range(2):
                            ht4 = (sub * 2 + half) % 4
                            pcol = half * 256
                            for kk in range(8):
                                i = half * 8 + kk
                                aft = []
                                if i == 0:
                                    aft = [tg, silu_prev] + (h2_tl[tl] if bi == 0 else [])
                                gm.append(k.op("pe", lambda h, ht4=ht4, kk=kk, buf=buf,
                                               pcol=pcol, i=i, tl=tl: h.matmul(
                                                   psG[:, pcol:pcol + 256],
                                                   wgg[buf][:, kk * 512 + ht4 * 128:kk * 512 + (ht4 + 1) * 128],
                                                   h2all[:, kk * T2 + tl * S:kk * T2 + tl * S + 256],
                                                   start=(i == 0), stop=(i == 15),
                                                   skip_group_check=True), after=aft))
                        um = []
                        for half in range(2):
                            ht4 = (sub * 2 + half) % 4
                            pcol = half * 256
                            for kk in range(8):
                                i = half * 8 + kk
                                aft = []
                                if i == 0:
                                    aft = [tu, prod_prev]
                                um.append(k.op("pe", lambda h, ht4=ht4, kk=kk, buf=buf,
                                               pcol=pcol, i=i, tl=tl: h.matmul(
                                                   psU[:, pcol:pcol + 256],
                                                   wuu[buf][:, kk * 512 + ht4 * 128:kk * 512 + (ht4 + 1) * 128],
                                                   h2all[:, kk * T2 + tl * S:kk * T2 + tl * S + 256],
                                                   start=(i == 0), stop=(i == 15),
                                                   skip_group_check=True), after=aft))
                        silu_now = k.op("ac", lambda h: h.activation(
                            silu_bf[:], psG[:], AF.Silu),
                            after=[gm[-1], prod_prev])
                        prod_now = k.op("dv", lambda h, sub=sub: h.tensor_tensor(
                            prod_bf[:, (sub % 2) * 512:(sub % 2) * 512 + 512],
                            psU[:], silu_bf[:], ALU.mult),
                            after=[silu_now, um[-1], dn_last])
                        if pend is not None:
                            emit_dn(pend)
                        pend = (prod_now, td, buf, bi, tl, sub, idx,
                                sub == 0, sub == 1)
                        silu_prev = silu_now
                        prod_prev = prod_now
            emit_dn(pend)
            pend = None
            ph_pe = ("pe", k.cnt["pe"])
            yacc_readers_prev = yacc_readers
            yacc_readers = []
            for tl in range(4):
                mx_user[hi * 4 + tl] = mx_user[hi * 4 + tl + 8]
        k.join()





def emit_b1(k, c, xfm, w):
    import os
    NCH_B1 = int(os.environ.get("B1_NCH", str(NCH)))
    SCALE = 1.0 / 127.0
    with k.scope():
        row_scr = k.sb("b1_rowscr", [1, 1024], F32)
        ab_bc = k.sb("b1_abbc", [P, 512], F32)
        tmp = k.sb("b1_tmp", [P, 256], F32)
        sq2 = k.sb("b1_sq", [P, NF * S], BF16)
        ybf = k.sb("b1_ybf", [P, NF * S], BF16)
        x1all = k.sb("b1_x1all", [P, NF * T], BF16)
        yacc = k.sb("b1_yacc", [P, NF * 4 * S], F32)
        silu_bf = k.sb("b1_silu", [P, 512], BF16)
        prod_bf = k.sb("b1_prod", [P, 1024], BF16)
        wgg = [k.sb(f"b1_wgg{i}", [P, 8 * 512], BF16) for i in range(2)]
        wuu = [k.sb(f"b1_wuu{i}", [P, 8 * 512], BF16) for i in range(2)]
        wdd = [k.sb(f"b1_wdd{i}", [P, 4 * 1024], BF16) for i in range(2)]
        psS = k.ps("b1_psS", [P, 512])
        psBC = k.ps("b1_psBC", [P, 512])
        psF = [k.ps(f"b1_psF{i}", [P, 512]) for i in range(4)]
        psG = k.ps("b1_psG", [P, 512])
        psU = k.ps("b1_psU", [P, 512])
        guard = {}

        NB = 6
        wdma = {}
        wuser = {}

        def load_blk(idx):
            ti, bi = divmod(idx, NB)
            buf = idx % 2
            aft = [wuser.get(idx - 2)]
            ta = k.dma("qg", f"q_b1fa{buf}", wgg[buf][:].rearrange("p (a m) -> p a m", m=512),
                       w["wgT"][:, bi * 512:(bi + 1) * 512].rearrange(
                           "(a p) m -> p a m", p=128), after=aft)
            tb = k.dma("qg", f"q_b1fb{buf}", wuu[buf][:].rearrange("p (a m) -> p a m", m=512),
                       w["wuT"][:, bi * 512:(bi + 1) * 512].rearrange(
                           "(a p) m -> p a m", p=128), after=aft)
            t3 = k.dma("qg", f"q_b1fc{buf}", wdd[buf][:].rearrange("p (a m) -> p a m", m=1024),
                       w["wdT"][bi * 512:(bi + 1) * 512, :].rearrange(
                           "(a p) m -> p a m", p=128), after=aft)
            wdma[idx] = (ta, tb, t3)

        load_blk(0)
        load_blk(1)

        T2 = 4 * S
        yac3 = lambda p, tl: yacc[:].rearrange("p (f t) -> p f t", t=T2)[
            :, 2 * p:2 * p + 2, tl * S:tl * S + 256]
        psF3 = lambda p: psF[p][:].rearrange("p (m s) -> p m s", s=256)

        evF_prev = [None] * 4
        yacc_readers = []
        yacc_readers_prev = []
        ph_pe = None
        silu_prev = None
        prod_prev = None
        dn_last = None
        pend = None
        ln_prev = []
        cast_all = [k.op("ac", lambda h, f=f: h.activation(
            x1all[:, f * T:(f + 1) * T], xfm[f][:, 0:T],
            AF.Copy, scale=SCALE)) for f in range(NF)]
        for hi in range(2):
            cast_toks = cast_all

            def emit_ph3(tl, ev, hi=hi):
                nonlocal ln_prev
                ti = hi * 4 + tl
                cs = slice(ti * S, (ti + 1) * S)
                y2_toks = []
                for mo in range(NF):
                    tok = k.op("dv", lambda h, mo=mo, tl=tl, cs=cs: h.tensor_tensor(
                        xfm[mo][:, cs], xfm[mo][:, cs],
                        yacc[:, mo * T2 + tl * S:mo * T2 + tl * S + 256], ALU.add),
                        after=[ev[mo // 2]] + ln_prev)
                    y2_toks.append(tok)
                yacc_readers.extend(y2_toks)
                ln_prev = emit_ln_chunk(k, c, xfm, cs, sq2, ybf, psS, psBC, row_scr,
                                        ab_bc, tmp, guard, in_toks=y2_toks)

            def emit_dn(pend, hi=hi):
                nonlocal dn_last, evF_prev
                (p_prod, p_td, p_buf, p_bi, p_tl, p_sub, p_idx, p_first, p_last) = pend
                dn = []
                for half in range(2):
                    ht4 = (p_sub * 2 + half)
                    for mo in range(NF):
                        p, mh = mo // 2, mo % 2
                        aft = [p_prod, p_td]
                        if p_first and half == 0 and mh == 0:
                            aft.append(evF_prev[p])
                        dn.append(k.op("pe", lambda h, ht4=ht4, mo=mo, p=p,
                                       mh=mh, buf=p_buf, half=half,
                                       fp=p_first, lp=p_last, ps=p_sub: h.matmul(
                                           psF[p][:, mh * 256:mh * 256 + 256],
                                           wdd[buf][:, ht4 * 1024 + mo * 128:ht4 * 1024 + (mo + 1) * 128],
                                           prod_bf[:, (ps % 2) * 512 + half * 256:(ps % 2) * 512 + half * 256 + 256],
                                           start=(fp and half == 0 and mh == 0),
                                           stop=(lp and half == 1 and mh == 1),
                                           skip_group_check=True), after=aft))
                dn_last = dn[-1]
                if p_sub == 1:
                    ev = []
                    for p in range(4):
                        if p_bi == 0:
                            ev.append(k.op("dv", lambda h, p=p, tl=p_tl:
                                           h.tensor_copy(yac3(p, tl), psF3(p)),
                                           after=[dn_last] + yacc_readers_prev))
                        else:
                            ev.append(k.op("dv", lambda h, p=p, tl=p_tl:
                                           h.tensor_tensor(yac3(p, tl), yac3(p, tl),
                                                           psF3(p), ALU.add),
                                           after=[dn_last]))
                    evF_prev = ev
                    if p_tl == 3:
                        wuser[p_idx] = dn_last
                        if p_idx + 2 < 2 * NB:
                            load_blk(p_idx + 2)
                    if p_bi == NB - 1:
                        emit_ph3(p_tl, ev)

            for bi in range(NB):
                idx = hi * NB + bi
                buf = idx % 2
                ta_, tb_, td = wdma[idx]
                for tl in range(4):
                    for sub in range(2):
                        gm = []
                        for half in range(2):
                            ht4 = (sub * 2 + half)
                            pcol = half * 256
                            for kk in range(8):
                                i = half * 8 + kk
                                aft = []
                                if i == 0:
                                    aft = [ta_, silu_prev] + (cast_toks if bi == 0 else [])
                                gm.append(k.op("pe", lambda h, ht4=ht4, kk=kk, buf=buf,
                                               pcol=pcol, i=i, tl=tl, hi=hi: h.matmul(
                                                   psG[:, pcol:pcol + 256],
                                                   wgg[buf][:, kk * 512 + ht4 * 128:kk * 512 + (ht4 + 1) * 128],
                                                   x1all[:, kk * T + hi * T2 + tl * S:kk * T + hi * T2 + tl * S + 256],
                                                   start=(i == 0), stop=(i == 15),
                                                   skip_group_check=True), after=aft))
                        um = []
                        for half in range(2):
                            ht4 = (sub * 2 + half)
                            pcol = half * 256
                            for kk in range(8):
                                i = half * 8 + kk
                                aft = []
                                if i == 0:
                                    aft = [tb_, prod_prev]
                                um.append(k.op("pe", lambda h, ht4=ht4, kk=kk, buf=buf,
                                               pcol=pcol, i=i, tl=tl, hi=hi: h.matmul(
                                                   psU[:, pcol:pcol + 256],
                                                   wuu[buf][:, kk * 512 + ht4 * 128:kk * 512 + (ht4 + 1) * 128],
                                                   x1all[:, kk * T + hi * T2 + tl * S:kk * T + hi * T2 + tl * S + 256],
                                                   start=(i == 0), stop=(i == 15),
                                                   skip_group_check=True), after=aft))
                        silu_now = k.op("ac", lambda h: h.activation(
                            silu_bf[:], psG[:], AF.Silu), after=[gm[-1], prod_prev])
                        prod_now = k.op("dv", lambda h, sub=sub: h.scalar_tensor_tensor(
                            prod_bf[:, (sub % 2) * 512:(sub % 2) * 512 + 512],
                            psU[:], SCALE, silu_bf[:], ALU.mult, ALU.mult),
                            after=[silu_now, um[-1], dn_last])
                        if pend is not None:
                            emit_dn(pend)
                        pend = (prod_now, td, buf, bi, tl, sub, idx,
                                sub == 0, sub == 1)
                        silu_prev = silu_now
                        prod_prev = prod_now
            emit_dn(pend)
            pend = None
            ph_pe = ("pe", k.cnt["pe"])
            yacc_readers_prev = yacc_readers
            yacc_readers = []
        k.join()


B1_SHAPES = {
    "wgT": ([D, 3072], I8), "wuT": ([D, 3072], I8), "wdT": ([3072, D], I8),
}


def build_b1_program():
    k = K()
    nc = k.nc
    x_in = nc.dram_tensor("x_in", [T, D], F32, kind="ExternalInput").ap()
    w = {n: nc.dram_tensor("w_" + n, B1_SHAPES[n][0], B1_SHAPES[n][1],
                           kind="ExternalInput").ap() for n in B1_SHAPES}
    x_out = nc.dram_tensor("x_out", [T, D], F32, kind="ExternalOutput").ap()
    with k.es:
        k.open()
        c = emit_consts(k)
        xfm = [k.sb(f"xfm{f}", [P, T], F32) for f in range(NF)]
        emit_load_stream_fm(k, c, x_in, xfm)
        emit_b1(k, c, xfm, w)
        out_toks = emit_store_stream_fm(k, c, xfm, x_out)
        k.finish(out_toks)
        k.replay()
    return k.nc


def _b1_inputs(inputs, ci, x_cur):
    gi = lambda n: np.asarray(inputs["L0_ffn_" + n], dtype=np.int8)
    return {
        **({"x_in": np.ascontiguousarray(np.asarray(x_cur, dtype=np.float32))}
           if x_cur is not None else {}),
        "w_wgT": np.ascontiguousarray(gi("w_gate").T),
        "w_wuT": np.ascontiguousarray(gi("w_up").T),
        "w_wdT": np.ascontiguousarray(gi("w_down").T),
    }





def emit_b2(k, c, xfm, w):
    import os
    NH = int(os.environ.get("B2_NH", "16"))
    NQC = int(os.environ.get("B2_NQC", "4"))
    with k.scope():
        qfm = [k.sb(f"b2_q{f}", [P, T], F16) for f in range(NF)]
        kfm = [k.sb(f"b2_k{f}", [P, T], F16) for f in range(NF)]
        att = qfm
        vt = [k.sb(f"b2_v{t}", [P, 16 * 65], BF16) for t in range(16)]
        alicol = k.sb("b2_alicol", [P, 16 * 16], F32)
        iota2d = k.sb("b2_iota2d", [P, 16], F32)
        iota_row = k.sb("b2_iotarow", [1, T], F16)
        ident_bf = k.sb("b2_identbf", [P, 128], F16)
        ones64 = k.sb("b2_ones64", [P, 64], F32)
        ones_h = k.sb("b2_onesh", [1, 128], F16)
        qkb = k.sb("b2_qkb", [1, 128], F32)
        psS = k.ps("b2_psS", [P, 512])
        psBC = k.ps("b2_psBC", [P, 512])
        guard = {}

        t_i2 = k.op("pl", lambda h: h.iota(iota2d[:, 0:16], pattern=[[128, 16]],
                                           base=0, channel_multiplier=1,
                                           allow_small_or_imprecise_dtypes=True))
        t_ir = k.op("pl", lambda h: h.iota(iota_row[0:1, 0:T], pattern=[[1, T]],
                                           base=0, channel_multiplier=0,
                                           allow_small_or_imprecise_dtypes=True))
        t_on = k.op("pl", lambda h: h.memset(ones64[:], 1.0))
        t_oh = k.op("pl", lambda h: h.memset(ones_h[:], 1.0))
        tz = k.op("pl", lambda h: h.memset(ident_bf[:], 0.0))
        t_id = k.op("pl", lambda h: h.affine_select(
            out=ident_bf[:], in_=ident_bf[:], compare_op=ALU.not_equal,
            fill=1.0, base=0, pattern=[[-1, 128]], channel_multiplier=1), after=[tz])
        ali_toks = []
        for hh in range(16):
            slope = 2.0 ** (-0.5 * (hh + 1))
            ali_toks.append(k.op("dv", lambda h, hh=hh, slope=slope: h.tensor_scalar_mul(
                alicol[:, hh * 16:(hh + 1) * 16], iota2d[:, 0:16], slope),
                after=[t_i2]))
        consts = [t_i2, t_ir, t_on, t_oh, t_id] + ali_toks + c["tok"]

        with k.scope():
            x3bf = k.sb("b2_x3bf", [P, NF * 1024], F16)
            wbuf = k.sb("b2_wbuf", [P, 8192], F16)
            psq = [k.ps(f"b2_psq{i}", [P, 512]) for i in range(4)]
            bank_guard = [None] * 4
            w_names = ["q_w", "k_w", "v_w"]

            def load_w(j, aft):
                return k.dma("qg", "q_b2w", wbuf[:].rearrange("p (a m) -> p a m", m=1024),
                             w[w_names[j]].rearrange("(a p) m -> p a m", p=128),
                             after=aft)

            t_w = load_w(0, [])
            i = 0
            last_mm = None
            pe_hw = None
            ones_done = {}
            for j in range(3):
                for ht in range(2):
                    cast_toks = [k.op("ac", lambda h, f=f, ht=ht: h.activation(
                        x3bf[:, f * 1024:(f + 1) * 1024],
                        xfm[f][:, ht * 1024:(ht + 1) * 1024], AF.Copy),
                        after=[pe_hw]) for f in range(NF)]
                    if j < 2:
                        dst = qfm if j == 0 else kfm
                        scale = 0.125 if j == 0 else 1.0
                        for mo in range(NF):
                            for nkk in range(2):
                                nk = ht * 2 + nkk
                                bank = i % 4
                                mms = []
                                for kk in range(NF):
                                    aft = ([bank_guard[bank], t_w] + cast_toks) if kk == 0 else []
                                    mms.append(k.op("pe", lambda h, mo=mo, nkk=nkk, kk=kk,
                                                    bank=bank: h.matmul(
                                                        psq[bank][:],
                                                        wbuf[:, kk * 1024 + mo * 128:kk * 1024 + (mo + 1) * 128],
                                                        x3bf[:, kk * 1024 + nkk * 512:kk * 1024 + (nkk + 1) * 512],
                                                        start=(kk == 0), stop=(kk == NF - 1),
                                                        skip_group_check=True), after=aft))
                                bank_guard[bank] = k.op(
                                    "ac", lambda h, mo=mo, nk=nk, bank=bank, dst=dst,
                                    scale=scale: h.activation(
                                        dst[mo][:, nk * 512:(nk + 1) * 512],
                                        psq[bank][:], AF.Copy, scale=scale),
                                    after=[mms[-1]])
                                last_mm = mms[-1]
                                i += 1
                    else:
                        for tt8 in range(8):
                            tt = ht * 8 + tt8
                            for oc in range(2):
                                bank = i % 4
                                mms = []
                                for kk in range(NF):
                                    aft = ([bank_guard[bank], t_w] + cast_toks) if kk == 0 else []
                                    mms.append(k.op("pe", lambda h, tt8=tt8, oc=oc, kk=kk,
                                                    bank=bank: h.matmul(
                                                        psq[bank][:],
                                                        x3bf[:, kk * 1024 + tt8 * 128:kk * 1024 + (tt8 + 1) * 128],
                                                        wbuf[:, kk * 1024 + oc * 512:kk * 1024 + (oc + 1) * 512],
                                                        start=(kk == 0), stop=(kk == NF - 1),
                                                        skip_group_check=True), after=aft))
                                tok = k.op("dv", lambda h, tt=tt, oc=oc, bank=bank: h.tensor_copy(
                                    vt[tt][:].rearrange("p (g cc) -> p g cc", cc=65)[:, oc * 8:(oc + 1) * 8, 0:64],
                                    psq[bank][:].rearrange("p (g cc) -> p g cc", cc=64)),
                                    after=[mms[-1]])
                                bank_guard[bank] = tok
                                last_mm = mms[-1]
                                i += 1
                            ones_done[tt] = k.op("pl", lambda h, tt=tt: h.memset(
                                vt[tt][:].rearrange("p (g cc) -> p g cc", cc=65)[:, 0:16, 64:65], 1.0))
                    pe_hw = ("pe", k.cnt["pe"])
                if j + 1 < 3:
                    t_w = load_w(j + 1, [last_mm])
            k.join()

        with k.scope():
            sqn = k.sb("b2_sqn", [P, T], BF16)
            psN = k.ps("b2_psN", [P, 2048])
            cell = k.sb("b2_cell", [1, 8], F32)
            gN = None
            for f in range(NF):
                sq_k = k.op("ac", lambda h, f=f: h.activation(sqn[:], kfm[f][:], AF.Square))
                for half in range(2):
                    hh = 2 * f + half
                    base = 64 * half
                    mms = []
                    for nk in range(4):
                        mms.append(k.op("pe", lambda h, base=base, nk=nk: h.matmul(
                            psN[0:1, nk * 512:(nk + 1) * 512],
                            c["ones_col_bf"][base:base + 64, 0:1],
                            sqn[base:base + 64, nk * 512:(nk + 1) * 512],
                            start=True, stop=True, skip_group_check=True),
                            after=[sq_k, gN] + consts))
                    t = k.op("dv", lambda h: h.reduce_max(cell[0:1, 0:1], psN[0:1, 0:2048],
                                                          axis=AX.X), after=[mms[-1]])
                    gN = k.op("ac", lambda h, hh=hh: h.activation(
                        qkb[0:1, 64 + hh:65 + hh], cell[0:1, 0:1], AF.Sqrt), after=[t])
                sq_q = k.op("ac", lambda h, f=f: h.activation(sqn[:], qfm[f][:], AF.Square),
                            after=[gN])
                for half in range(2):
                    hh = 2 * f + half
                    base = 64 * half
                    mms = []
                    for nk in range(4):
                        mms.append(k.op("pe", lambda h, base=base, nk=nk: h.matmul(
                            psN[0:1, nk * 512:(nk + 1) * 512],
                            c["ones_col_bf"][base:base + 64, 0:1],
                            sqn[base:base + 64, nk * 512:(nk + 1) * 512],
                            start=True, stop=True, skip_group_check=True),
                            after=[sq_q, gN]))
                    for qc in range(4):
                        t = k.op("dv", lambda h, qc=qc: h.reduce_max(
                            cell[0:1, 1 + qc:2 + qc], psN[0:1, qc * 512:(qc + 1) * 512],
                            axis=AX.X), after=[mms[-1]])
                        t = k.op("ac", lambda h, qc=qc: h.activation(
                            cell[0:1, 1 + qc:2 + qc], cell[0:1, 1 + qc:2 + qc], AF.Sqrt),
                            after=[t])
                        gN = k.op("dv", lambda h, hh=hh, qc=qc: h.tensor_scalar(
                            qkb[0:1, hh * 4 + qc:hh * 4 + qc + 1],
                            cell[0:1, 1 + qc:2 + qc], qkb[0:1, 64 + hh:65 + hh],
                            6.907755, ALU.mult, ALU.subtract), after=[t])
            k.join()

        with k.scope():
            pt = k.sb("b2_pt", [P, 2 * 512], BF16)
            masks = k.sb("b2_masks", [P, 4 * 512], F16)
            negm = k.sb("b2_negm", [1, 2 * 512], F16)
            lrow = k.sb("b2_lrow", [P, 512], F32)
            bcast_sb = k.sb("b2_bcast", [P, 512], F32)
            mask_toks = []
            for i in range(4):
                tz2 = k.op("pl", lambda h, i=i: h.memset(masks[:, i * 512:(i + 1) * 512], 0.0))
                mask_toks.append(k.op("pl", lambda h, i=i: h.affine_select(
                    out=masks[:, i * 512:(i + 1) * 512],
                    in_=masks[:, i * 512:(i + 1) * 512],
                    compare_op=ALU.is_ge, fill=-30000.0, base=-i * 128,
                    pattern=[[1, 512]], channel_multiplier=-1), after=[tz2]))
            psST = [k.ps(f"b2_psST{i}", [P, 512]) for i in range(2)]
            psAV = [k.ps(f"b2_psAV{i}", [P, 512]) for i in range(2)]
            psL = k.ps("b2_psL", [P, 512])
            exp_prev = {}
            av_read_prev = {}
            negm_user = {}
            psl_user = None
            pend_norm = None

            def emit_norm(pn):
                nonlocal psl_user
                (p_avb, p_f, p_base, p_qc, p_av) = pn
                t = k.op("dv", lambda h, avb=p_avb: h.tensor_copy(
                    lrow[64:65, 0:512], psAV[avb][64:65, :]), after=[p_av])
                t = k.op("dv", lambda h: h.reciprocal(lrow[64:65, 0:512],
                                                      lrow[64:65, 0:512]), after=[t])
                t = k.op("pe", lambda h: h.matmul(
                    psL[0:64, :], ones64[64:65, 0:64], lrow[64:65, 0:512],
                    start=True, stop=True, skip_group_check=True),
                    after=[t, psl_user])
                t = k.op("dv", lambda h: h.tensor_copy(bcast_sb[0:64, :],
                                                       psL[0:64, :]), after=[t])
                psl_user = t
                t = k.op("dv", lambda h, f=p_f, base=p_base, qc=p_qc, avb=p_avb:
                         h.tensor_tensor(
                             att[f][base:base + 64, qc * 512:(qc + 1) * 512],
                             psAV[avb][0:64, :], bcast_sb[0:64, :], ALU.mult),
                         after=[t, p_av])
                av_read_prev[p_avb] = t

            it = 0
            for hh in range(NH):
                f, base = hh // 2, 64 * (hh % 2)
                slope = 2.0 ** (-0.5 * (hh + 1))
                for qc in range(NQC):
                    nkb = qc * 4 + 4
                    slot = it % 2
                    avb = it % 2
                    t_negm = k.op("dv", lambda h, hh=hh, qc=qc, slope=slope, slot=slot:
                                  h.tensor_scalar(
                                      negm[0:1, slot * 512:(slot + 1) * 512],
                                      iota_row[0:1, qc * 512:(qc + 1) * 512],
                                      -slope, qkb[0:1, hh * 4 + qc:hh * 4 + qc + 1],
                                      ALU.mult, ALU.subtract),
                                  after=[negm_user.get(slot)] + consts)
                    av_last = None
                    pend_av = None
                    for kb in range(nkb):
                        b = kb % 2
                        mm1 = k.op("pe", lambda h, f=f, base=base, kb=kb, qc=qc, b=b:
                                   h.matmul(psST[b][:],
                                            kfm[f][base:base + 64, kb * 128:(kb + 1) * 128],
                                            qfm[f][base:base + 64, qc * 512:(qc + 1) * 512],
                                            start=True, stop=False, skip_group_check=True),
                                   after=[exp_prev.get(b), t_negm])
                        has_mask = kb >= qc * 4
                        mm2 = k.op("pe", lambda h, slot=slot, b=b, hm=has_mask: h.matmul(
                            psST[b][:], ones_h[0:1, 0:128],
                            negm[0:1, slot * 512:(slot + 1) * 512],
                            start=False, stop=(not hm), skip_group_check=True))
                        last_mm = mm2
                        if has_mask:
                            off = kb - qc * 4
                            last_mm = k.op("pe", lambda h, off=off, b=b: h.matmul(
                                psST[b][:], ident_bf[:],
                                masks[:, off * 512:(off + 1) * 512],
                                start=False, stop=True, skip_group_check=True),
                                after=mask_toks)
                        if pend_av is not None:
                            av_last = k.op("pe", pend_av[0], after=pend_av[1])
                        if kb == 2 and pend_norm is not None:
                            emit_norm(pend_norm)
                            pend_norm = None
                        t_exp = k.op("ac", lambda h, b=b, hh=hh, kb=kb: h.activation(
                            pt[:, b * 512:(b + 1) * 512], psST[b][:], AF.Exp,
                            bias=alicol[:, hh * 16 + kb:hh * 16 + kb + 1]),
                            after=[last_mm])
                        exp_prev[b] = t_exp
                        pend_av = (
                            (lambda h, kb=kb, hh=hh, b=b, avb=avb, nkb=nkb: h.matmul(
                                psAV[avb][0:65, :],
                                vt[kb][:, hh * 65:(hh + 1) * 65],
                                pt[:, b * 512:(b + 1) * 512],
                                start=(kb == 0), stop=(kb == nkb - 1),
                                skip_group_check=True)),
                            [t_exp] + ([av_read_prev.get(avb)] if kb == 0 else [])
                            + [ones_done[kb]])
                    av_last = k.op("pe", pend_av[0], after=pend_av[1])
                    pend_norm = (avb, f, base, qc, av_last)
                    negm_user[slot] = mm2
                    it += 1
            emit_norm(pend_norm)
            pend_norm = None
            k.join()

        with k.scope():
            row_scr = k.sb("b2_rowscr", [1, 1024], F32)
            ab_bc = k.sb("b2_abbc", [P, 512], F32)
            tmp = k.sb("b2_tmp", [P, 256], F32)
            sqb = k.sb("b2_sqb", [P, NF * S], BF16)
            ybf = k.sb("b2_ybf", [P, NF * S], BF16)
            pso = [k.ps(f"b2_pso{i}", [P, 512]) for i in range(4)]
            ev_prev = [None] * 4
            wo2 = k.sb("b2_wo2", [P, 8192], F16)
            t_wo = k.dma("qg", "q_b2w2", wo2[:].rearrange("p (a m) -> p a m", m=1024),
                         w["o_w"].rearrange("(a p) m -> p a m", p=128))
            ln_prev = []
            pend_ln = None
            for ti in range(NCH):
                cs = slice(ti * S, (ti + 1) * S)
                ev = []
                for mo in range(NF):
                    pp, mh = mo // 2, mo % 2
                    mms = []
                    for kk in range(NF):
                        aft = [t_wo, ev_prev[pp]] if kk == 0 else []
                        mms.append(k.op("pe", lambda h, mo=mo, ti=ti, kk=kk, pp=pp, mh=mh:
                                        h.matmul(
                                            pso[pp][:, mh * 256:mh * 256 + 256],
                                            wo2[:, kk * 1024 + mo * 128:kk * 1024 + (mo + 1) * 128],
                                            att[kk][:, ti * S:(ti + 1) * S],
                                            start=(kk == 0), stop=(kk == NF - 1),
                                            skip_group_check=True), after=aft))
                    tok = k.op("dv", lambda h, mo=mo, ti=ti, pp=pp, mh=mh: h.tensor_tensor(
                        xfm[mo][:, ti * S:(ti + 1) * S],
                        xfm[mo][:, ti * S:(ti + 1) * S],
                        pso[pp][:, mh * 256:mh * 256 + 256], ALU.add),
                        after=[mms[-1]])
                    ev.append(tok)
                    ev_prev[pp] = tok
                if pend_ln is not None:
                    ln_prev = emit_ln_chunk(k, c, xfm, pend_ln[0], sqb, ybf, psS,
                                            psBC, row_scr, ab_bc, tmp, guard,
                                            in_toks=pend_ln[1] + ln_prev)
                pend_ln = (cs, ev)
            ln_prev = emit_ln_chunk(k, c, xfm, pend_ln[0], sqb, ybf, psS, psBC,
                                    row_scr, ab_bc, tmp, guard,
                                    in_toks=pend_ln[1] + ln_prev)
            k.join()


B2_SHAPES = {"q_w": [D, D], "k_w": [D, D], "v_w": [D, D], "o_w": [D, D]}


def build_b2_program():
    k = K()
    nc = k.nc
    x_in = nc.dram_tensor("x_in", [T, D], F32, kind="ExternalInput").ap()
    w = {n: nc.dram_tensor("w_" + n, B2_SHAPES[n], F32, kind="ExternalInput").ap()
         for n in B2_SHAPES}
    x_out = nc.dram_tensor("x_out", [T, D], F32, kind="ExternalOutput").ap()
    with k.es:
        k.open()
        c = emit_consts(k)
        xfm = [k.sb(f"xfm{f}", [P, T], F32) for f in range(NF)]
        emit_load_stream_fm(k, c, x_in, xfm)
        emit_b2(k, c, xfm, w)
        out_toks = emit_store_stream_fm(k, c, xfm, x_out)
        k.finish(out_toks)
        k.replay()
    return k.nc


def _b2_inputs(inputs, ci, x_cur):
    g = lambda n: np.ascontiguousarray(np.asarray(inputs["L1_attn_" + n], dtype=np.float32))
    return {
        **({"x_in": np.ascontiguousarray(np.asarray(x_cur, dtype=np.float32))}
           if x_cur is not None else {}),
        "w_q_w": g("q_w"), "w_k_w": g("k_w"), "w_v_w": g("v_w"), "w_o_w": g("o_w"),
    }




def emit_b3(k, c, x_in_ap, xfmh, w, x_out_ap):
    import os
    NE = int(os.environ.get("B3_NE", "16"))
    B3_SH = int(os.environ.get("B3_SH", "1"))
    B3_ROUTE = int(os.environ.get("B3_ROUTE", "2"))
    nc = k.nc
    with k.scope():
        wte = k.sb("b3_wte", [P, 16 * 16], F32)
        wteT = k.sb("b3_wteT", [P, 16 * 16], F32)
        tauones = k.sb("b3_tauones", [2, T], F16)
        ones_h = k.sb("b3_onesh", [P, 1], BF16)
        sbf = k.sb("b3_sbf", [P, 4 * T], F16)
        sqs = k.sb("b3_sqs", [P, 512], BF16)
        hh_sb = k.sb("b3_hhsb", [P, 2048], F16)
        zt = k.sb("b3_zt", [P, 2048], BF16)
        zsq = k.sb("b3_zsq", [P, 2048], BF16)
        cols = k.sb("b3_cols", [P, 16], F32)
        colx = k.sb("b3_colx", [P, 8], F32)
        colSH = k.sb("b3_cSH", [P, 16], F32)
        colTau = k.sb("b3_cTau", [P, 16], F32)
        colBB = k.sb("b3_cBB", [P, 16], F32)
        colU0 = k.sb("b3_cU0", [P, 16], F32)
        colQ = k.sb("b3_cQ", [P, 16], F32)
        colS = k.sb("b3_cS", [P, 16], F32)
        colIS2 = k.sb("b3_cIS2", [P, 16], F32)
        colWp = k.sb("b3_cWp", [P, 16], F32)
        colWp2 = k.sb("b3_cWp2", [P, 16], F32)
        colR2 = k.sb("b3_cR2", [P, 16], F32)
        colR = k.sb("b3_cR", [P, 16], F32)
        tmpA = k.sb("b3_tmpA", [P, 16], F32)
        tmpB = k.sb("b3_tmpB", [P, 16], F32)
        tmpC = k.sb("b3_tmpC", [P, 16], F32)
        vrow = k.sb("b3_vrow", [1, 1024], BF16)
        taub2 = [k.sb(f"b3_taub2{i}", [2, 1024], F16) for i in range(2)]
        guard = {}
        t_vp = k.op("pl", lambda h: h.memset(vrow[0:1, 511:512], 0.0))
        t_vp2 = k.op("pl", lambda h: h.memset(vrow[0:1, 1023:1024], 0.0), after=[t_vp])
        t_to = k.op("pl", lambda h: h.memset(tauones[0:2, :], 1.0))
        t_oh = k.op("pl", lambda h: h.memset(ones_h[:], 1.0))
        consts = [t_to, t_oh, t_vp2] + c["tok"]

        with k.scope():
            xtm = [k.sb(f"b3_x{t}", [P, 1024], F32) for t in range(16)]
            ld_toks = [k.dma("qs", "q_xin", xtm[tt][:],
                             x_in_ap[tt * 128:(tt + 1) * 128, :]) for tt in range(16)]
            ld_all = ld_toks[-1]
            xlo = k.sb("b3_xlo", [P, NF * T], F16)
            with k.scope():
                psT2 = [k.ps(f"b3_ldT{i}", [P, 512], F32) for i in range(2)]
                ev = {}
                xlo_toks = []
                for tt in range(16 if B3_ROUTE else 0):
                    for f in range(NF):
                        bank, slot = f // 4, f % 4
                        ev[(tt, f, "t")] = k.op(
                            "pe", lambda h, tt=tt, f=f, bank=bank, slot=slot:
                            h.transpose(psT2[bank][:, slot * 128:(slot + 1) * 128],
                                        xtm[tt][:, f * 128:(f + 1) * 128],
                                        c["ident_f"][:]),
                            after=[ld_all] + [ev.get((tt - 1, bank * 4 + s))
                                              for s in range(4)] + c["tok"])
                    for f in range(NF):
                        bank, slot = f // 4, f % 4
                        hi = k.op("dv", lambda h, tt=tt, f=f, bank=bank, slot=slot:
                                  h.tensor_copy(
                                      xfmh[:, f * T + tt * 128:f * T + (tt + 1) * 128],
                                      psT2[bank][:, slot * 128:(slot + 1) * 128]),
                                  after=[ev[(tt, bank * 4 + 3, "t")]])
                        lo = k.op("dv", lambda h, tt=tt, f=f, bank=bank, slot=slot:
                                  h.tensor_tensor(
                                      xlo[:, f * T + tt * 128:f * T + (tt + 1) * 128],
                                      psT2[bank][:, slot * 128:(slot + 1) * 128],
                                      xfmh[:, f * T + tt * 128:f * T + (tt + 1) * 128],
                                      ALU.subtract), after=[hi])
                        ev[(tt, f)] = lo
                        xlo_toks += [hi, lo]
            gwh = k.sb("b3_gw", [P, 3 * 128], F16)
            gw32 = k.sb("b3_gw32", [P, 128], F32)
            sc16 = k.sb("b3_sc16", [P, 64], F32)
            psR = k.ps("b3_psR", [P, 512])
            t_g = k.dma("qs", "q_b3g", gw32[:].rearrange("p (a m) -> p a m", m=16),
                        w["gwT"].rearrange("(a p) m -> p a m", p=128))
            t_gh = k.op("ac", lambda h: h.activation(gwh[:, 0:128], gw32[:], AF.Copy),
                        after=[t_g])
            t_gl = k.op("dv", lambda h: h.tensor_tensor(gwh[:, 128:256], gw32[:],
                                                        gwh[:, 0:128], ALU.subtract),
                        after=[t_gh])
            lgb = k.sb("b3_lgb", [P, 256], F32)
            lgt = k.sb("b3_lgt", [P, 256], F32)
            colM = k.sb("b3_colM", [P, 16], F32)
            mms = []
            first = True
            for tt in range(16 if B3_ROUTE >= 2 else 0):
                i = 0
                for pas in range(2):
                    src_ = xfmh if pas == 0 else xlo
                    gcol = 0
                    for kk in range(NF):
                        aft = ([t_gh, t_gl] + xlo_toks) if i == 0 else []
                        last = (tt == 15 and i == 2 * NF - 1)
                        mms.append(k.op("pe", lambda h, tt=tt, kk=kk, src=src_,
                                        gcol=gcol, first=first, last=last:
                                        h.matmul(psR[:, tt * 16:tt * 16 + 16],
                                                 src[:, kk * T + tt * 128:kk * T + (tt + 1) * 128],
                                                 gwh[:, gcol + kk * 16:gcol + (kk + 1) * 16],
                                                 start=first, stop=last,
                                                 skip_group_check=True), after=aft))
                        first = False
                        i += 1
            if B3_ROUTE >= 2:
                seg = lambda t_: t_[:].rearrange("p (t e) -> p t e", e=16)
                col3 = lambda t_: t_[:].rearrange("p (t o) -> p t o", o=1)
                colb = lambda t_: col3(t_).broadcast_to([P, 16, 16])
                t = k.op("dv", lambda h: h.tensor_copy(lgb[:], psR[:, 0:256]),
                         after=[mms[-1]])
                t = k.op("dv", lambda h: h.reduce_max(
                    col3(colM), seg(lgb), axis=AX.X, negate=True), after=[t])
                t = k.op("dv", lambda h: h.tensor_tensor(
                    seg(lgb), seg(lgb), colb(colM), ALU.add), after=[t])
                t = k.op("ac", lambda h: h.activation(lgb[:], lgb[:], AF.Exp),
                         after=[t])
                t = k.op("dv", lambda h: h.reduce_sum(
                    col3(colM), seg(lgb), axis=AX.X), after=[t])
                t = k.op("dv", lambda h: h.reciprocal(colM[:], colM[:]), after=[t])
                t = k.op("dv", lambda h: h.tensor_tensor(
                    seg(lgb), seg(lgb), colb(colM), ALU.mult), after=[t])
                t = k.op("dv", lambda h: h.reduce_max(
                    col3(colM), seg(lgb), axis=AX.X), after=[t])
                t = k.op("dv", lambda h: h.tensor_tensor(
                    seg(lgt), seg(lgb), colb(colM), ALU.is_equal), after=[t])
                t = k.op("dv", lambda h: h.scalar_tensor_tensor(
                    lgt[:], lgt[:], -1e30, lgb[:], ALU.mult, ALU.add), after=[t])
                t = k.op("dv", lambda h: h.reduce_max(
                    col3(colM), seg(lgt), axis=AX.X), after=[t])
                t = k.op("dv", lambda h: h.tensor_tensor(
                    seg(lgt), seg(lgb), colb(colM), ALU.is_ge), after=[t])
                k.op("dv", lambda h: h.tensor_tensor(
                    wte[:], lgt[:], lgb[:], ALU.mult), after=[t])
            k.join()

        wte3 = wte[:].rearrange("p (t e) -> p e t", e=16)
        t_wteT = None
        for e_ in range(16):
            t_wteT = k.dma("qs", "q_wtT",
                           wteT[:, e_ * 16:(e_ + 1) * 16].rearrange(
                               "p (a t) -> p a t", a=1),
                           wte3[:, e_:e_ + 1, :],
                           allow_slow_non_contiguous=True)
        ytm = [k.sb(f"b3_y{t}", [P, 1024], F32) for t in range(16)]
        yinit = []
        for tt in range(16):
            yinit.append(k.op("pl", lambda h, tt=tt: h.memset(ytm[tt][:], 0.0)))
        for t_, v_ in ((colU0, 1.0), (colQ, 1.0), (colS, 1.0), (colIS2, 1.0)):
            yinit.append(k.op("pl", lambda h, t_=t_, v_=v_: h.memset(t_[:], v_)))
        t_wp0 = k.op("dv", lambda h: h.tensor_tensor(
            colWp[:], wteT[:, 0:16], colS[:], ALU.mult), after=[t_wteT] + yinit)
        t_wp0b = k.op("dv", lambda h: h.tensor_scalar_mul(colWp2[:], colWp[:], 2.0),
                      after=[t_wp0])
        k.wait("dv", *yinit)
        k.wait("ac", *yinit)

        psU1 = [k.ps(f"b3_psU1{i}", [P, 512]) for i in range(2)]
        psU3 = [k.ps(f"b3_psU3{i}", [P, 512]) for i in range(2)]
        psT_ = k.ps("b3_psTau", [P, 512])
        psW2 = k.ps("b3_psW2", [P, 1024])
        silu_sb = k.sb("b3_silu", [P, 1024], BF16)

        w_user = {}
        wA_user = {}
        bias_user = None

        lr_guards = {"colsh": None, "tau": None, "r": None}

        def lres_phaseB(mm_emit, wp_next_src):
            prev_chh = lr_guards.get("chh", [None, None])
            hs_hist = lr_guards.get("hs2", [None, None])
            z_hist = lr_guards.get("z2", [None, None])
            prev_hs = None
            for tt in range(16):
                hp = (tt % 2) * 1024
                half_toks = mm_emit(tt, prev_chh)
                c_a = k.op("ac", lambda h, hp=hp: h.activation(
                    hh_sb[:, hp:hp + 512], psW2[:, 0:512], AF.Copy),
                    after=[half_toks[0], hs_hist[-2], z_hist[-2]])
                c_b = k.op("ac", lambda h, hp=hp: h.activation(
                    hh_sb[:, hp + 512:hp + 1023], psW2[:, 512:1023], AF.Copy),
                    after=[half_toks[1]])
                t_hs = k.op("ac", lambda h, tt=tt, hp=hp: h.activation(
                    zsq[:, hp:hp + 1023], hh_sb[:, hp:hp + 1023], AF.Square,
                    accum_out=colSH[:, tt:tt + 1]),
                    after=[c_b] + ([lr_guards["colsh"]] if tt == 0 else []))
                k.op("dv", lambda h, tt=tt, hp=hp: h.scalar_tensor_tensor(
                    zt[:, hp:hp + 1023], hh_sb[:, hp:hp + 1023], 1.0, ytm[tt][:, 1:1024],
                    ALU.bypass, ALU.mult, accum_out=colBB[:, tt:tt + 1]),
                    after=[c_b])
                t_z = k.op("dv", lambda h, tt=tt, hp=hp: h.scalar_tensor_tensor(
                    ytm[tt][:, 1:1024], hh_sb[:, hp:hp + 1023], colWp[:, tt:tt + 1],
                    ytm[tt][:, 1:1024], ALU.mult, ALU.add), after=[c_b])
                prev_chh = [c_a, c_b]
                hs_hist.append(t_hs)
                z_hist.append(t_z)
                prev_hs = t_hs
            lr_guards["chh"] = prev_chh
            lr_guards["hs2"] = hs_hist[-2:]
            lr_guards["z2"] = z_hist[-2:]
            a_tau = k.op("ac", lambda h: h.activation(
                colTau[:], colSH[:], AF.Sqrt, bias=c["ccols"][:, 1:2]),
                after=[lr_guards["tau"], prev_hs] + consts)
            t = k.op("dv", lambda h: h.tensor_tensor(
                tmpA[:], colU0[:], colTau[:], ALU.mult), after=[a_tau])
            t = k.op("dv", lambda h: h.tensor_tensor(
                tmpA[:], tmpA[:], colBB[:], ALU.add), after=[t])
            t_u = k.op("dv", lambda h: h.tensor_tensor(
                tmpB[:], colTau[:], colWp[:], ALU.mult), after=[t])
            lr_guards["tau"] = t_u
            t_u = k.op("dv", lambda h: h.tensor_tensor(
                colU0[:], colU0[:], tmpB[:], ALU.add), after=[t_u])
            q = k.op("dv", lambda h: h.tensor_scalar(
                tmpC[:], colSH[:], 2.0, 1.0, ALU.mult, ALU.add), after=[t_u])
            lr_guards["colsh"] = q
            q = k.op("dv", lambda h: h.tensor_tensor(
                tmpC[:], tmpC[:], colWp[:], ALU.mult), after=[q])
            q = k.op("dv", lambda h: h.tensor_tensor(
                tmpC[:], tmpC[:], colWp[:], ALU.mult), after=[q])
            q = k.op("dv", lambda h: h.tensor_tensor(
                tmpA[:], tmpA[:], colWp2[:], ALU.mult), after=[q])
            q = k.op("dv", lambda h: h.tensor_tensor(
                colQ[:], colQ[:], tmpA[:], ALU.add), after=[q])
            q = k.op("dv", lambda h: h.tensor_tensor(
                colQ[:], colQ[:], tmpC[:], ALU.add), after=[q])
            t = k.op("dv", lambda h: h.tensor_tensor(
                tmpA[:], colU0[:], colU0[:], ALU.mult), after=[q])
            t = k.op("dv", lambda h: h.scalar_tensor_tensor(
                colR2[:], tmpA[:], 2.0, colQ[:], ALU.mult, ALU.subtract),
                after=[t])
            t = k.op("dv", lambda h: h.tensor_tensor(
                colR2[:], colR2[:], colIS2[:], ALU.mult), after=[t])
            t = k.op("dv", lambda h: h.tensor_scalar_max(
                colR2[:], colR2[:], 1e-6), after=[t])
            a_r = k.op("ac", lambda h: h.activation(
                colR[:], colR2[:], AF.Sqrt), after=[t, lr_guards["r"]])
            t_s = k.op("dv", lambda h: h.tensor_tensor(
                colS[:], colS[:], colR[:], ALU.mult), after=[a_r])
            lr_guards["r"] = t_s
            t = k.op("dv", lambda h: h.reciprocal(colR2[:], colR2[:]), after=[t_s])
            t = k.op("dv", lambda h: h.tensor_tensor(
                colIS2[:], colIS2[:], colR2[:], ALU.mult), after=[t])
            if wp_next_src == "shared":
                t = k.op("dv", lambda h: h.tensor_copy(colWp[:], colS[:]),
                         after=[t])
                t = k.op("dv", lambda h: h.tensor_scalar_mul(
                    colWp2[:], colWp[:], 2.0), after=[t])
            elif wp_next_src is not None:
                t = k.op("dv", lambda h, s_=wp_next_src: h.tensor_tensor(
                    colWp[:], s_, colS[:], ALU.mult), after=[t])
                t = k.op("dv", lambda h: h.tensor_scalar_mul(
                    colWp2[:], colWp[:], 2.0), after=[t])
            return t

        lr_last = None
        es_exp = ExitStack()
        k._scopes.append(es_exp)
        we1 = [k.sb(f"b3_we1{i}", [P, 8 * 511], F16) for i in range(2)]
        we3 = [k.sb(f"b3_we3{i}", [P, 8 * 511], F16) for i in range(2)]
        we2 = [k.sb(f"b3_we2{i}", [P, 4 * 1023], F16) for i in range(2)]
        for e in range(NE):
            wb = e % 2
            aftw = [w_user.get(e - 2)]
            t_w1 = k.dma("qg", f"q_b3w0_{wb}", we1[wb][:].rearrange("p (a m) -> p a m", m=511),
                         w["We1T"][e].rearrange("(a p) m -> p a m", p=128), after=aftw)
            t_w3 = k.dma("qg", f"q_b3w1_{wb}", we3[wb][:].rearrange("p (a m) -> p a m", m=511),
                         w["We3T"][e].rearrange("(a p) m -> p a m", p=128), after=aftw)
            t_w2a = k.dma("qg", f"q_b3w2_{wb}", we2[wb][0:128, 0:3 * 1023].rearrange("p (a m) -> p a m", m=1023),
                          w["We2T"][e][1:385].rearrange("(a p) m -> p a m", p=128),
                          after=aftw)
            t_w2b = k.dma("qg", f"q_b3w3_{wb}", we2[wb][0:127, 3 * 1023:4 * 1023],
                          w["We2T"][e][385:512], after=aftw)
            t_tb1 = k.dma("qg", f"q_b3w4_{wb}", taub2[wb][0:1, 0:1023], w["We2T"][e][0:1], after=aftw)
            t_tb2 = k.dma("qg", f"q_b3w5_{wb}", taub2[wb][1:2, 0:1023],
                          w["be2"][e].rearrange("(a d) -> a d", a=1), after=aftw)
            t_wall = [t_w2a, t_w2b, t_tb1, t_tb2]
            t_b1r = k.dma("qg", "q_b3b1", vrow[0:1, 0:511],
                          w["be1"][e].rearrange("(a d) -> a d", a=1), after=[bias_user])
            t_b3r = k.dma("qg", "q_b3b3", vrow[0:1, 512:1023],
                          w["be3"][e].rearrange("(a d) -> a d", a=1), after=[t_b1r])
            bc_toks = []
            for m in range(4):
                k.op("pe", lambda h, m=m: h.matmul(
                    psT_[:, m:m + 1], vrow[0:1, m * 128:(m + 1) * 128],
                    c["ones_row_bf"][0:1, 0:1], start=(m == 0), stop=False,
                    skip_group_check=True),
                    after=[t_b1r, t_b3r, guard.get("psT")] + consts)
                bc_toks.append(k.op("pe", lambda h, m=m: h.matmul(
                    psT_[:, 4 + m:5 + m], vrow[0:1, 512 + m * 128:512 + (m + 1) * 128],
                    c["ones_row_bf"][0:1, 0:1], start=False, stop=(m == 3),
                    skip_group_check=True)))
            t_bc = k.op("dv", lambda h: h.tensor_copy(cols[:, 0:8], psT_[:, 0:8]),
                        after=[bc_toks[-1]])
            bias_user = t_bc
            guard["psTau"] = t_bc

            s_toks = []
            pend_tm = None

            def emit_tm(pend):
                (p_sq, p_m, p_kr, p_c0) = pend
                t_tm = k.op("pe", lambda h, m=p_m, kr=p_kr: h.matmul(
                    psT_[0:1, 0:512], ones_h[0:kr, 0:1], sqs[0:kr, 0:512],
                    start=(m == 0), stop=(m == 3), skip_group_check=True),
                    after=[p_sq, guard.get("psTau")])
                guard["sqs"] = t_tm
                if p_m == 3:
                    t_tau = k.op("ac", lambda h, c0=p_c0: h.activation(
                        tauones[0:1, c0:c0 + 512], psT_[0:1, 0:512], AF.Sqrt,
                        bias=c["cpool"][0:1, 2:3]), after=[t_tm] + consts)
                    guard["psTau"] = t_tau

            for tc in range(4):
                c0 = tc * 512
                for m in range(4):
                    kr = 128 if m < 3 else 127
                    b = m % 2
                    mm1, mm3 = [], []
                    for kk in range(NF):
                        aft1 = ([t_w1, guard.get(("U1", b))] if kk == 0 else [])
                        mm1.append(k.op("pe", lambda h, m=m, kr=kr, kk=kk, b=b, c0=c0,
                                        wb=wb: h.matmul(
                                            psU1[b][0:kr, 0:512],
                                            we1[wb][:, kk * 511 + m * 128:kk * 511 + m * 128 + kr],
                                            xfmh[:, kk * T + c0:kk * T + c0 + 512],
                                            start=(kk == 0), stop=(kk == NF - 1),
                                            skip_group_check=True), after=aft1))
                    for kk in range(NF):
                        aft3 = [t_w3, guard.get(("U3", b))] if kk == 0 else []
                        mm3.append(k.op("pe", lambda h, m=m, kr=kr, kk=kk, b=b, c0=c0,
                                        wb=wb: h.matmul(
                                            psU3[b][0:kr, 0:512],
                                            we3[wb][:, kk * 511 + m * 128:kk * 511 + m * 128 + kr],
                                            xfmh[:, kk * T + c0:kk * T + c0 + 512],
                                            start=(kk == 0), stop=(kk == NF - 1),
                                            skip_group_check=True), after=aft3))
                    if pend_tm is not None:
                        emit_tm(pend_tm)
                    t_si = k.op("ac", lambda h, m=m, kr=kr, b=b: h.activation(
                        silu_sb[0:kr, b * 512:b * 512 + 512],
                        psU1[b][0:kr, 0:512],
                        AF.Silu, bias=cols[0:kr, m:m + 1]),
                        after=[mm1[-1], t_bc, s_toks[-2] if len(s_toks) >= 2 else None])
                    guard[("U1", b)] = t_si
                    t_s = k.op("dv", lambda h, m=m, kr=kr, b=b, c0=c0: h.scalar_tensor_tensor(
                        sbf[0:kr, m * T + c0:m * T + c0 + 512],
                        psU3[b][0:kr, 0:512],
                        cols[0:kr, 4 + m:5 + m], silu_sb[0:kr, b * 512:b * 512 + 512],
                        ALU.add, ALU.mult), after=[mm3[-1], t_si, t_bc])
                    guard[("U3", b)] = t_s
                    s_toks.append(t_s)
                    t_sq = k.op("dv", lambda h, m=m, kr=kr, c0=c0: h.tensor_tensor(
                        sqs[0:kr, 0:512],
                        sbf[0:kr, m * T + c0:m * T + c0 + 512],
                        sbf[0:kr, m * T + c0:m * T + c0 + 512],
                        ALU.mult), after=[t_s, guard.get("sqs")])
                    pend_tm = (t_sq, m, kr, c0)
            emit_tm(pend_tm)
            pend_tm = None
            wA_user[e] = ("pe", k.cnt["pe"])
            ready = t_wall + [("dv", k.cnt["dv"]), ("ac", k.cnt["ac"])]

            def mm_emit(tt, aft0, ready=ready):
                half_toks = []
                for half in range(2):
                    ncols = 512 if half == 0 else 511
                    i = 0
                    for kk in range(4):
                        kr = 128 if kk < 3 else 127
                        aft = (ready + [aft0[half]]) if i == 0 else []
                        k.op("pe", lambda h, tt=tt, half=half, kk=kk, kr=kr,
                             i=i, ncols=ncols, wb=wb: h.matmul(
                                 psW2[:, half * 512:half * 512 + ncols],
                                 sbf[0:kr, kk * T + tt * 128:kk * T + (tt + 1) * 128],
                                 we2[wb][0:kr, kk * 1023 + half * 512:kk * 1023 + half * 512 + ncols],
                                 start=(i == 0), stop=False,
                                 skip_group_check=True),
                             after=aft)
                        i += 1
                    tok = k.op("pe", lambda h, tt=tt, half=half, ncols=ncols,
                               wb=wb: h.matmul(
                                   psW2[:, half * 512:half * 512 + ncols],
                                   tauones[0:2, tt * 128:(tt + 1) * 128],
                                   taub2[wb][0:2, half * 512:half * 512 + ncols],
                                   start=False, stop=True,
                                   skip_group_check=True))
                    half_toks.append(tok)
                return half_toks

            nxt = (wteT[:, (e + 1) * 16:(e + 2) * 16] if e + 1 < NE
                   else ("shared" if B3_SH else None))
            lr_last = lres_phaseB(mm_emit, nxt)
            w_user[e] = ("pe", k.cnt["pe"])
            guard["psT"] = guard["psTau"]
        k.join()
        if os.environ.get("B3_DEBUG"):
            for nm, tens in [("d_y0", ytm[0]), ("d_y1", ytm[1]), ("d_wte", wte),
                             ("d_tau", tauones), ("d_sbf", sbf), ("d_we2", we2[0]),
                             ("d_taub2", taub2[0])]:
                dt_ = {"d_y0": F32, "d_y1": F32, "d_wte": F32, "d_tau": F16,
                       "d_sbf": F16, "d_we2": F16, "d_taub2": F16}[nm]
                o = k.nc.dram_tensor(nm, [tens.shape[0], tens.shape[1]], dt_,
                                     kind="ExternalOutput").ap()
                k.dma("qs", "q_dbg", o, tens[:])
            k.join()
        k._scopes.pop()
        es_exp.close()

        if B3_SH:
         with k.scope():
             ws1 = k.sb("b3_ws1", [P, 8 * 1023], F16)
             ws3 = k.sb("b3_ws3", [P, 8 * 1023], F16)
             ws2 = k.sb("b3_ws2", [P, 8 * 1023], F16)
             aftw = [w_user.get(NE - 1)]
             aftwA = [wA_user.get(NE - 1)]
             t_w1 = k.dma("qg", "q_b3s0", ws1[:].rearrange("p (a m) -> p a m", m=1023),
                          w["Ws1T"].rearrange("(a p) m -> p a m", p=128), after=aftwA)
             t_w3 = k.dma("qg", "q_b3s1", ws3[:].rearrange("p (a m) -> p a m", m=1023),
                          w["Ws3T"].rearrange("(a p) m -> p a m", p=128), after=aftwA)
             b_sh = NE % 2
             t_tb1 = k.dma("qg", "q_b3s4", taub2[b_sh][0:1, 0:1023], w["Ws2T"][0:1],
                           after=[w_user.get(NE - 2)])
             t_tb2 = k.dma("qg", "q_b3s5", taub2[b_sh][1:2, 0:1023],
                           row1(w["bs2"]), after=[w_user.get(NE - 2)])
             t_b1r = k.dma("qg", "q_b3b", vrow[0:1, 0:1023], row1(w["bs1"]),
                           after=[bias_user])
             bct = []
             for m in range(8):
                 bct.append(k.op("pe", lambda h, m=m: h.matmul(
                     psT_[:, m:m + 1], vrow[0:1, m * 128:(m + 1) * 128],
                     c["ones_row_bf"][0:1, 0:1], start=(m == 0), stop=(m == 7),
                     skip_group_check=True),
                     after=[t_b1r, guard.get("psT")] + consts))
             t_bc1 = k.op("dv", lambda h: h.tensor_copy(cols[:, 0:8], psT_[:, 0:8]),
                          after=[bct[-1]])
             t_w2a = k.dma("qg", "q_b3s2", ws2[0:128, 0:7 * 1023].rearrange(
                 "p (a m) -> p a m", m=1023),
                 w["Ws2T"][1:897].rearrange("(a p) m -> p a m", p=128), after=aftw)
             t_w2b = k.dma("qg", "q_b3s3", ws2[0:127, 7 * 1023:8 * 1023],
                           w["Ws2T"][897:1024], after=aftw)
             t_wall = [t_w2a, t_w2b, t_tb1, t_tb2]
             t_b3r = k.dma("qg", "q_b3b", vrow[0:1, 0:1023], row1(w["bs3"]),
                           after=[bct[-1]])
             bct = []
             for m in range(8):
                 bct.append(k.op("pe", lambda h, m=m: h.matmul(
                     psT_[:, 8 + m:9 + m], vrow[0:1, m * 128:(m + 1) * 128],
                     c["ones_row_bf"][0:1, 0:1], start=(m == 0), stop=(m == 7),
                     skip_group_check=True), after=[t_b3r, t_bc1]))
             t_bc3 = k.op("dv", lambda h: h.tensor_copy(cols[:, 8:16], psT_[:, 8:16]),
                          after=[bct[-1]])
             guard["psTau"] = t_bc3

             s2b = k.sb("b3_s2b", [P, 4 * T], F16)

             def stile(m):
                 return (sbf, m) if m < 4 else (s2b, m - 4)

             sq_toks, s_toks = [], []
             pend_tm = None

             def emit_tm_sh(pend):
                 (p_sq, p_m, p_kr, p_c0) = pend
                 t_tm = k.op("pe", lambda h, m=p_m, kr=p_kr: h.matmul(
                     psT_[0:1, 0:512], ones_h[0:kr, 0:1], sqs[0:kr, 0:512],
                     start=(m == 0), stop=(m == 7), skip_group_check=True),
                     after=[p_sq, guard.get("psTau")])
                 guard["sqs"] = t_tm
                 if p_m == 7:
                     t_tau = k.op("ac", lambda h, c0=p_c0: h.activation(
                         tauones[0:1, c0:c0 + 512], psT_[0:1, 0:512], AF.Sqrt,
                         bias=c["cpool"][0:1, 2:3]), after=[t_tm] + consts)
                     guard["psTau"] = t_tau

             for tc in range(4):
                 c0 = tc * 512
                 for m in range(8):
                     kr = 128 if m < 7 else 127
                     b = m % 2
                     mm1, mm3 = [], []
                     for kk in range(NF):
                         aft1 = ([t_w1,
                                  guard.get(("U1", b))] if kk == 0 else [])
                         mm1.append(k.op("pe", lambda h, m=m, kr=kr, kk=kk, b=b, c0=c0:
                                         h.matmul(
                                             psU1[b][0:kr, 0:512],
                                             ws1[:, kk * 1023 + m * 128:kk * 1023 + m * 128 + kr],
                                             xfmh[:, kk * T + c0:kk * T + c0 + 512],
                                             start=(kk == 0), stop=(kk == NF - 1),
                                             skip_group_check=True), after=aft1))
                     for kk in range(NF):
                         aft3 = [t_w3, guard.get(("U3", b))] if kk == 0 else []
                         mm3.append(k.op("pe", lambda h, m=m, kr=kr, kk=kk, b=b, c0=c0:
                                         h.matmul(
                                             psU3[b][0:kr, 0:512],
                                             ws3[:, kk * 1023 + m * 128:kk * 1023 + m * 128 + kr],
                                             xfmh[:, kk * T + c0:kk * T + c0 + 512],
                                             start=(kk == 0), stop=(kk == NF - 1),
                                             skip_group_check=True), after=aft3))
                     if pend_tm is not None:
                         emit_tm_sh(pend_tm)
                     t_si = k.op("ac", lambda h, m=m, kr=kr, b=b: h.activation(
                         silu_sb[0:kr, 0:512],
                         psU1[b][0:kr, 0:512],
                         AF.Silu, bias=cols[0:kr, m:m + 1]),
                         after=[mm1[-1], t_bc1, s_toks[-1] if s_toks else None])
                     guard[("U1", b)] = t_si
                     st, sm = stile(m)
                     t_s = k.op("dv", lambda h, m=m, kr=kr, b=b, c0=c0, st=st, sm=sm:
                                h.scalar_tensor_tensor(
                                    st[0:kr, sm * T + c0:sm * T + c0 + 512],
                                    psU3[b][0:kr, 0:512],
                                    cols[0:kr, 8 + m:9 + m], silu_sb[0:kr, 0:512],
                                    ALU.add, ALU.mult), after=[mm3[-1], t_si, t_bc3])
                     guard[("U3", b)] = t_s
                     s_toks.append(t_s)
                     sq_toks.append(k.op("dv", lambda h, m=m, kr=kr, c0=c0, st=st,
                                         sm=sm: h.tensor_tensor(
                         sqs[0:kr, 0:512], st[0:kr, sm * T + c0:sm * T + c0 + 512],
                         st[0:kr, sm * T + c0:sm * T + c0 + 512],
                         ALU.mult), after=[t_s, guard.get("sqs")]))
                     pend_tm = (sq_toks[-1], m, kr, c0)
             emit_tm_sh(pend_tm)
             pend_tm = None
             ready = t_wall + [("dv", k.cnt["dv"]), ("ac", k.cnt["ac"])]

             def mm_emit_sh(tt, aft0, ready=ready):
                 half_toks = []
                 for half in range(2):
                     ncols = 512 if half == 0 else 511
                     i = 0
                     for kk in range(8):
                         kr = 128 if kk < 7 else 127
                         st, sm = stile(kk)
                         aft = (ready + [aft0[half]]) if i == 0 else []
                         k.op("pe", lambda h, tt=tt, half=half, kk=kk, kr=kr,
                              i=i, ncols=ncols, st=st, sm=sm: h.matmul(
                                  psW2[:, half * 512:half * 512 + ncols],
                                  st[0:kr, sm * T + tt * 128:sm * T + (tt + 1) * 128],
                                  ws2[0:kr, kk * 1023 + half * 512:kk * 1023 + half * 512 + ncols],
                                  start=(i == 0), stop=False,
                                  skip_group_check=True),
                              after=aft)
                         i += 1
                     tok = k.op("pe", lambda h, tt=tt, half=half, ncols=ncols:
                                h.matmul(
                                    psW2[:, half * 512:half * 512 + ncols],
                                    tauones[0:2, tt * 128:(tt + 1) * 128],
                                    taub2[b_sh][0:2, half * 512:half * 512 + ncols],
                                    start=False, stop=True,
                                    skip_group_check=True))
                     half_toks.append(tok)
                 return half_toks

             lr_last = lres_phaseB(mm_emit_sh, None)
             k.join()

        t_rs = k.op("dv", lambda h: h.reciprocal(tmpA[:], colS[:]), after=[lr_last])
        for tt in range(16):
            t1 = k.op("dv", lambda h, tt=tt: h.tensor_copy(
                ytm[tt][:, 0:1], colU0[:, tt:tt + 1]), after=[t_rs])
            k.op("dv", lambda h, tt=tt: h.tensor_scalar_mul(
                ytm[tt][:], ytm[tt][:], tmpA[:, tt:tt + 1]), after=[t1])
        k.join()

        with k.scope():
            stg = [k.sb(f"b3_stg{i}", [P, 1024], F32) for i in range(2)]
            xldg = [k.sb(f"b3_xld{i}", [P, 1024], F32) for i in range(2)]
            st_toks = []
            dma_prev = {}
            xread_prev = {}
            xdma = {}
            for tt in range(2):
                xdma[tt] = k.dma("qs", f"q_xre{tt % 2}", xldg[tt % 2][:],
                                 x_in_ap[tt * 128:(tt + 1) * 128, :], after=[])
            prev_o = None
            for tt in range(16):
                b = tt % 2
                t_y = k.op("dv", lambda h, tt=tt, b=b: h.tensor_tensor(
                    ytm[tt][:], ytm[tt][:], xldg[b][:], ALU.add),
                    after=[xdma[tt]])
                xread_prev[b] = t_y
                if tt + 2 < 16:
                    xdma[tt + 2] = k.dma("qs", f"q_xre{b}", xldg[b][:],
                                         x_in_ap[(tt + 2) * 128:(tt + 3) * 128, :],
                                         after=[t_y])
                t_m = k.op("dv", lambda h, tt=tt: h.reduce_sum(
                    colx[:, 0:1], ytm[tt][:], axis=AX.X), after=[t_y, prev_o])
                t_qq = k.op("ac", lambda h, tt=tt: h.activation(
                    zsq[:, 0:1024], ytm[tt][:], AF.Square, accum_out=colx[:, 1:2]),
                    after=[t_y, prev_o, dma_prev.get(b)])
                t_m2 = k.op("dv", lambda h: h.tensor_scalar_mul(
                    colx[:, 0:1], colx[:, 0:1], 1.0 / D), after=[t_m])
                t_v = k.op("ac", lambda h: h.activation(
                    colx[:, 2:3], colx[:, 0:1], AF.Square), after=[t_m2])
                t_v = k.op("dv", lambda h: h.scalar_tensor_tensor(
                    colx[:, 3:4], colx[:, 1:2], 1.0 / D, colx[:, 2:3],
                    ALU.mult, ALU.subtract), after=[t_qq, t_v])
                t_v = k.op("ac", lambda h: h.activation(
                    colx[:, 4:5], colx[:, 3:4], AF.Sqrt, bias=c["ccols"][:, 2:3]),
                    after=[t_v] + consts)
                t_v = k.op("dv", lambda h: h.reciprocal(colx[:, 4:5], colx[:, 4:5]),
                           after=[t_v])
                t_o = k.op("dv", lambda h, tt=tt, b=b: h.tensor_scalar(
                    stg[b][:], ytm[tt][:], colx[:, 0:1], colx[:, 4:5],
                    ALU.subtract, ALU.mult), after=[t_v, dma_prev.get(b)])
                tok = k.dma("qs", f"q_out{b}", x_out_ap[tt * 128:(tt + 1) * 128, :],
                            stg[b][:], after=[t_o])
                dma_prev[b] = tok
                prev_o = t_o
                st_toks.append(tok)
        return st_toks




B3_SHAPES = {
    "gwT": ([D, 16], F32), "We1T": ([16, D, 511], F32), "be1": ([16, 511], F32),
    "We3T": ([16, D, 511], F32), "be3": ([16, 511], F32),
    "We2T": ([16, 512, 1023], F32), "be2": ([16, 1023], F32),
    "Ws1T": ([D, 1023], F32), "bs1": ([1023], F32),
    "Ws3T": ([D, 1023], F32), "bs3": ([1023], F32),
    "Ws2T": ([D, 1023], F32), "bs2": ([1023], F32),
}


def build_b3_program():
    k = K()
    nc = k.nc
    x_in = nc.dram_tensor("x_in", [T, D], F32, kind="ExternalInput").ap()
    w = {n: nc.dram_tensor("w_" + n, B3_SHAPES[n][0], B3_SHAPES[n][1],
                           kind="ExternalInput").ap() for n in B3_SHAPES}
    x_out = nc.dram_tensor("x_out", [T, D], F32, kind="ExternalOutput").ap()
    with k.es:
        k.open()
        c = emit_consts(k)
        xfmh = k.sb("b3_xfmh", [P, NF * T], F16)
        st_toks = emit_b3(k, c, x_in, xfmh, w, x_out)
        k.finish(st_toks)
        k.replay()
    return k.nc


def _b3_inputs(inputs, ci, x_cur):
    gf = lambda n: np.asarray(inputs["L1_ffn_" + n], dtype=np.float32)
    return {
        **({"x_in": np.ascontiguousarray(np.asarray(x_cur, dtype=np.float32))}
           if x_cur is not None else {}),
        "w_gwT": np.ascontiguousarray(gf("gate_w").T),
        "w_We1T": np.ascontiguousarray(np.transpose(gf("We1"), (0, 2, 1))),
        "w_be1": np.ascontiguousarray(gf("be1")),
        "w_We3T": np.ascontiguousarray(np.transpose(gf("We3"), (0, 2, 1))),
        "w_be3": np.ascontiguousarray(gf("be3")),
        "w_We2T": np.ascontiguousarray(np.transpose(gf("We2"), (0, 2, 1))),
        "w_be2": np.ascontiguousarray(gf("be2")),
        "w_Ws1T": np.ascontiguousarray(gf("Ws1").T),
        "w_bs1": np.ascontiguousarray(gf("bs1")),
        "w_Ws3T": np.ascontiguousarray(gf("Ws3").T),
        "w_bs3": np.ascontiguousarray(gf("bs3")),
        "w_Ws2T": np.ascontiguousarray(gf("Ws2").T),
        "w_bs2": np.ascontiguousarray(gf("bs2")),
    }



B0_SHAPES = {
    "conv_w": [D, 7], "norm1_w": [D], "norm2_w": [D], "Wdecay": [D, D],
    "bdecay": [D], "Wgate": [D, D], "bgate": [D], "state_scale": [1],
    "Wk": [D, 256], "M_T": [256, 256], "Wmem": [256, D], "mem_scale": [1],
    "Wc": [3 * D, D], "bc": [D], "Wup": [D, 4096], "Wg": [D, 4096],
    "Wdown": [4096, D],
}


import os
B0_STAGE = int(os.environ.get("B0_STAGE", "2"))


def build_b0_program():
    k = K()
    nc = k.nc
    x_in = nc.dram_tensor("x_in", [T, D], F32, kind="ExternalInput").ap()
    w = {n: nc.dram_tensor("w_" + n, B0_SHAPES[n], F32, kind="ExternalInput").ap()
         for n in B0_SHAPES}
    mx_kind = "ExternalOutput" if os.environ.get("B0_DUMP_MX") else "Internal"
    w["mx_scr"] = nc.dram_tensor("mx_scr", [P, NF, T], F32, kind=mx_kind).ap()
    x_out = nc.dram_tensor("x_out", [T, D], F32, kind="ExternalOutput").ap()

    with k.es:
        k.open()
        c = emit_consts(k)
        xfm = [k.sb(f"xfm{f}", [P, T], F32) for f in range(NF)]
        emit_load_stream_fm(k, c, x_in, xfm)
        if B0_STAGE >= 1:
            emit_b0(k, c, xfm, w)
        out_toks = emit_store_stream_fm(k, c, xfm, x_out)
        k.finish(out_toks)
        k.replay()
    return k.nc


def _b0_inputs(inputs, ci, x_cur):
    g = lambda n: np.ascontiguousarray(np.asarray(inputs["L0_attn_" + n], dtype=np.float32))
    return {
        "x_in": np.ascontiguousarray(np.asarray(x_cur, dtype=np.float32)),
        "w_conv_w": g("conv_w"), "w_norm1_w": g("norm1_w"), "w_norm2_w": g("norm2_w"),
        "w_Wdecay": g("Wdecay"), "w_bdecay": g("bdecay"), "w_Wgate": g("Wgate"),
        "w_bgate": g("bgate"),
        "w_state_scale": g("state_scale").reshape(1),
        "w_Wk": g("Wk"), "w_M_T": np.ascontiguousarray(g("M").T), "w_Wmem": g("Wmem"),
        "w_mem_scale": g("mem_scale").reshape(1),
        "w_Wc": g("Wc"), "w_bc": g("bc"), "w_Wup": g("Wup"), "w_Wg": g("Wg"),
        "w_Wdown": g("Wdown"),
    }


def build_full_program():
    k = K()
    nc = k.nc
    x_in = nc.dram_tensor("x_in", [T, D], F32, kind="ExternalInput").ap()
    w0 = {n: nc.dram_tensor("b0w_" + n, B0_SHAPES[n], F32, kind="ExternalInput").ap()
          for n in B0_SHAPES}
    w0["mx_scr"] = nc.dram_tensor("mx_scr", [P, NF, T], F32, kind="Internal").ap()
    w1 = {n: nc.dram_tensor("b1w_" + n, B1_SHAPES[n][0], B1_SHAPES[n][1],
                            kind="ExternalInput").ap() for n in B1_SHAPES}
    w2 = {n: nc.dram_tensor("b2w_" + n, B2_SHAPES[n], F32, kind="ExternalInput").ap()
          for n in B2_SHAPES}
    w3 = {n: nc.dram_tensor("b3w_" + n, B3_SHAPES[n][0], B3_SHAPES[n][1],
                            kind="ExternalInput").ap() for n in B3_SHAPES}
    str_scr = nc.dram_tensor("str_scr", [T, D], F32, kind="Internal").ap()
    x_out = nc.dram_tensor("x_out", [T, D], F32, kind="ExternalOutput").ap()
    with k.es:
        k.open()
        c = emit_consts(k)
        with k.scope():
            xfm = [k.sb(f"xfm{f}", [P, T], F32) for f in range(NF)]
            emit_load_stream_fm(k, c, x_in, xfm)
            with k.scope():
                emit_b0(k, c, xfm, w0)
            with k.scope():
                emit_b1(k, c, xfm, w1)
            with k.scope():
                emit_b2(k, c, xfm, w2)
            seam_toks = emit_store_stream_fm(k, c, xfm, str_scr)
        k.wait("qs", *seam_toks)
        k.join()
        xfmh = k.sb("b3_xfmh", [P, NF * T], F16)
        st_toks = emit_b3(k, c, str_scr, xfmh, w3, x_out)
        k.finish(st_toks)
        k.replay()
    return k.nc


def _full_inputs(inputs, ci):
    m = {"x_in": np.ascontiguousarray(
        np.asarray(inputs["L0_attn_x"][ci], dtype=np.float32))}
    d0 = _b0_inputs(inputs, ci, inputs["L0_attn_x"][ci])
    d1 = _b1_inputs(inputs, ci, None)
    d2 = _b2_inputs(inputs, ci, None)
    d3 = _b3_inputs(inputs, ci, None)
    for dd, pref in [(d0, "b0w_"), (d1, "b1w_"), (d2, "b2w_"), (d3, "b3w_")]:
        for k2, v in dd.items():
            if k2 == "x_in":
                continue
            assert k2.startswith("w_")
            m[pref + k2[2:]] = v
    return m


def kernel(**inputs):
    import os
    if os.environ.get("KMODE") == "unfused":
        x = np.asarray(inputs["L0_attn_x"], dtype=np.float32)
        cur = [np.ascontiguousarray(x[i]) for i in range(NCORES)]
        progs = [
            (build_b0_program, _b0_inputs),
            (build_b1_program, _b1_inputs),
            (build_b2_program, _b2_inputs),
            (build_b3_program, _b3_inputs),
        ]
        for build, mkin in progs:
            nc = build()
            in_maps = [mkin(inputs, i, cur[i]) for i in range(NCORES)]
            res = run_bass_kernel_spmd(nc, in_maps, core_ids=list(range(NCORES)))
            cur = [np.asarray(res.results[i]["x_out"]) for i in range(NCORES)]
        return np.stack(cur, axis=0)
    nc = build_full_program()
    in_maps = [_full_inputs(inputs, i) for i in range(NCORES)]
    res = run_bass_kernel_spmd(nc, in_maps, core_ids=list(range(NCORES)))
    return np.stack([np.asarray(res.results[i]["x_out"]) for i in range(NCORES)],
                    axis=0)


# version 32
# speedup vs baseline: 1.0076x; 1.0076x over previous
import numpy as np
from contextlib import ExitStack, contextmanager

import concourse.bass as bass
import concourse.mybir as mybir
from concourse.bass_utils import run_bass_kernel_spmd

F32 = mybir.dt.float32
BF16 = mybir.dt.bfloat16
F16 = mybir.dt.float16
I32 = mybir.dt.int32
I8 = mybir.dt.int8
AF = mybir.ActivationFunctionType
ALU = mybir.AluOpType
AX = mybir.AxisListType

P = 128
T = 2048
D = 1024
NF = D // P
NCORES = 8
S = 256
NCH = T // S
LN_EPS = 1e-5
RMS_EPS = 1e-6
LOG1EM7 = -16.11809565095832


class K:
    STREAMS = ("pe", "dv", "ac", "pl", "qs")

    def __init__(self):
        self.nc = bass.Bass("TRN2", target_bir_lowering=False)
        self.es = ExitStack()
        self.plan = {e: [] for e in self.STREAMS}
        self.cnt = {s: 0 for s in self.STREAMS}
        self.sem = {}
        self.hw = {e: {} for e in self.STREAMS}
        self._scopes = []

    def open(self):
        for s in self.STREAMS:
            self.sem[s] = self.es.enter_context(self.nc.semaphore("sem_" + s))

    def dma_sem(self, key):
        if key not in self.sem:
            self.sem[key] = self.es.enter_context(self.nc.semaphore(key))
            self.cnt[key] = 0
        return self.sem[key]

    def _owner(self):
        return self._scopes[-1] if self._scopes else self.es

    def sb(self, name, shape, dt):
        return self._owner().enter_context(self.nc.sbuf_tensor(name, shape, dt))

    def ps(self, name, shape, dt=F32):
        return self._owner().enter_context(self.nc.psum_tensor(name, shape, dt))

    @contextmanager
    def scope(self):
        s = ExitStack()
        self._scopes.append(s)
        try:
            yield s
        finally:
            self._scopes.pop()
            s.close()

    def wait(self, eng, *tokens):
        best = {}
        for tok in tokens:
            if tok is None:
                continue
            s, v = tok
            if v > best.get(s, 0):
                best[s] = v
        for s, v in best.items():
            if v <= 0 or self.hw[eng].get(s, 0) >= v:
                continue
            self.hw[eng][s] = v
            sem = self.sem[s]

            def run(h, sem=sem, v=v):
                h.wait_ge(sem, v)

            self.plan[eng].append(run)

    def op(self, eng, fn, after=()):
        self.wait(eng, *after)
        self.cnt[eng] += 1
        n = self.cnt[eng]
        sem = self.sem[eng]

        def run(h, fn=fn, sem=sem):
            fn(h).then_inc(sem, 1)

        self.plan[eng].append(run)
        return (eng, n)

    def dma(self, queue, key, out, in_, after=(), **kw):
        stream = "qs" if queue == "qs" else "pl"
        sem = self.dma_sem(key)
        self.wait(stream, *after)
        self.cnt[key] += 16
        n = self.cnt[key]

        def run(h, out=out, in_=in_, sem=sem, kw=kw):
            h.dma_start(out=out, in_=in_, **kw).then_inc(sem, 16)

        self.plan[stream].append(run)
        return (key, n)

    def snap(self):
        return [(s, v) for s, v in self.cnt.items() if v > 0]

    def join(self):
        toks = self.snap()
        for e in self.STREAMS:
            self.wait(e, *toks)

    def finish(self, final_tokens):
        self.wait("qs", *final_tokens)

    def replay(self):
        blk = self.es.enter_context(self.nc.Block())
        plans = self.plan

        @blk.tensor
        def _(h):
            for f in plans["pe"]:
                f(h)

        @blk.vector
        def _(h):
            for f in plans["dv"]:
                f(h)

        @blk.scalar
        def _(h):
            for f in plans["ac"]:
                f(h)

        @blk.gpsimd
        def _(h):
            for f in plans["pl"]:
                f(h)

        @blk.sync
        def _(h):
            for f in plans["qs"]:
                f(h)


def row1(ap):
    return ap.rearrange("(a d) -> a d", a=1)



def emit_consts(k):
    c = {}
    c["ones_row_bf"] = k.sb("ones_row_bf", [1, 512], BF16)
    c["ones_col_bf"] = k.sb("ones_col_bf", [128, 1], BF16)
    c["ones_row_f"] = k.sb("ones_row_f", [1, 128], F32)
    c["ident_f"] = k.sb("ident_f", [128, 128], F32)
    t1 = k.op("pl", lambda h: h.memset(c["ones_row_bf"][:], 1.0))
    t2 = k.op("pl", lambda h: h.memset(c["ones_col_bf"][:], 1.0))
    t3 = k.op("pl", lambda h: h.memset(c["ones_row_f"][:], 1.0))
    tz = k.op("pl", lambda h: h.memset(c["ident_f"][:], 0.0))
    t4 = k.op("pl", lambda h: h.affine_select(
        out=c["ident_f"][:], in_=c["ident_f"][:], compare_op=ALU.not_equal,
        fill=1.0, base=0, pattern=[[-1, 128]], channel_multiplier=1), after=[tz])
    c["cpool"] = k.sb("cpool", [1, 4], F32)
    c["ccols"] = k.sb("ccols", [P, 4], F32)
    t5 = k.op("pl", lambda h: h.memset(c["cpool"][0:1, 0:1], RMS_EPS))
    t6 = k.op("pl", lambda h: h.memset(c["cpool"][0:1, 1:2], LN_EPS))
    t7 = k.op("pl", lambda h: h.memset(c["cpool"][0:1, 2:3], 1.0))
    t8 = k.op("pl", lambda h: h.memset(c["ccols"][:, 0:1], 1e-6))
    t9 = k.op("pl", lambda h: h.memset(c["ccols"][:, 1:2], 1.0))
    t10 = k.op("pl", lambda h: h.memset(c["ccols"][:, 2:3], LN_EPS))
    c["tok"] = [t1, t2, t3, t4, t5, t6, t7, t8, t9, t10]
    return c


def vec_to_cols(k, c, row_ap, ncols, psbank, dst, dst_col0, after, guard):
    mm_toks = []
    for f in range(ncols):
        mm_toks.append(k.op("pe", lambda h, f=f: h.matmul(
            psbank[:, f:f + 1], row_ap[0:1, f * 128:(f + 1) * 128],
            c["ones_row_f"][0:1, 0:1], start=(f == 0), stop=(f == ncols - 1),
            skip_group_check=True),
            after=list(after) + c["tok"] + ([guard] if f == 0 else [])))
    return k.op("dv", lambda h: h.tensor_copy(dst[:, dst_col0:dst_col0 + ncols],
                                              psbank[:, 0:ncols]), after=mm_toks)



def emit_load_stream_fm(k, c, src_ap, xfm):
    with k.scope():
        stag = [k.sb(f"ldstag{i}", [P, D], F32) for i in range(2)]
        psT = [k.ps(f"ldpsT{i}", [P, 512], F32) for i in range(2)]
        dma_tok, tr_tok, ev_tok = {}, {}, {}
        for tb in range(T // P):
            b = tb % 2
            after = [tr_tok.get((tb - 2, f)) for f in range(NF)]
            dma_tok[tb] = k.dma("qs", f"q_ld{b}", stag[b][:],
                                src_ap[tb * P:(tb + 1) * P, :], after=after)
            for f in range(NF):
                bank, slot = f // 4, f % 4
                tr_tok[(tb, f)] = k.op(
                    "pe", lambda h, b=b, f=f, bank=bank, slot=slot:
                    h.transpose(psT[bank][:, slot * 128:(slot + 1) * 128],
                                stag[b][:, f * 128:(f + 1) * 128], c["ident_f"][:]),
                    after=[dma_tok[tb]] + [ev_tok.get((tb - 1, bank * 4 + s))
                                           for s in range(4)] + c["tok"])
            for f in range(NF):
                bank, slot = f // 4, f % 4
                ev_tok[(tb, f)] = k.op(
                    "dv", lambda h, f=f, bank=bank, slot=slot, tb=tb:
                    h.tensor_copy(xfm[f][:, tb * P:(tb + 1) * P],
                                  psT[bank][:, slot * 128:(slot + 1) * 128]),
                    after=[tr_tok[(tb, bank * 4 + 3)]])
        k.join()


def emit_store_stream_fm(k, c, xfm, dst_ap):
    out_toks = []
    with k.scope():
        stag = [k.sb(f"ststag{i}", [P, D], F32) for i in range(2)]
        psT = [k.ps(f"stpsT{i}", [P, 512], F32) for i in range(2)]
        dma_tok, tr_tok, ev_tok = {}, {}, {}
        for tb in range(T // P):
            b = tb % 2
            for f in range(NF):
                bank, slot = f // 4, f % 4
                tr_tok[(tb, f)] = k.op(
                    "pe", lambda h, f=f, tb=tb, bank=bank, slot=slot:
                    h.transpose(psT[bank][:, slot * 128:(slot + 1) * 128],
                                xfm[f][:, tb * P:(tb + 1) * P], c["ident_f"][:]),
                    after=[ev_tok.get((tb - 1, bank * 4 + s)) for s in range(4)]
                    + [dma_tok.get(tb - 2)] + c["tok"])
            for f in range(NF):
                bank, slot = f // 4, f % 4
                ev_tok[(tb, f)] = k.op(
                    "dv", lambda h, b=b, f=f, bank=bank, slot=slot:
                    h.tensor_copy(stag[b][:, f * 128:(f + 1) * 128],
                                  psT[bank][:, slot * 128:(slot + 1) * 128]),
                    after=[tr_tok[(tb, bank * 4 + 3)]])
            dma_tok[tb] = k.dma("qs", f"q_st{b}", dst_ap[tb * P:(tb + 1) * P, :],
                                stag[b][:],
                                after=[ev_tok[(tb, f)] for f in range(NF)])
            out_toks.append(dma_tok[tb])
    return out_toks



def emit_rms_chunk(k, c, src_view, w_cols, wcol0, out_view, sq, psS, psBC,
                   row_scr, rstd_bc, guard, extra_after=()):
    sq_toks = []
    for f in range(NF):
        sq_toks.append(k.op("ac", lambda h, f=f: h.activation(
            sq[:, f * S:(f + 1) * S], src_view(f), AF.Square),
            after=list(extra_after)))
    mm = []
    for f in range(NF):
        mm.append(k.op("pe", lambda h, f=f: h.matmul(
            psS[0:1, 0:S], c["ones_col_bf"][:, 0:1], sq[:, f * S:(f + 1) * S],
            start=(f == 0), stop=(f == NF - 1), skip_group_check=True),
            after=[sq_toks[f]] + c["tok"] + ([guard.get("psS")] if f == 0 else [])))
    t = k.op("ac", lambda h: h.activation(row_scr[0:1, 0:S], psS[0:1, 0:S],
                                          AF.Sqrt, bias=c["cpool"][0:1, 0:1],
                                          scale=1.0 / D),
             after=[mm[-1]] + c["tok"])
    guard["psS"] = t
    t = k.op("dv", lambda h: h.reciprocal(row_scr[0:1, 0:S], row_scr[0:1, 0:S]),
             after=[t])
    t = k.op("pe", lambda h: h.matmul(psBC[:, 0:S], c["ones_row_f"][0:1, :],
                                      row_scr[0:1, 0:S], start=True, stop=True,
                                      skip_group_check=True),
             after=[t, guard.get("psBC")])
    t = k.op("dv", lambda h: h.tensor_copy(rstd_bc[:, 0:S], psBC[:, 0:S]), after=[t])
    guard["psBC"] = t
    out_toks = []
    for f in range(NF):
        out_toks.append(k.op("dv", lambda h, f=f: h.scalar_tensor_tensor(
            out_view(f), src_view(f), w_cols[:, wcol0 + f:wcol0 + f + 1],
            rstd_bc[:, 0:S], ALU.mult, ALU.mult),
            after=[t] + list(extra_after)))
    return out_toks


def emit_ln_chunk(k, c, xfm, cs, sq, ybf, psS, psBC, row_scr, ab_bc, tmp,
                  guard, in_toks=()):
    cast_toks, sq_toks = [], []
    for f in range(NF):
        ct = k.op("ac", lambda h, f=f: h.activation(
            ybf[:, f * S:(f + 1) * S], xfm[f][:, cs], AF.Copy), after=list(in_toks))
        cast_toks.append(ct)
        sq_toks.append(k.op("ac", lambda h, f=f: h.activation(
            sq[:, f * S:(f + 1) * S], ybf[:, f * S:(f + 1) * S], AF.Square),
            after=[ct]))
    mm = []
    nmm = 2 * NF
    i = 0
    for f in range(NF):
        mm.append(k.op("pe", lambda h, f=f, i=i: h.matmul(
            psS[0:1, 0:S], c["ones_col_bf"][:, 0:1], sq[:, f * S:(f + 1) * S],
            start=(i == 0), stop=(i == nmm - 1), skip_group_check=True),
            after=[sq_toks[f]] + c["tok"] + ([guard.get("psS")] if i == 0 else [])))
        i += 1
        mm.append(k.op("pe", lambda h, f=f, i=i: h.matmul(
            psS[0:1, S:2 * S], c["ones_col_bf"][:, 0:1], ybf[:, f * S:(f + 1) * S],
            start=(i == 0), stop=(i == nmm - 1), skip_group_check=True),
            after=[cast_toks[f]]))
        i += 1
    rR = row_scr[0:1, 0:256]
    rM = row_scr[0:1, 256:512]
    rV = row_scr[0:1, 512:768]
    rQ = row_scr[0:1, 768:1024]
    t = k.op("ac", lambda h: h.activation(rM[:, 0:S], psS[0:1, S:2 * S], AF.Copy,
                                          scale=1.0 / D), after=[mm[-1]])
    t = k.op("ac", lambda h: h.activation(rQ[:, 0:S], rM[:, 0:S], AF.Square),
             after=[t])
    t = k.op("dv", lambda h: h.scalar_tensor_tensor(
        rV[:, 0:S], psS[0:1, 0:S], 1.0 / D, rQ[:, 0:S], ALU.mult, ALU.subtract),
        after=[t])
    guard["psS"] = t
    t = k.op("ac", lambda h: h.activation(rR[:, 0:S], rV[:, 0:S], AF.Sqrt,
                                          bias=c["cpool"][0:1, 1:2]),
             after=[t] + c["tok"])
    t = k.op("dv", lambda h: h.reciprocal(rR[:, 0:S], rR[:, 0:S]), after=[t])
    tb = k.op("dv", lambda h: h.scalar_tensor_tensor(
        rM[:, 0:S], rM[:, 0:S], -1.0, rR[:, 0:S], ALU.mult, ALU.mult), after=[t])
    t1 = k.op("pe", lambda h: h.matmul(psBC[:, 0:S], c["ones_row_f"][0:1, :],
                                       rR[:, 0:S], start=True, stop=False,
                                       skip_group_check=True),
              after=[tb, guard.get("psBC")])
    t2 = k.op("pe", lambda h: h.matmul(psBC[:, S:2 * S], c["ones_row_f"][0:1, :],
                                       rM[:, 0:S], start=False, stop=True,
                                       skip_group_check=True), after=[tb])
    t = k.op("dv", lambda h: h.tensor_copy(ab_bc[:, 0:2 * S], psBC[:, 0:2 * S]),
             after=[t1, t2])
    guard["psBC"] = t
    out_toks = []
    prev2 = None
    for f in range(NF):
        tt = k.op("dv", lambda h, f=f: h.tensor_tensor(
            tmp[:, 0:S], xfm[f][:, cs], ab_bc[:, 0:S], ALU.mult),
            after=[t, prev2] + list(in_toks))
        prev2 = k.op("dv", lambda h, f=f: h.tensor_tensor(
            xfm[f][:, cs], tmp[:, 0:S], ab_bc[:, S:2 * S], ALU.add), after=[tt])
        out_toks.append(prev2)
    return out_toks



def emit_b0(k, c, xfm, w):
    import os
    B0_PHASE = int(os.environ.get("B0_PHASE", "99"))
    NCH_MIX = int(os.environ.get("B0_NCH", str(NCH)))
    NCH_FFN = int(os.environ.get("B0_FFN", str(NCH)))
    nc = k.nc
    cols = k.sb("b0_cols", [P, 32], F32)
    cw_cols = k.sb("b0_cwcols", [P, 56], F32)
    brow = k.sb("b0_brow", [1, 2048], BF16)
    row_scr = k.sb("b0_rowscr", [1, 1024], F32)
    rstd_bc = k.sb("b0_rstdbc", [P, 256], F32)
    carry = k.sb("b0_carry", [P, 16], F32)
    guard = {}

    with k.scope():
        psS = k.ps("b0_psS", [P, 512])
        psBC = k.ps("b0_psBC", [P, 512])
        vrow = k.sb("b0_vrow", [1, 3 * D], F32)
        srow = k.sb("b0_srow", [1, 2], F32)
        k.dma("qs", "q_b0c", vrow[0:1, 0:D], row1(w["norm1_w"]))
        k.dma("qs", "q_b0c", vrow[0:1, D:2 * D], row1(w["norm2_w"]))
        k.dma("qs", "q_b0c", vrow[0:1, 2 * D:3 * D], row1(w["bc"]))
        k.dma("qs", "q_b0c", srow[0:1, 0:1], row1(w["state_scale"]))
        k.dma("qs", "q_b0c", srow[0:1, 1:2], row1(w["mem_scale"]))
        t_bd = k.dma("qg", "q_b0cg", brow[0:1, 0:D], row1(w["bdecay"]))
        t_bg = k.dma("qg", "q_b0cg2", brow[0:1, D:2 * D], row1(w["bgate"]))
        t_cv = k.dma("qs", "q_b0c", cw_cols[:, :].rearrange("p (f j) -> p f j", j=7),
                     w["conv_w"].rearrange("(f p) j -> p f j", p=128))
        g1 = vec_to_cols(k, c, vrow[0:1, 0:D], 8, psS, cols, 0, [t_cv], None)
        g2 = vec_to_cols(k, c, vrow[0:1, D:2 * D], 8, psS, cols, 8, [t_cv], g1)
        g3 = vec_to_cols(k, c, vrow[0:1, 2 * D:3 * D], 8, psS, cols, 16, [t_cv], g2)
        guard["psS"] = g3
        t = k.op("pe", lambda h: h.matmul(psBC[:, 0:2], c["ones_row_f"][0:1, :],
                                          srow[0:1, 0:2], start=True, stop=True,
                                          skip_group_check=True),
                 after=[t_cv] + c["tok"])
        guard["psBC"] = k.op("dv", lambda h: h.tensor_copy(cols[:, 24:26], psBC[:, 0:2]),
                             after=[t])
        k.op("pl", lambda h: h.memset(carry[:], 0.0))
        k.join()

    with k.scope():
        h32 = k.sb("b0_h32", [P, NF * 264], BF16)
        cvt = k.sb("b0_cvt", [P, NF * S], BF16)
        sq = k.sb("b0_sq", [P, NF * S], BF16)
        SA = k.sb("b0_SA", [P, NF * S], F32)
        SB = k.sb("b0_SB", [P, NF * S], F32)
        SC = k.sb("b0_SC", [P, NF * S], F32)
        local_bf = k.sb("b0_local", [P, 2 * NF * S], BF16)
        state_bf = k.sb("b0_state", [P, NF * S], BF16)
        mem_bf = k.sb("b0_mem", [P, NF * S], BF16)
        k_bf = k.sb("b0_kbf", [P, 512], BF16)
        mo_bf = k.sb("b0_mobf", [P, 1024], BF16)
        wd_sb = k.sb("b0_wd", [P, 8192], BF16)
        wg_sb = k.sb("b0_wg", [P, 8192], BF16)
        wk_sb = k.sb("b0_wk", [P, 2048], BF16)
        mt_sb = k.sb("b0_mt", [P, 512], BF16)
        wmem_sb = k.sb("b0_wmem", [P, 2048], BF16)
        wc_sb = [k.sb(f"b0_wc{i}", [P, 8192], BF16) for i in range(2)]
        psA = [k.ps(f"b0_psA{i}", [P, 512]) for i in range(4)]
        psDG = [k.ps(f"b0_psDG{i}", [P, 512]) for i in range(2)]
        psB0 = k.ps("b0_psB0", [P, 512])
        psB1 = k.ps("b0_psB1", [P, 512])

        h3 = lambda off, w: h32[:].rearrange("p (f r) -> p f r", r=264)[:, :, off:off + w]
        SA3 = lambda: SA[:].rearrange("p (f s) -> p f s", s=S)
        SB3 = lambda: SB[:].rearrange("p (f s) -> p f s", s=S)
        SC3 = lambda: SC[:].rearrange("p (f s) -> p f s", s=S)
        cvt3 = lambda: cvt[:].rearrange("p (f s) -> p f s", s=S)
        car3 = lambda o: carry[:, o:o + 8].rearrange("p (f j) -> p f j", j=1)
        cw_bc = lambda j: cw_cols[:].rearrange("p (f j) -> p f j", j=7)[
            :, :, j:j + 1].broadcast_to([P, NF, S])

        def ld_kxm(key, dst, src2d, m):
            return k.dma("qg", key, dst[:].rearrange("p (a m) -> p a m", m=m),
                         src2d.rearrange("(a p) m -> p a m", p=128))

        t_wd = ld_kxm("q_bw0", wd_sb, w["Wdecay"], 1024)
        t_wg = ld_kxm("q_bw1", wg_sb, w["Wgate"], 1024)
        t_wk = ld_kxm("q_bw2", wk_sb, w["Wk"], 256)
        t_mt = ld_kxm("q_bw3", mt_sb, w["M_T"], 256)
        t_wm = ld_kxm("q_bw4", wmem_sb, w["Wmem"], 1024)

        wc_dma = {}
        wc_user = {}

        def load_wc(idx):
            ci, third = divmod(idx, 3)
            buf = idx % 2
            wc_dma[idx] = k.dma(
                "qg", f"q_wc{buf}", wc_sb[buf][:].rearrange("p (a m) -> p a m", m=1024),
                w["Wc"][third * 1024:(third + 1) * 1024, :].rearrange(
                    "(a p) m -> p a m", p=128),
                after=[wc_user.get(idx - 2)])

        load_wc(0)
        load_wc(1)

        t_tail0 = [k.op("pl", lambda h, f=f: h.memset(h32[:, f * 264 + 2:f * 264 + 8], 0.0))
                   for f in range(NF)]

        prev_h_toks = None
        prev_hreaders = []
        kmm_prev = None
        evA_prev = [None] * 4
        evB0_prev = None
        evB1_prev = None
        bf_users_prev = []
        dg_guard = [None, None]
        prev_state = None
        sc_guard = None
        mx_out_toks = []
        pend = None
        pend_d = {}

        def emit_memB(pc):
            nonlocal bf_users_prev
            d = pend_d[pc]
            po = d["po"]
            mem_last = {}
            for p in range(4):
                i = 0
                for half in range(2):
                    m = 2 * p + half
                    for kk in range(2):
                        aft = [d["t_mobf"], t_wm]
                        if i == 0:
                            aft.append(evA_prev[p])
                        tok = k.op("pe", lambda h, p=p, half=half, m=m, kk=kk,
                                   i=i, po=po: h.matmul(
                                       psA[p][:, half * 256:half * 256 + 256],
                                       wmem_sb[:, kk * 1024 + m * 128:kk * 1024 + (m + 1) * 128],
                                       mo_bf[:, po + kk * 256:po + (kk + 1) * 256],
                                       start=(i == 0), stop=(i == 3),
                                       skip_group_check=True), after=aft)
                        i += 1
                mem_last[p] = tok
            mem_toks = [k.op("ac", lambda h, p=p: h.activation(
                mem_bf[:, p * 512:(p + 1) * 512], psA[p][:], AF.Copy),
                after=[mem_last[p]] + bf_users_prev) for p in range(4)]

            srcs = [local_bf, state_bf, mem_bf]
            offs = [d["lp"], 0, 0]
            src_tok = [d["local"], d["state"], mem_toks]
            comb_last = {}
            for third in range(3):
                idx = pc * 3 + third
                buf = idx % 2
                dma_t = wc_dma[idx]
                for p in range(4):
                    for half in range(2):
                        m = 2 * p + half
                        for kk in range(8):
                            first = (third == 0 and half == 0 and kk == 0)
                            last = (third == 2 and half == 1 and kk == 7)
                            aft = [dma_t] if kk == 0 else []
                            if first:
                                aft = aft + [mem_toks[p]]
                            if kk == 0:
                                aft = aft + [src_tok[third][p]]
                            tok = k.op("pe", lambda h, buf=buf, p=p, half=half, m=m,
                                       kk=kk, third=third, first=first, last=last,
                                       off=offs[third]:
                                       h.matmul(
                                           psA[p][:, half * 256:half * 256 + 256],
                                           wc_sb[buf][:, kk * 1024 + m * 128:kk * 1024 + (m + 1) * 128],
                                           srcs[third][:, off + kk * S:off + (kk + 1) * S],
                                           start=first, stop=last,
                                           skip_group_check=True), after=aft)
                        comb_last[(third, m)] = tok
                wc_user[idx] = tok
                if idx + 2 < NCH * 3:
                    load_wc(idx + 2)
            d["comb_last"] = comb_last
            bf_users_prev = [comb_last[(2, 7)]]

        def emit_mixed(pc):
            nonlocal evA_prev, sc_guard
            d = pend_d.pop(pc)
            comb_last = d["comb_last"]
            cs = d["cs"]
            mixed_toks = {}
            evA = [None] * 4
            for m in range(NF):
                p, half = m // 2, m % 2
                tok = k.op("dv", lambda h, m=m, p=p, half=half, cs=cs:
                           h.scalar_tensor_tensor(
                               SB[:, m * S:(m + 1) * S],
                               psA[p][:, half * 256:half * 256 + 256],
                               cols[:, 16 + m:17 + m], xfm[m][:, cs],
                               ALU.add, ALU.add),
                           after=[comb_last[(2, 2 * p)], comb_last[(2, 2 * p + 1)],
                                  sc_guard])
                mixed_toks[m] = tok
                evA[p] = tok
            t_mxout = k.dma("qs", "q_mx", w["mx_scr"][:, :, cs],
                            SB[:].rearrange("p (f s) -> p f s", s=S),
                            after=list(mixed_toks.values()))
            mx_out_toks.append(t_mxout)
            evA_prev = evA
            sc_guard = t_mxout

        for ci in range(NCH_MIX):
            cs = slice(ci * S, (ci + 1) * S)
            po = (ci % 2) * 512
            if ci > 0:
                tail_toks = [k.op("dv", lambda h: h.tensor_copy(
                    h3(2, 6), h3(258, 6)),
                    after=list(prev_h_toks) + prev_hreaders)]
            else:
                tail_toks = t_tail0

            h_toks = emit_rms_chunk(
                k, c, lambda f, cs=cs: xfm[f][:, cs], cols, 0,
                lambda f: h32[:, f * 264 + 8:f * 264 + 264],
                sq, psB0, psB1, row_scr, rstd_bc, guard,
                extra_after=tail_toks + prev_hreaders)

            sigA, sigC = [], []
            for wsb, brow_off, dst, w_tok, sig_list in (
                    (wd_sb, 0, SA, t_wd, sigA), (wg_sb, D, SC, t_wg, sigC)):
                for p in range(4):
                    b = p % 2
                    i = 0
                    last_mm = None
                    for half in range(2):
                        m = 2 * p + half
                        for kk in range(8):
                            aft = []
                            if i == 0:
                                aft = list(h_toks) + [dg_guard[b], w_tok]
                            last_mm = k.op("pe", lambda h, b=b, half=half, m=m,
                                           kk=kk, wsb=wsb, i=i: h.matmul(
                                               psDG[b][:, half * 256:half * 256 + 256],
                                               wsb[:, kk * 1024 + m * 128:kk * 1024 + (m + 1) * 128],
                                               h32[:, kk * 264 + 8:kk * 264 + 264],
                                               start=(i == 0), stop=False,
                                               skip_group_check=True), after=aft)
                            i += 1
                        last_mm = k.op("pe", lambda h, b=b, half=half, m=m,
                                       brow_off=brow_off, i=i: h.matmul(
                                           psDG[b][:, half * 256:half * 256 + 256],
                                           brow[0:1, brow_off + m * 128:brow_off + (m + 1) * 128],
                                           c["ones_row_bf"][0:1, 0:256],
                                           start=False, stop=(i == 17),
                                           skip_group_check=True),
                                       after=[t_bd, t_bg])
                        i += 1
                    tok = k.op("ac", lambda h, b=b, p=p, dst=dst: h.activation(
                        dst[:, p * 512:(p + 1) * 512], psDG[b][:], AF.Sigmoid),
                        after=[last_mm, prev_state])
                    dg_guard[b] = tok
                    sig_list.append(tok)

            kmm = []
            i = 0
            for m2 in range(2):
                for kk in range(8):
                    aft = []
                    if i == 0:
                        aft = [t_wk, evB0_prev, guard.get("psS")] + list(h_toks)
                    kmm.append(k.op("pe", lambda h, m2=m2, kk=kk, i=i: h.matmul(
                        psB0[:, m2 * 256:m2 * 256 + 256],
                        wk_sb[:, kk * 256 + m2 * 128:kk * 256 + (m2 + 1) * 128],
                        h32[:, kk * 264 + 8:kk * 264 + 264],
                        start=(i == 0), stop=(i == 15), skip_group_check=True),
                        after=aft))
                    i += 1
            t_kbf = k.op("ac", lambda h: h.activation(k_bf[:], psB0[:], AF.Copy),
                         after=[kmm[-1]])
            guard["psS"] = t_kbf
            evB0_prev = t_kbf
            momm = []
            i = 0
            for m2 in range(2):
                for kk in range(2):
                    aft = [t_kbf, t_mt]
                    if i == 0:
                        aft.append(evB1_prev)
                        aft.append(guard.get("psBC"))
                    momm.append(k.op("pe", lambda h, m2=m2, kk=kk, i=i: h.matmul(
                        psB1[:, m2 * 256:m2 * 256 + 256],
                        mt_sb[:, kk * 256 + m2 * 128:kk * 256 + (m2 + 1) * 128],
                        k_bf[:, kk * 256:(kk + 1) * 256],
                        start=(i == 0), stop=(i == 3), skip_group_check=True),
                        after=aft))
                    i += 1

            t = k.op("dv", lambda h: h.tensor_tensor(
                SB3(), h3(2, S), cw_bc(0), ALU.mult),
                after=list(h_toks) + [sc_guard] + tail_toks + bf_users_prev)
            for j in range(1, 7):
                t1 = k.op("dv", lambda h, j=j: h.tensor_tensor(
                    cvt3(), h3(2 + j, S), cw_bc(j), ALU.mult),
                    after=list(h_toks) + [t])
                t = k.op("dv", lambda h: h.tensor_tensor(
                    SB3(), SB3(), cvt3(), ALU.add), after=[t, t1])
            conv_tok = t
            conv_toks = [conv_tok]
            t_mobf = k.op("dv", lambda h, po=po: h.tensor_scalar_mul(
                mo_bf[:, po:po + 512], psB1[:], cols[:, 25:26]), after=[momm[-1]])
            guard["psBC"] = t_mobf
            evB1_prev = t_mobf
            lp = (ci % 2) * NF * S
            local_tok = k.op("ac", lambda h, lp=lp: h.activation(
                local_bf[:, lp:lp + NF * S], SB[:], AF.Copy),
                after=[conv_tok] + bf_users_prev)
            if pend is not None:
                emit_memB(pend)

            clip_tok = k.op("dv", lambda h: h.tensor_scalar(
                SA[:], SA[:], 0.05, 0.95, ALU.max, ALU.min), after=sigA)
            ln_tok = k.op("ac", lambda h: h.activation(
                SA[:], SA[:], AF.Ln), after=[clip_tok])
            t0f = k.op("dv", lambda h: h.tensor_scalar_max(
                SB3()[:, :, 0:1], car3(0), LOG1EM7),
                after=[ln_tok, local_tok])
            scan_toks = []
            for m in range(NF):
                a0 = m * S
                scan_toks.append(k.op("dv", lambda h, m=m, a0=a0: h.tensor_tensor_scan(
                    SA[:, a0:a0 + S], SA[:, a0:a0 + S], SA[:, a0:a0 + S],
                    carry[:, m:m + 1], ALU.add, ALU.bypass),
                    after=[ln_tok, t0f]))
            t2f = k.op("dv", lambda h: h.tensor_scalar_max(
                SB3()[:, :, 1:S], SA3()[:, :, 0:S - 1], LOG1EM7),
                after=[scan_toks[-1], local_tok])
            t3f = k.op("dv", lambda h: h.tensor_copy(
                car3(0), SA3()[:, :, S - 1:S]), after=[scan_toks[-1]])
            einv_tok = k.op("ac", lambda h: h.activation(
                SB[:], SB[:], AF.Exp, scale=-1.0), after=[t0f, t2f])
            cd_tok = k.op("ac", lambda h: h.activation(
                SA[:], SA[:], AF.Exp), after=[t2f, t3f])
            upd_tok = k.op("dv", lambda h: h.tensor_tensor(
                SC3(), SC3(), h3(8, S), ALU.mult), after=sigC + list(h_toks))
            upd_toks = [upd_tok]
            wu_tok = k.op("dv", lambda h: h.tensor_tensor(
                SC[:], SC[:], SB[:], ALU.mult), after=[upd_tok, einv_tok])
            cwu_toks = []
            for m in range(NF):
                a0 = m * S
                cwu_toks.append(k.op("dv", lambda h, m=m, a0=a0: h.tensor_tensor_scan(
                    SC[:, a0:a0 + S], SC[:, a0:a0 + S], SC[:, a0:a0 + S],
                    carry[:, 8 + m:9 + m], ALU.add, ALU.bypass), after=[wu_tok]))
            k.op("dv", lambda h: h.tensor_copy(
                car3(8), SC3()[:, :, S - 1:S]), after=[cwu_toks[-1]])
            state_tok = k.op("dv", lambda h: h.scalar_tensor_tensor(
                state_bf[:], SC[:], cols[:, 24:25], SA[:], ALU.mult, ALU.mult),
                after=[cwu_toks[-1], cd_tok] + bf_users_prev)
            state_toks = [state_tok] * 4
            local_toks = [local_tok] * 4

            if pend is not None:
                emit_mixed(pend)
            pend = ci
            pend_d[ci] = {"cs": cs, "po": po, "t_mobf": t_mobf, "lp": lp,
                          "local": local_toks, "state": state_toks}
            prev_state = state_tok
            prev_h_toks = h_toks
            prev_hreaders = [conv_tok, upd_tok, kmm[-1]]
            kmm_prev = kmm[-1]
        emit_memB(pend)
        emit_mixed(pend)
        k.join()

    T2 = 4 * S
    with k.scope():
        h2all = k.sb("b0_h2all", [P, NF * T2], BF16)
        yacc = k.sb("b0_yacc", [P, NF * T2], F32)
        mx_in = [k.sb(f"b0_mxin{i}", [P, NF * S], F32) for i in range(2)]
        ab_bc = k.sb("b0_abbc", [P, 512], F32)
        tmp = k.sb("b0_tmp", [P, 256], F32)
        sq2 = k.sb("b0_sq2", [P, NF * S], BF16)
        silu_bf = k.sb("b0_silu", [P, 512], BF16)
        prod_bf = k.sb("b0_prod", [P, 1024], BF16)
        ybf = k.sb("b0_ybf", [P, NF * S], BF16)
        wgg = [k.sb(f"b0_wgg{i}", [P, 8 * 512], BF16) for i in range(2)]
        wuu = [k.sb(f"b0_wuu{i}", [P, 8 * 512], BF16) for i in range(2)]
        wdd = [k.sb(f"b0_wdd{i}", [P, 4 * 1024], BF16) for i in range(2)]
        psF = [k.ps(f"b0_psF{i}", [P, 512]) for i in range(4)]
        psG = k.ps("b0_psG", [P, 512])
        psU = k.ps("b0_psU", [P, 512])
        psS = k.ps("b0_psS2", [P, 512])
        psBC = k.ps("b0_psBC2", [P, 512])

        NB = 8
        wdma = {}
        wuser = {}

        def load_ffn_blk(idx):
            hi, bi = divmod(idx, NB)
            buf = idx % 2
            aft = [wuser.get(idx - 2)]
            ta = k.dma("qg", f"q_fba{buf}", wgg[buf][:].rearrange("p (a m) -> p a m", m=512),
                       w["Wg"][:, bi * 512:(bi + 1) * 512].rearrange(
                           "(a p) m -> p a m", p=128), after=aft)
            tb = k.dma("qg", f"q_fbb{buf}", wuu[buf][:].rearrange("p (a m) -> p a m", m=512),
                       w["Wup"][:, bi * 512:(bi + 1) * 512].rearrange(
                           "(a p) m -> p a m", p=128), after=aft)
            t3 = k.dma("qg", f"q_fbc{buf}", wdd[buf][:].rearrange("p (a m) -> p a m", m=1024),
                       w["Wdown"][bi * 512:(bi + 1) * 512, :].rearrange(
                           "(a p) m -> p a m", p=128), after=aft)
            wdma[idx] = (ta, tb, t3)

        load_ffn_blk(0)
        load_ffn_blk(1)

        yac3 = lambda p, tl: yacc[:].rearrange("p (f t) -> p f t", t=T2)[
            :, 2 * p:2 * p + 2, tl * S:tl * S + 256]
        psF3 = lambda p: psF[p][:].rearrange("p (m s) -> p m s", s=256)

        evF_prev = [None] * 4
        yacc_readers = []
        yacc_readers_prev = []
        mx_dma = {}
        mx_user = {}
        ph_pe = None
        silu_prev = None
        prod_prev = None
        dn_last = None
        pend = None
        ln_prev = []
        for hi in range(2):
            h2_tl = {}
            for tl in range(4):
                ti = hi * 4 + tl
                mb = ti % 2
                mx_dma[ti] = k.dma("qs", f"q_mxi{mb}",
                                   mx_in[mb][:].rearrange("p (f s) -> p f s", s=S),
                                   w["mx_scr"][:, :, ti * S:(ti + 1) * S],
                                   after=[mx_user.get(ti - 2)])
                h2_toks = emit_rms_chunk(
                    k, c, lambda f, mb=mb: mx_in[mb][:, f * S:(f + 1) * S], cols, 8,
                    lambda f, tl=tl: h2all[:, f * T2 + tl * S:f * T2 + tl * S + 256],
                    sq2, psS, psBC, row_scr, rstd_bc, guard,
                    extra_after=[mx_dma[ti], ph_pe])
                mx_user[ti] = h2_toks[-1]
                h2_tl[tl] = list(h2_toks)

            def emit_ph3(tl, ev, hi=hi):
                nonlocal ln_prev
                ti = hi * 4 + tl
                mb = ti % 2
                mx_dma[ti + 8] = k.dma(
                    "qs", f"q_mxi{mb}", mx_in[mb][:].rearrange("p (f s) -> p f s", s=S),
                    w["mx_scr"][:, :, ti * S:(ti + 1) * S],
                    after=[mx_user.get(ti + 2 if tl < 2 else ti + 6)])
                t_m = k.op("dv", lambda h, mb=mb, tl=tl: h.tensor_tensor(
                    mx_in[mb][:].rearrange("p (f s) -> p f s", s=S),
                    mx_in[mb][:].rearrange("p (f s) -> p f s", s=S),
                    yacc[:].rearrange("p (f t) -> p f t", t=T2)[:, :, tl * S:tl * S + 256],
                    ALU.add), after=[mx_dma[ti + 8]] + ev + ln_prev)
                yacc_readers.append(t_m)
                y1_toks = []
                for mo in range(NF):
                    tok = k.op("dv", lambda h, mo=mo, mb=mb, ti=ti: h.tensor_tensor(
                        xfm[mo][:, ti * S:(ti + 1) * S], xfm[mo][:, ti * S:(ti + 1) * S],
                        mx_in[mb][:, mo * S:(mo + 1) * S], ALU.add), after=[t_m])
                    y1_toks.append(tok)
                mx_user[ti + 8] = y1_toks[-1]
                ln_prev = emit_ln_chunk(k, c, xfm, slice(ti * S, (ti + 1) * S),
                                        sq2, ybf, psS, psBC, row_scr,
                                        ab_bc, tmp, guard, in_toks=y1_toks)

            def emit_dn(pend, hi=hi):
                nonlocal dn_last, evF_prev
                (p_prod, p_td, p_buf, p_bi, p_tl, p_sub, p_idx, p_first, p_last) = pend
                dn = []
                for half in range(2):
                    ht4 = (p_sub * 2 + half) % 4
                    for mo in range(NF):
                        p, mh = mo // 2, mo % 2
                        aft = [p_prod, p_td]
                        if p_first and half == 0 and mh == 0:
                            aft.append(evF_prev[p])
                        dn.append(k.op("pe", lambda h, ht4=ht4, mo=mo, p=p,
                                       mh=mh, buf=p_buf, half=half,
                                       fp=p_first, lp=p_last, ps=p_sub: h.matmul(
                                           psF[p][:, mh * 256:mh * 256 + 256],
                                           wdd[buf][:, ht4 * 1024 + mo * 128:ht4 * 1024 + (mo + 1) * 128],
                                           prod_bf[:, (ps % 2) * 512 + half * 256:(ps % 2) * 512 + half * 256 + 256],
                                           start=(fp and half == 0 and mh == 0),
                                           stop=(lp and half == 1 and mh == 1),
                                           skip_group_check=True), after=aft))
                dn_last = dn[-1]
                if p_sub == 1:
                    ev = []
                    for p in range(4):
                        if p_bi == 0:
                            ev.append(k.op("dv", lambda h, p=p, tl=p_tl:
                                           h.tensor_copy(yac3(p, tl), psF3(p)),
                                           after=[dn_last] + yacc_readers_prev))
                        else:
                            ev.append(k.op("dv", lambda h, p=p, tl=p_tl:
                                           h.tensor_tensor(yac3(p, tl), yac3(p, tl),
                                                           psF3(p), ALU.add),
                                           after=[dn_last]))
                    evF_prev = ev
                    if p_tl == 3:
                        wuser[p_idx] = dn_last
                        if p_idx + 2 < 2 * NB:
                            load_ffn_blk(p_idx + 2)
                    if p_bi == NB - 1:
                        emit_ph3(p_tl, ev)

            for bi in range(NB):
                idx = hi * NB + bi
                buf = idx % 2
                tg, tu, td = wdma[idx]
                for tl in range(4):
                    for sub in range(2):
                        gm = []
                        for half in range(2):
                            ht4 = (sub * 2 + half) % 4
                            pcol = half * 256
                            for kk in range(8):
                                i = half * 8 + kk
                                aft = []
                                if i == 0:
                                    aft = [tg, silu_prev] + (h2_tl[tl] if bi == 0 else [])
                                gm.append(k.op("pe", lambda h, ht4=ht4, kk=kk, buf=buf,
                                               pcol=pcol, i=i, tl=tl: h.matmul(
                                                   psG[:, pcol:pcol + 256],
                                                   wgg[buf][:, kk * 512 + ht4 * 128:kk * 512 + (ht4 + 1) * 128],
                                                   h2all[:, kk * T2 + tl * S:kk * T2 + tl * S + 256],
                                                   start=(i == 0), stop=(i == 15),
                                                   skip_group_check=True), after=aft))
                        um = []
                        for half in range(2):
                            ht4 = (sub * 2 + half) % 4
                            pcol = half * 256
                            for kk in range(8):
                                i = half * 8 + kk
                                aft = []
                                if i == 0:
                                    aft = [tu, prod_prev]
                                um.append(k.op("pe", lambda h, ht4=ht4, kk=kk, buf=buf,
                                               pcol=pcol, i=i, tl=tl: h.matmul(
                                                   psU[:, pcol:pcol + 256],
                                                   wuu[buf][:, kk * 512 + ht4 * 128:kk * 512 + (ht4 + 1) * 128],
                                                   h2all[:, kk * T2 + tl * S:kk * T2 + tl * S + 256],
                                                   start=(i == 0), stop=(i == 15),
                                                   skip_group_check=True), after=aft))
                        silu_now = k.op("ac", lambda h: h.activation(
                            silu_bf[:], psG[:], AF.Silu),
                            after=[gm[-1], prod_prev])
                        prod_now = k.op("dv", lambda h, sub=sub: h.tensor_tensor(
                            prod_bf[:, (sub % 2) * 512:(sub % 2) * 512 + 512],
                            psU[:], silu_bf[:], ALU.mult),
                            after=[silu_now, um[-1], dn_last])
                        if pend is not None:
                            emit_dn(pend)
                        pend = (prod_now, td, buf, bi, tl, sub, idx,
                                sub == 0, sub == 1)
                        silu_prev = silu_now
                        prod_prev = prod_now
            emit_dn(pend)
            pend = None
            ph_pe = ("pe", k.cnt["pe"])
            yacc_readers_prev = yacc_readers
            yacc_readers = []
            for tl in range(4):
                mx_user[hi * 4 + tl] = mx_user[hi * 4 + tl + 8]
        k.join()





def emit_b1(k, c, xfm, w):
    import os
    NCH_B1 = int(os.environ.get("B1_NCH", str(NCH)))
    SCALE = 1.0 / 127.0
    with k.scope():
        row_scr = k.sb("b1_rowscr", [1, 1024], F32)
        ab_bc = k.sb("b1_abbc", [P, 512], F32)
        tmp = k.sb("b1_tmp", [P, 256], F32)
        sq2 = k.sb("b1_sq", [P, NF * S], BF16)
        ybf = k.sb("b1_ybf", [P, NF * S], BF16)
        x1all = k.sb("b1_x1all", [P, NF * T], BF16)
        yacc = k.sb("b1_yacc", [P, NF * 4 * S], F32)
        silu_bf = k.sb("b1_silu", [P, 512], BF16)
        prod_bf = k.sb("b1_prod", [P, 1024], BF16)
        wgg = [k.sb(f"b1_wgg{i}", [P, 8 * 512], BF16) for i in range(2)]
        wuu = [k.sb(f"b1_wuu{i}", [P, 8 * 512], BF16) for i in range(2)]
        wdd = [k.sb(f"b1_wdd{i}", [P, 4 * 1024], BF16) for i in range(2)]
        psS = k.ps("b1_psS", [P, 512])
        psBC = k.ps("b1_psBC", [P, 512])
        psF = [k.ps(f"b1_psF{i}", [P, 512]) for i in range(4)]
        psG = k.ps("b1_psG", [P, 512])
        psU = k.ps("b1_psU", [P, 512])
        guard = {}

        NB = 6
        wdma = {}
        wuser = {}

        def load_blk(idx):
            ti, bi = divmod(idx, NB)
            buf = idx % 2
            aft = [wuser.get(idx - 2)]
            ta = k.dma("qg", f"q_b1fa{buf}", wgg[buf][:].rearrange("p (a m) -> p a m", m=512),
                       w["wgT"][:, bi * 512:(bi + 1) * 512].rearrange(
                           "(a p) m -> p a m", p=128), after=aft)
            tb = k.dma("qg", f"q_b1fb{buf}", wuu[buf][:].rearrange("p (a m) -> p a m", m=512),
                       w["wuT"][:, bi * 512:(bi + 1) * 512].rearrange(
                           "(a p) m -> p a m", p=128), after=aft)
            t3 = k.dma("qg", f"q_b1fc{buf}", wdd[buf][:].rearrange("p (a m) -> p a m", m=1024),
                       w["wdT"][bi * 512:(bi + 1) * 512, :].rearrange(
                           "(a p) m -> p a m", p=128), after=aft)
            wdma[idx] = (ta, tb, t3)

        load_blk(0)
        load_blk(1)

        T2 = 4 * S
        yac3 = lambda p, tl: yacc[:].rearrange("p (f t) -> p f t", t=T2)[
            :, 2 * p:2 * p + 2, tl * S:tl * S + 256]
        psF3 = lambda p: psF[p][:].rearrange("p (m s) -> p m s", s=256)

        evF_prev = [None] * 4
        yacc_readers = []
        yacc_readers_prev = []
        ph_pe = None
        silu_prev = None
        prod_prev = None
        dn_last = None
        pend = None
        ln_prev = []
        cast_all = [k.op("ac", lambda h, f=f: h.activation(
            x1all[:, f * T:(f + 1) * T], xfm[f][:, 0:T],
            AF.Copy, scale=SCALE)) for f in range(NF)]
        for hi in range(2):
            cast_toks = cast_all

            def emit_ph3(tl, ev, hi=hi):
                nonlocal ln_prev
                ti = hi * 4 + tl
                cs = slice(ti * S, (ti + 1) * S)
                y2_toks = []
                for mo in range(NF):
                    tok = k.op("dv", lambda h, mo=mo, tl=tl, cs=cs: h.tensor_tensor(
                        xfm[mo][:, cs], xfm[mo][:, cs],
                        yacc[:, mo * T2 + tl * S:mo * T2 + tl * S + 256], ALU.add),
                        after=[ev[mo // 2]] + ln_prev)
                    y2_toks.append(tok)
                yacc_readers.extend(y2_toks)
                ln_prev = emit_ln_chunk(k, c, xfm, cs, sq2, ybf, psS, psBC, row_scr,
                                        ab_bc, tmp, guard, in_toks=y2_toks)

            def emit_dn(pend, hi=hi):
                nonlocal dn_last, evF_prev
                (p_prod, p_td, p_buf, p_bi, p_tl, p_sub, p_idx, p_first, p_last) = pend
                dn = []
                for half in range(2):
                    ht4 = (p_sub * 2 + half)
                    for mo in range(NF):
                        p, mh = mo // 2, mo % 2
                        aft = [p_prod, p_td]
                        if p_first and half == 0 and mh == 0:
                            aft.append(evF_prev[p])
                        dn.append(k.op("pe", lambda h, ht4=ht4, mo=mo, p=p,
                                       mh=mh, buf=p_buf, half=half,
                                       fp=p_first, lp=p_last, ps=p_sub: h.matmul(
                                           psF[p][:, mh * 256:mh * 256 + 256],
                                           wdd[buf][:, ht4 * 1024 + mo * 128:ht4 * 1024 + (mo + 1) * 128],
                                           prod_bf[:, (ps % 2) * 512 + half * 256:(ps % 2) * 512 + half * 256 + 256],
                                           start=(fp and half == 0 and mh == 0),
                                           stop=(lp and half == 1 and mh == 1),
                                           skip_group_check=True), after=aft))
                dn_last = dn[-1]
                if p_sub == 1:
                    ev = []
                    for p in range(4):
                        if p_bi == 0:
                            ev.append(k.op("dv", lambda h, p=p, tl=p_tl:
                                           h.tensor_copy(yac3(p, tl), psF3(p)),
                                           after=[dn_last] + yacc_readers_prev))
                        else:
                            ev.append(k.op("dv", lambda h, p=p, tl=p_tl:
                                           h.tensor_tensor(yac3(p, tl), yac3(p, tl),
                                                           psF3(p), ALU.add),
                                           after=[dn_last]))
                    evF_prev = ev
                    if p_tl == 3:
                        wuser[p_idx] = dn_last
                        if p_idx + 2 < 2 * NB:
                            load_blk(p_idx + 2)
                    if p_bi == NB - 1:
                        emit_ph3(p_tl, ev)

            for bi in range(NB):
                idx = hi * NB + bi
                buf = idx % 2
                ta_, tb_, td = wdma[idx]
                for tl in range(4):
                    for sub in range(2):
                        gm = []
                        for half in range(2):
                            ht4 = (sub * 2 + half)
                            pcol = half * 256
                            for kk in range(8):
                                i = half * 8 + kk
                                aft = []
                                if i == 0:
                                    aft = [ta_, silu_prev] + (cast_toks if bi == 0 else [])
                                gm.append(k.op("pe", lambda h, ht4=ht4, kk=kk, buf=buf,
                                               pcol=pcol, i=i, tl=tl, hi=hi: h.matmul(
                                                   psG[:, pcol:pcol + 256],
                                                   wgg[buf][:, kk * 512 + ht4 * 128:kk * 512 + (ht4 + 1) * 128],
                                                   x1all[:, kk * T + hi * T2 + tl * S:kk * T + hi * T2 + tl * S + 256],
                                                   start=(i == 0), stop=(i == 15),
                                                   skip_group_check=True), after=aft))
                        um = []
                        for half in range(2):
                            ht4 = (sub * 2 + half)
                            pcol = half * 256
                            for kk in range(8):
                                i = half * 8 + kk
                                aft = []
                                if i == 0:
                                    aft = [tb_, prod_prev]
                                um.append(k.op("pe", lambda h, ht4=ht4, kk=kk, buf=buf,
                                               pcol=pcol, i=i, tl=tl, hi=hi: h.matmul(
                                                   psU[:, pcol:pcol + 256],
                                                   wuu[buf][:, kk * 512 + ht4 * 128:kk * 512 + (ht4 + 1) * 128],
                                                   x1all[:, kk * T + hi * T2 + tl * S:kk * T + hi * T2 + tl * S + 256],
                                                   start=(i == 0), stop=(i == 15),
                                                   skip_group_check=True), after=aft))
                        silu_now = k.op("ac", lambda h: h.activation(
                            silu_bf[:], psG[:], AF.Silu), after=[gm[-1], prod_prev])
                        prod_now = k.op("dv", lambda h, sub=sub: h.scalar_tensor_tensor(
                            prod_bf[:, (sub % 2) * 512:(sub % 2) * 512 + 512],
                            psU[:], SCALE, silu_bf[:], ALU.mult, ALU.mult),
                            after=[silu_now, um[-1], dn_last])
                        if pend is not None:
                            emit_dn(pend)
                        pend = (prod_now, td, buf, bi, tl, sub, idx,
                                sub == 0, sub == 1)
                        silu_prev = silu_now
                        prod_prev = prod_now
            emit_dn(pend)
            pend = None
            ph_pe = ("pe", k.cnt["pe"])
            yacc_readers_prev = yacc_readers
            yacc_readers = []
        k.join()


B1_SHAPES = {
    "wgT": ([D, 3072], I8), "wuT": ([D, 3072], I8), "wdT": ([3072, D], I8),
}


def build_b1_program():
    k = K()
    nc = k.nc
    x_in = nc.dram_tensor("x_in", [T, D], F32, kind="ExternalInput").ap()
    w = {n: nc.dram_tensor("w_" + n, B1_SHAPES[n][0], B1_SHAPES[n][1],
                           kind="ExternalInput").ap() for n in B1_SHAPES}
    x_out = nc.dram_tensor("x_out", [T, D], F32, kind="ExternalOutput").ap()
    with k.es:
        k.open()
        c = emit_consts(k)
        xfm = [k.sb(f"xfm{f}", [P, T], F32) for f in range(NF)]
        emit_load_stream_fm(k, c, x_in, xfm)
        emit_b1(k, c, xfm, w)
        out_toks = emit_store_stream_fm(k, c, xfm, x_out)
        k.finish(out_toks)
        k.replay()
    return k.nc


def _b1_inputs(inputs, ci, x_cur):
    gi = lambda n: np.asarray(inputs["L0_ffn_" + n], dtype=np.int8)
    return {
        **({"x_in": np.ascontiguousarray(np.asarray(x_cur, dtype=np.float32))}
           if x_cur is not None else {}),
        "w_wgT": np.ascontiguousarray(gi("w_gate").T),
        "w_wuT": np.ascontiguousarray(gi("w_up").T),
        "w_wdT": np.ascontiguousarray(gi("w_down").T),
    }





def emit_b2(k, c, xfm, w):
    import os
    NH = int(os.environ.get("B2_NH", "16"))
    NQC = int(os.environ.get("B2_NQC", "4"))
    with k.scope():
        qfm = [k.sb(f"b2_q{f}", [P, T], F16) for f in range(NF)]
        kfm = [k.sb(f"b2_k{f}", [P, T], F16) for f in range(NF)]
        att = qfm
        vt = [k.sb(f"b2_v{t}", [P, 16 * 65], BF16) for t in range(16)]
        alicol = k.sb("b2_alicol", [P, 16 * 16], F32)
        iota2d = k.sb("b2_iota2d", [P, 16], F32)
        iota_row = k.sb("b2_iotarow", [1, T], F16)
        ident_bf = k.sb("b2_identbf", [P, 128], F16)
        ones64 = k.sb("b2_ones64", [P, 64], F32)
        ones_h = k.sb("b2_onesh", [1, 128], F16)
        qkb = k.sb("b2_qkb", [1, 128], F32)
        psS = k.ps("b2_psS", [P, 512])
        psBC = k.ps("b2_psBC", [P, 512])
        guard = {}

        t_i2 = k.op("pl", lambda h: h.iota(iota2d[:, 0:16], pattern=[[128, 16]],
                                           base=0, channel_multiplier=1,
                                           allow_small_or_imprecise_dtypes=True))
        t_ir = k.op("pl", lambda h: h.iota(iota_row[0:1, 0:T], pattern=[[1, T]],
                                           base=0, channel_multiplier=0,
                                           allow_small_or_imprecise_dtypes=True))
        t_on = k.op("pl", lambda h: h.memset(ones64[:], 1.0))
        t_oh = k.op("pl", lambda h: h.memset(ones_h[:], 1.0))
        tz = k.op("pl", lambda h: h.memset(ident_bf[:], 0.0))
        t_id = k.op("pl", lambda h: h.affine_select(
            out=ident_bf[:], in_=ident_bf[:], compare_op=ALU.not_equal,
            fill=1.0, base=0, pattern=[[-1, 128]], channel_multiplier=1), after=[tz])
        ali_toks = []
        for hh in range(16):
            slope = 2.0 ** (-0.5 * (hh + 1))
            ali_toks.append(k.op("dv", lambda h, hh=hh, slope=slope: h.tensor_scalar_mul(
                alicol[:, hh * 16:(hh + 1) * 16], iota2d[:, 0:16], slope),
                after=[t_i2]))
        consts = [t_i2, t_ir, t_on, t_oh, t_id] + ali_toks + c["tok"]

        with k.scope():
            x3bf = k.sb("b2_x3bf", [P, NF * 1024], F16)
            wbuf = k.sb("b2_wbuf", [P, 8192], F16)
            psq = [k.ps(f"b2_psq{i}", [P, 512]) for i in range(4)]
            bank_guard = [None] * 4
            w_names = ["q_w", "k_w", "v_w"]

            def load_w(j, aft):
                return k.dma("qg", "q_b2w", wbuf[:].rearrange("p (a m) -> p a m", m=1024),
                             w[w_names[j]].rearrange("(a p) m -> p a m", p=128),
                             after=aft)

            t_w = load_w(0, [])
            i = 0
            last_mm = None
            pe_hw = None
            ones_done = {}
            for j in range(3):
                for ht in range(2):
                    cast_toks = [k.op("ac", lambda h, f=f, ht=ht: h.activation(
                        x3bf[:, f * 1024:(f + 1) * 1024],
                        xfm[f][:, ht * 1024:(ht + 1) * 1024], AF.Copy),
                        after=[pe_hw]) for f in range(NF)]
                    if j < 2:
                        dst = qfm if j == 0 else kfm
                        scale = 0.125 if j == 0 else 1.0
                        for mo in range(NF):
                            for nkk in range(2):
                                nk = ht * 2 + nkk
                                bank = i % 4
                                mms = []
                                for kk in range(NF):
                                    aft = ([bank_guard[bank], t_w] + cast_toks) if kk == 0 else []
                                    mms.append(k.op("pe", lambda h, mo=mo, nkk=nkk, kk=kk,
                                                    bank=bank: h.matmul(
                                                        psq[bank][:],
                                                        wbuf[:, kk * 1024 + mo * 128:kk * 1024 + (mo + 1) * 128],
                                                        x3bf[:, kk * 1024 + nkk * 512:kk * 1024 + (nkk + 1) * 512],
                                                        start=(kk == 0), stop=(kk == NF - 1),
                                                        skip_group_check=True), after=aft))
                                bank_guard[bank] = k.op(
                                    "ac", lambda h, mo=mo, nk=nk, bank=bank, dst=dst,
                                    scale=scale: h.activation(
                                        dst[mo][:, nk * 512:(nk + 1) * 512],
                                        psq[bank][:], AF.Copy, scale=scale),
                                    after=[mms[-1]])
                                last_mm = mms[-1]
                                i += 1
                    else:
                        for tt8 in range(8):
                            tt = ht * 8 + tt8
                            for oc in range(2):
                                bank = i % 4
                                mms = []
                                for kk in range(NF):
                                    aft = ([bank_guard[bank], t_w] + cast_toks) if kk == 0 else []
                                    mms.append(k.op("pe", lambda h, tt8=tt8, oc=oc, kk=kk,
                                                    bank=bank: h.matmul(
                                                        psq[bank][:],
                                                        x3bf[:, kk * 1024 + tt8 * 128:kk * 1024 + (tt8 + 1) * 128],
                                                        wbuf[:, kk * 1024 + oc * 512:kk * 1024 + (oc + 1) * 512],
                                                        start=(kk == 0), stop=(kk == NF - 1),
                                                        skip_group_check=True), after=aft))
                                tok = k.op("dv", lambda h, tt=tt, oc=oc, bank=bank: h.tensor_copy(
                                    vt[tt][:].rearrange("p (g cc) -> p g cc", cc=65)[:, oc * 8:(oc + 1) * 8, 0:64],
                                    psq[bank][:].rearrange("p (g cc) -> p g cc", cc=64)),
                                    after=[mms[-1]])
                                bank_guard[bank] = tok
                                last_mm = mms[-1]
                                i += 1
                            ones_done[tt] = k.op("pl", lambda h, tt=tt: h.memset(
                                vt[tt][:].rearrange("p (g cc) -> p g cc", cc=65)[:, 0:16, 64:65], 1.0))
                    pe_hw = ("pe", k.cnt["pe"])
                if j + 1 < 3:
                    t_w = load_w(j + 1, [last_mm])
            k.join()

        with k.scope():
            sqn = k.sb("b2_sqn", [P, T], BF16)
            psN = k.ps("b2_psN", [P, 2048])
            cell = k.sb("b2_cell", [1, 8], F32)
            gN = None
            for f in range(NF):
                sq_k = k.op("ac", lambda h, f=f: h.activation(sqn[:], kfm[f][:], AF.Square))
                for half in range(2):
                    hh = 2 * f + half
                    base = 64 * half
                    mms = []
                    for nk in range(4):
                        mms.append(k.op("pe", lambda h, base=base, nk=nk: h.matmul(
                            psN[0:1, nk * 512:(nk + 1) * 512],
                            c["ones_col_bf"][base:base + 64, 0:1],
                            sqn[base:base + 64, nk * 512:(nk + 1) * 512],
                            start=True, stop=True, skip_group_check=True),
                            after=[sq_k, gN] + consts))
                    t = k.op("dv", lambda h: h.reduce_max(cell[0:1, 0:1], psN[0:1, 0:2048],
                                                          axis=AX.X), after=[mms[-1]])
                    gN = k.op("ac", lambda h, hh=hh: h.activation(
                        qkb[0:1, 64 + hh:65 + hh], cell[0:1, 0:1], AF.Sqrt), after=[t])
                sq_q = k.op("ac", lambda h, f=f: h.activation(sqn[:], qfm[f][:], AF.Square),
                            after=[gN])
                for half in range(2):
                    hh = 2 * f + half
                    base = 64 * half
                    mms = []
                    for nk in range(4):
                        mms.append(k.op("pe", lambda h, base=base, nk=nk: h.matmul(
                            psN[0:1, nk * 512:(nk + 1) * 512],
                            c["ones_col_bf"][base:base + 64, 0:1],
                            sqn[base:base + 64, nk * 512:(nk + 1) * 512],
                            start=True, stop=True, skip_group_check=True),
                            after=[sq_q, gN]))
                    for qc in range(4):
                        t = k.op("dv", lambda h, qc=qc: h.reduce_max(
                            cell[0:1, 1 + qc:2 + qc], psN[0:1, qc * 512:(qc + 1) * 512],
                            axis=AX.X), after=[mms[-1]])
                        t = k.op("ac", lambda h, qc=qc: h.activation(
                            cell[0:1, 1 + qc:2 + qc], cell[0:1, 1 + qc:2 + qc], AF.Sqrt),
                            after=[t])
                        gN = k.op("dv", lambda h, hh=hh, qc=qc: h.tensor_scalar(
                            qkb[0:1, hh * 4 + qc:hh * 4 + qc + 1],
                            cell[0:1, 1 + qc:2 + qc], qkb[0:1, 64 + hh:65 + hh],
                            6.907755, ALU.mult, ALU.subtract), after=[t])
            k.join()

        with k.scope():
            pt = k.sb("b2_pt", [P, 2 * 512], BF16)
            masks = k.sb("b2_masks", [P, 4 * 512], F16)
            negm = k.sb("b2_negm", [1, 2 * 512], F16)
            lrow = k.sb("b2_lrow", [P, 512], F32)
            bcast_sb = k.sb("b2_bcast", [P, 512], F32)
            mask_toks = []
            for i in range(4):
                tz2 = k.op("pl", lambda h, i=i: h.memset(masks[:, i * 512:(i + 1) * 512], 0.0))
                mask_toks.append(k.op("pl", lambda h, i=i: h.affine_select(
                    out=masks[:, i * 512:(i + 1) * 512],
                    in_=masks[:, i * 512:(i + 1) * 512],
                    compare_op=ALU.is_ge, fill=-30000.0, base=-i * 128,
                    pattern=[[1, 512]], channel_multiplier=-1), after=[tz2]))
            psST = [k.ps(f"b2_psST{i}", [P, 512]) for i in range(2)]
            psAV = [k.ps(f"b2_psAV{i}", [P, 512]) for i in range(2)]
            psL = k.ps("b2_psL", [P, 512])
            exp_prev = {}
            av_read_prev = {}
            negm_user = {}
            psl_user = None
            pend_norm = None

            def emit_norm(pn):
                nonlocal psl_user
                (p_avb, p_f, p_base, p_qc, p_av) = pn
                t = k.op("dv", lambda h, avb=p_avb: h.tensor_copy(
                    lrow[64:65, 0:512], psAV[avb][64:65, :]), after=[p_av])
                t = k.op("dv", lambda h: h.reciprocal(lrow[64:65, 0:512],
                                                      lrow[64:65, 0:512]), after=[t])
                t = k.op("pe", lambda h: h.matmul(
                    psL[0:64, :], ones64[64:65, 0:64], lrow[64:65, 0:512],
                    start=True, stop=True, skip_group_check=True),
                    after=[t, psl_user])
                t = k.op("dv", lambda h: h.tensor_copy(bcast_sb[0:64, :],
                                                       psL[0:64, :]), after=[t])
                psl_user = t
                t = k.op("dv", lambda h, f=p_f, base=p_base, qc=p_qc, avb=p_avb:
                         h.tensor_tensor(
                             att[f][base:base + 64, qc * 512:(qc + 1) * 512],
                             psAV[avb][0:64, :], bcast_sb[0:64, :], ALU.mult),
                         after=[t, p_av])
                av_read_prev[p_avb] = t

            it = 0
            for hh in range(NH):
                f, base = hh // 2, 64 * (hh % 2)
                slope = 2.0 ** (-0.5 * (hh + 1))
                for qc in range(NQC):
                    nkb = qc * 4 + 4
                    slot = it % 2
                    avb = it % 2
                    t_negm = k.op("dv", lambda h, hh=hh, qc=qc, slope=slope, slot=slot:
                                  h.tensor_scalar(
                                      negm[0:1, slot * 512:(slot + 1) * 512],
                                      iota_row[0:1, qc * 512:(qc + 1) * 512],
                                      -slope, qkb[0:1, hh * 4 + qc:hh * 4 + qc + 1],
                                      ALU.mult, ALU.subtract),
                                  after=[negm_user.get(slot)] + consts)
                    av_last = None
                    pend_av = None
                    for kb in range(nkb):
                        b = kb % 2
                        mm1 = k.op("pe", lambda h, f=f, base=base, kb=kb, qc=qc, b=b:
                                   h.matmul(psST[b][:],
                                            kfm[f][base:base + 64, kb * 128:(kb + 1) * 128],
                                            qfm[f][base:base + 64, qc * 512:(qc + 1) * 512],
                                            start=True, stop=False, skip_group_check=True),
                                   after=[exp_prev.get(b), t_negm])
                        has_mask = kb >= qc * 4
                        mm2 = k.op("pe", lambda h, slot=slot, b=b, hm=has_mask: h.matmul(
                            psST[b][:], ones_h[0:1, 0:128],
                            negm[0:1, slot * 512:(slot + 1) * 512],
                            start=False, stop=(not hm), skip_group_check=True))
                        last_mm = mm2
                        if has_mask:
                            off = kb - qc * 4
                            last_mm = k.op("pe", lambda h, off=off, b=b: h.matmul(
                                psST[b][:], ident_bf[:],
                                masks[:, off * 512:(off + 1) * 512],
                                start=False, stop=True, skip_group_check=True),
                                after=mask_toks)
                        if pend_av is not None:
                            av_last = k.op("pe", pend_av[0], after=pend_av[1])
                        if kb == 2 and pend_norm is not None:
                            emit_norm(pend_norm)
                            pend_norm = None
                        t_exp = k.op("ac", lambda h, b=b, hh=hh, kb=kb: h.activation(
                            pt[:, b * 512:(b + 1) * 512], psST[b][:], AF.Exp,
                            bias=alicol[:, hh * 16 + kb:hh * 16 + kb + 1]),
                            after=[last_mm])
                        exp_prev[b] = t_exp
                        pend_av = (
                            (lambda h, kb=kb, hh=hh, b=b, avb=avb, nkb=nkb: h.matmul(
                                psAV[avb][0:65, :],
                                vt[kb][:, hh * 65:(hh + 1) * 65],
                                pt[:, b * 512:(b + 1) * 512],
                                start=(kb == 0), stop=(kb == nkb - 1),
                                skip_group_check=True)),
                            [t_exp] + ([av_read_prev.get(avb)] if kb == 0 else [])
                            + [ones_done[kb]])
                    av_last = k.op("pe", pend_av[0], after=pend_av[1])
                    pend_norm = (avb, f, base, qc, av_last)
                    negm_user[slot] = mm2
                    it += 1
            emit_norm(pend_norm)
            pend_norm = None
            k.join()

        with k.scope():
            row_scr = k.sb("b2_rowscr", [1, 1024], F32)
            ab_bc = k.sb("b2_abbc", [P, 512], F32)
            tmp = k.sb("b2_tmp", [P, 256], F32)
            sqb = k.sb("b2_sqb", [P, NF * S], BF16)
            ybf = k.sb("b2_ybf", [P, NF * S], BF16)
            pso = [k.ps(f"b2_pso{i}", [P, 512]) for i in range(4)]
            ev_prev = [None] * 4
            wo2 = k.sb("b2_wo2", [P, 8192], F16)
            t_wo = k.dma("qg", "q_b2w2", wo2[:].rearrange("p (a m) -> p a m", m=1024),
                         w["o_w"].rearrange("(a p) m -> p a m", p=128))
            ln_prev = []
            pend_ln = None
            for ti in range(NCH):
                cs = slice(ti * S, (ti + 1) * S)
                ev = []
                for mo in range(NF):
                    pp, mh = mo // 2, mo % 2
                    mms = []
                    for kk in range(NF):
                        aft = [t_wo, ev_prev[pp]] if kk == 0 else []
                        mms.append(k.op("pe", lambda h, mo=mo, ti=ti, kk=kk, pp=pp, mh=mh:
                                        h.matmul(
                                            pso[pp][:, mh * 256:mh * 256 + 256],
                                            wo2[:, kk * 1024 + mo * 128:kk * 1024 + (mo + 1) * 128],
                                            att[kk][:, ti * S:(ti + 1) * S],
                                            start=(kk == 0), stop=(kk == NF - 1),
                                            skip_group_check=True), after=aft))
                    tok = k.op("dv", lambda h, mo=mo, ti=ti, pp=pp, mh=mh: h.tensor_tensor(
                        xfm[mo][:, ti * S:(ti + 1) * S],
                        xfm[mo][:, ti * S:(ti + 1) * S],
                        pso[pp][:, mh * 256:mh * 256 + 256], ALU.add),
                        after=[mms[-1]])
                    ev.append(tok)
                    ev_prev[pp] = tok
                if pend_ln is not None:
                    ln_prev = emit_ln_chunk(k, c, xfm, pend_ln[0], sqb, ybf, psS,
                                            psBC, row_scr, ab_bc, tmp, guard,
                                            in_toks=pend_ln[1] + ln_prev)
                pend_ln = (cs, ev)
            ln_prev = emit_ln_chunk(k, c, xfm, pend_ln[0], sqb, ybf, psS, psBC,
                                    row_scr, ab_bc, tmp, guard,
                                    in_toks=pend_ln[1] + ln_prev)
            k.join()


B2_SHAPES = {"q_w": [D, D], "k_w": [D, D], "v_w": [D, D], "o_w": [D, D]}


def build_b2_program():
    k = K()
    nc = k.nc
    x_in = nc.dram_tensor("x_in", [T, D], F32, kind="ExternalInput").ap()
    w = {n: nc.dram_tensor("w_" + n, B2_SHAPES[n], F32, kind="ExternalInput").ap()
         for n in B2_SHAPES}
    x_out = nc.dram_tensor("x_out", [T, D], F32, kind="ExternalOutput").ap()
    with k.es:
        k.open()
        c = emit_consts(k)
        xfm = [k.sb(f"xfm{f}", [P, T], F32) for f in range(NF)]
        emit_load_stream_fm(k, c, x_in, xfm)
        emit_b2(k, c, xfm, w)
        out_toks = emit_store_stream_fm(k, c, xfm, x_out)
        k.finish(out_toks)
        k.replay()
    return k.nc


def _b2_inputs(inputs, ci, x_cur):
    g = lambda n: np.ascontiguousarray(np.asarray(inputs["L1_attn_" + n], dtype=np.float32))
    return {
        **({"x_in": np.ascontiguousarray(np.asarray(x_cur, dtype=np.float32))}
           if x_cur is not None else {}),
        "w_q_w": g("q_w"), "w_k_w": g("k_w"), "w_v_w": g("v_w"), "w_o_w": g("o_w"),
    }




def emit_b3(k, c, x_in_ap, xfmh, w, x_out_ap):
    import os
    NE = int(os.environ.get("B3_NE", "16"))
    B3_SH = int(os.environ.get("B3_SH", "1"))
    B3_ROUTE = int(os.environ.get("B3_ROUTE", "2"))
    nc = k.nc
    with k.scope():
        wte = k.sb("b3_wte", [P, 16 * 16], F32)
        wteT = k.sb("b3_wteT", [P, 16 * 16], F32)
        tauones = k.sb("b3_tauones", [2, T], F16)
        ones_h = k.sb("b3_onesh", [P, 1], BF16)
        sbf = k.sb("b3_sbf", [P, 4 * T], F16)
        sqs = k.sb("b3_sqs", [P, 512], BF16)
        hh_sb = k.sb("b3_hhsb", [P, 2048], F16)
        zt = k.sb("b3_zt", [P, 2048], BF16)
        zsq = k.sb("b3_zsq", [P, 2048], BF16)
        cols = k.sb("b3_cols", [P, 16], F32)
        colx = k.sb("b3_colx", [P, 8], F32)
        colSH = k.sb("b3_cSH", [P, 16], F32)
        colTau = k.sb("b3_cTau", [P, 16], F32)
        colBB = k.sb("b3_cBB", [P, 16], F32)
        colU0 = k.sb("b3_cU0", [P, 16], F32)
        colQ = k.sb("b3_cQ", [P, 16], F32)
        colS = k.sb("b3_cS", [P, 16], F32)
        colIS2 = k.sb("b3_cIS2", [P, 16], F32)
        colWp = k.sb("b3_cWp", [P, 16], F32)
        colWp2 = k.sb("b3_cWp2", [P, 16], F32)
        colR2 = k.sb("b3_cR2", [P, 16], F32)
        colR = k.sb("b3_cR", [P, 16], F32)
        tmpA = k.sb("b3_tmpA", [P, 16], F32)
        tmpB = k.sb("b3_tmpB", [P, 16], F32)
        tmpC = k.sb("b3_tmpC", [P, 16], F32)
        vrow = k.sb("b3_vrow", [1, 1024], BF16)
        taub2 = [k.sb(f"b3_taub2{i}", [2, 1024], F16) for i in range(2)]
        guard = {}
        t_vp = k.op("pl", lambda h: h.memset(vrow[0:1, 511:512], 0.0))
        t_vp2 = k.op("pl", lambda h: h.memset(vrow[0:1, 1023:1024], 0.0), after=[t_vp])
        t_to = k.op("pl", lambda h: h.memset(tauones[0:2, :], 1.0))
        t_oh = k.op("pl", lambda h: h.memset(ones_h[:], 1.0))
        consts = [t_to, t_oh, t_vp2] + c["tok"]

        with k.scope():
            xtm = [k.sb(f"b3_x{t}", [P, 1024], F32) for t in range(16)]
            ld_toks = [k.dma("qs", "q_xin", xtm[tt][:],
                             x_in_ap[tt * 128:(tt + 1) * 128, :]) for tt in range(16)]
            ld_all = ld_toks[-1]
            xlo = k.sb("b3_xlo", [P, NF * T], F16)
            with k.scope():
                psT2 = [k.ps(f"b3_ldT{i}", [P, 512], F32) for i in range(2)]
                ev = {}
                xlo_toks = []
                for tt in range(16 if B3_ROUTE else 0):
                    for f in range(NF):
                        bank, slot = f // 4, f % 4
                        ev[(tt, f, "t")] = k.op(
                            "pe", lambda h, tt=tt, f=f, bank=bank, slot=slot:
                            h.transpose(psT2[bank][:, slot * 128:(slot + 1) * 128],
                                        xtm[tt][:, f * 128:(f + 1) * 128],
                                        c["ident_f"][:]),
                            after=[ld_all] + [ev.get((tt - 1, bank * 4 + s))
                                              for s in range(4)] + c["tok"])
                    for f in range(NF):
                        bank, slot = f // 4, f % 4
                        hi = k.op("dv", lambda h, tt=tt, f=f, bank=bank, slot=slot:
                                  h.tensor_copy(
                                      xfmh[:, f * T + tt * 128:f * T + (tt + 1) * 128],
                                      psT2[bank][:, slot * 128:(slot + 1) * 128]),
                                  after=[ev[(tt, bank * 4 + 3, "t")]])
                        lo = k.op("dv", lambda h, tt=tt, f=f, bank=bank, slot=slot:
                                  h.tensor_tensor(
                                      xlo[:, f * T + tt * 128:f * T + (tt + 1) * 128],
                                      psT2[bank][:, slot * 128:(slot + 1) * 128],
                                      xfmh[:, f * T + tt * 128:f * T + (tt + 1) * 128],
                                      ALU.subtract), after=[hi])
                        ev[(tt, f)] = lo
                        xlo_toks += [hi, lo]
            gwh = k.sb("b3_gw", [P, 3 * 128], F16)
            gw32 = k.sb("b3_gw32", [P, 128], F32)
            sc16 = k.sb("b3_sc16", [P, 64], F32)
            psR = k.ps("b3_psR", [P, 512])
            t_g = k.dma("qs", "q_b3g", gw32[:].rearrange("p (a m) -> p a m", m=16),
                        w["gwT"].rearrange("(a p) m -> p a m", p=128))
            t_gh = k.op("ac", lambda h: h.activation(gwh[:, 0:128], gw32[:], AF.Copy),
                        after=[t_g])
            t_gl = k.op("dv", lambda h: h.tensor_tensor(gwh[:, 128:256], gw32[:],
                                                        gwh[:, 0:128], ALU.subtract),
                        after=[t_gh])
            lgb = k.sb("b3_lgb", [P, 256], F32)
            lgt = k.sb("b3_lgt", [P, 256], F32)
            colM = k.sb("b3_colM", [P, 16], F32)
            mms = []
            first = True
            for tt in range(16 if B3_ROUTE >= 2 else 0):
                i = 0
                for pas in range(2):
                    src_ = xfmh if pas == 0 else xlo
                    gcol = 0
                    for kk in range(NF):
                        aft = ([t_gh, t_gl] + xlo_toks) if i == 0 else []
                        last = (tt == 15 and i == 2 * NF - 1)
                        mms.append(k.op("pe", lambda h, tt=tt, kk=kk, src=src_,
                                        gcol=gcol, first=first, last=last:
                                        h.matmul(psR[:, tt * 16:tt * 16 + 16],
                                                 src[:, kk * T + tt * 128:kk * T + (tt + 1) * 128],
                                                 gwh[:, gcol + kk * 16:gcol + (kk + 1) * 16],
                                                 start=first, stop=last,
                                                 skip_group_check=True), after=aft))
                        first = False
                        i += 1
            if B3_ROUTE >= 2:
                seg = lambda t_: t_[:].rearrange("p (t e) -> p t e", e=16)
                col3 = lambda t_: t_[:].rearrange("p (t o) -> p t o", o=1)
                colb = lambda t_: col3(t_).broadcast_to([P, 16, 16])
                t = k.op("dv", lambda h: h.tensor_copy(lgb[:], psR[:, 0:256]),
                         after=[mms[-1]])
                t = k.op("dv", lambda h: h.reduce_max(
                    col3(colM), seg(lgb), axis=AX.X, negate=True), after=[t])
                t = k.op("dv", lambda h: h.tensor_tensor(
                    seg(lgb), seg(lgb), colb(colM), ALU.add), after=[t])
                t = k.op("ac", lambda h: h.activation(lgb[:], lgb[:], AF.Exp),
                         after=[t])
                t = k.op("dv", lambda h: h.reduce_sum(
                    col3(colM), seg(lgb), axis=AX.X), after=[t])
                t = k.op("dv", lambda h: h.reciprocal(colM[:], colM[:]), after=[t])
                t = k.op("dv", lambda h: h.tensor_tensor(
                    seg(lgb), seg(lgb), colb(colM), ALU.mult), after=[t])
                t = k.op("dv", lambda h: h.reduce_max(
                    col3(colM), seg(lgb), axis=AX.X), after=[t])
                t = k.op("dv", lambda h: h.tensor_tensor(
                    seg(lgt), seg(lgb), colb(colM), ALU.is_equal), after=[t])
                t = k.op("dv", lambda h: h.scalar_tensor_tensor(
                    lgt[:], lgt[:], -1e30, lgb[:], ALU.mult, ALU.add), after=[t])
                t = k.op("dv", lambda h: h.reduce_max(
                    col3(colM), seg(lgt), axis=AX.X), after=[t])
                t = k.op("dv", lambda h: h.tensor_tensor(
                    seg(lgt), seg(lgb), colb(colM), ALU.is_ge), after=[t])
                k.op("dv", lambda h: h.tensor_tensor(
                    wte[:], lgt[:], lgb[:], ALU.mult), after=[t])
            k.join()

        wte3 = wte[:].rearrange("p (t e) -> p e t", e=16)
        t_wteT = None
        for e_ in range(16):
            t_wteT = k.dma("qs", "q_wtT",
                           wteT[:, e_ * 16:(e_ + 1) * 16].rearrange(
                               "p (a t) -> p a t", a=1),
                           wte3[:, e_:e_ + 1, :],
                           allow_slow_non_contiguous=True)
        ytm = [k.sb(f"b3_y{t}", [P, 1024], F32) for t in range(16)]
        yinit = []
        for tt in range(16):
            yinit.append(k.op("pl", lambda h, tt=tt: h.memset(ytm[tt][:], 0.0)))
        for t_, v_ in ((colU0, 1.0), (colQ, 1.0), (colS, 1.0), (colIS2, 1.0)):
            yinit.append(k.op("pl", lambda h, t_=t_, v_=v_: h.memset(t_[:], v_)))
        t_wp0 = k.op("dv", lambda h: h.tensor_tensor(
            colWp[:], wteT[:, 0:16], colS[:], ALU.mult), after=[t_wteT] + yinit)
        t_wp0b = k.op("dv", lambda h: h.tensor_scalar_mul(colWp2[:], colWp[:], 2.0),
                      after=[t_wp0])
        k.wait("dv", *yinit)
        k.wait("ac", *yinit)

        psU1 = [k.ps(f"b3_psU1{i}", [P, 512]) for i in range(2)]
        psU3 = [k.ps(f"b3_psU3{i}", [P, 512]) for i in range(2)]
        psT_ = k.ps("b3_psTau", [P, 512])
        psW2 = k.ps("b3_psW2", [P, 1024])
        silu_sb = k.sb("b3_silu", [P, 1024], BF16)

        w_user = {}
        wA_user = {}
        bias_user = None

        lr_guards = {"colsh": None, "tau": None, "r": None}

        def lres_phaseB(mm_emit, wp_next_src):
            prev_chh = lr_guards.get("chh", [None, None])
            hs_hist = lr_guards.get("hs2", [None, None])
            z_hist = lr_guards.get("z2", [None, None])
            prev_hs = None
            for tt in range(16):
                hp = (tt % 2) * 1024
                half_toks = mm_emit(tt, prev_chh)
                c_a = k.op("ac", lambda h, hp=hp: h.activation(
                    hh_sb[:, hp:hp + 512], psW2[:, 0:512], AF.Copy),
                    after=[half_toks[0], hs_hist[-2], z_hist[-2]])
                c_b = k.op("ac", lambda h, hp=hp: h.activation(
                    hh_sb[:, hp + 512:hp + 1023], psW2[:, 512:1023], AF.Copy),
                    after=[half_toks[1]])
                t_hs = k.op("ac", lambda h, tt=tt, hp=hp: h.activation(
                    zsq[:, hp:hp + 1023], hh_sb[:, hp:hp + 1023], AF.Square,
                    accum_out=colSH[:, tt:tt + 1]),
                    after=[c_b] + ([lr_guards["colsh"]] if tt == 0 else []))
                k.op("dv", lambda h, tt=tt, hp=hp: h.scalar_tensor_tensor(
                    zt[:, hp:hp + 1023], hh_sb[:, hp:hp + 1023], 1.0, ytm[tt][:, 1:1024],
                    ALU.bypass, ALU.mult, accum_out=colBB[:, tt:tt + 1]),
                    after=[c_b])
                t_z = k.op("dv", lambda h, tt=tt, hp=hp: h.scalar_tensor_tensor(
                    ytm[tt][:, 1:1024], hh_sb[:, hp:hp + 1023], colWp[:, tt:tt + 1],
                    ytm[tt][:, 1:1024], ALU.mult, ALU.add), after=[c_b])
                prev_chh = [c_a, c_b]
                hs_hist.append(t_hs)
                z_hist.append(t_z)
                prev_hs = t_hs
            lr_guards["chh"] = prev_chh
            lr_guards["hs2"] = hs_hist[-2:]
            lr_guards["z2"] = z_hist[-2:]
            a_tau = k.op("ac", lambda h: h.activation(
                colTau[:], colSH[:], AF.Sqrt, bias=c["ccols"][:, 1:2]),
                after=[lr_guards["tau"], prev_hs] + consts)
            t = k.op("dv", lambda h: h.tensor_tensor(
                tmpA[:], colU0[:], colTau[:], ALU.mult), after=[a_tau])
            t = k.op("dv", lambda h: h.tensor_tensor(
                tmpA[:], tmpA[:], colBB[:], ALU.add), after=[t])
            t_u = k.op("dv", lambda h: h.tensor_tensor(
                tmpB[:], colTau[:], colWp[:], ALU.mult), after=[t])
            lr_guards["tau"] = t_u
            t_u = k.op("dv", lambda h: h.tensor_tensor(
                colU0[:], colU0[:], tmpB[:], ALU.add), after=[t_u])
            q = k.op("dv", lambda h: h.tensor_scalar(
                tmpC[:], colSH[:], 2.0, 1.0, ALU.mult, ALU.add), after=[t_u])
            lr_guards["colsh"] = q
            q = k.op("dv", lambda h: h.tensor_tensor(
                tmpC[:], tmpC[:], colWp[:], ALU.mult), after=[q])
            q = k.op("dv", lambda h: h.tensor_tensor(
                tmpC[:], tmpC[:], colWp[:], ALU.mult), after=[q])
            q = k.op("dv", lambda h: h.tensor_tensor(
                tmpA[:], tmpA[:], colWp2[:], ALU.mult), after=[q])
            q = k.op("dv", lambda h: h.tensor_tensor(
                colQ[:], colQ[:], tmpA[:], ALU.add), after=[q])
            q = k.op("dv", lambda h: h.tensor_tensor(
                colQ[:], colQ[:], tmpC[:], ALU.add), after=[q])
            t = k.op("dv", lambda h: h.tensor_tensor(
                tmpA[:], colU0[:], colU0[:], ALU.mult), after=[q])
            t = k.op("dv", lambda h: h.scalar_tensor_tensor(
                colR2[:], tmpA[:], 2.0, colQ[:], ALU.mult, ALU.subtract),
                after=[t])
            t = k.op("dv", lambda h: h.tensor_tensor(
                colR2[:], colR2[:], colIS2[:], ALU.mult), after=[t])
            t = k.op("dv", lambda h: h.tensor_scalar_max(
                colR2[:], colR2[:], 1e-6), after=[t])
            a_r = k.op("ac", lambda h: h.activation(
                colR[:], colR2[:], AF.Sqrt), after=[t, lr_guards["r"]])
            t_s = k.op("dv", lambda h: h.tensor_tensor(
                colS[:], colS[:], colR[:], ALU.mult), after=[a_r])
            lr_guards["r"] = t_s
            t = k.op("dv", lambda h: h.reciprocal(colR2[:], colR2[:]), after=[t_s])
            t = k.op("dv", lambda h: h.tensor_tensor(
                colIS2[:], colIS2[:], colR2[:], ALU.mult), after=[t])
            if wp_next_src == "shared":
                t = k.op("dv", lambda h: h.tensor_copy(colWp[:], colS[:]),
                         after=[t])
                t = k.op("dv", lambda h: h.tensor_scalar_mul(
                    colWp2[:], colWp[:], 2.0), after=[t])
            elif wp_next_src is not None:
                t = k.op("dv", lambda h, s_=wp_next_src: h.tensor_tensor(
                    colWp[:], s_, colS[:], ALU.mult), after=[t])
                t = k.op("dv", lambda h: h.tensor_scalar_mul(
                    colWp2[:], colWp[:], 2.0), after=[t])
            return t

        lr_last = None
        es_exp = ExitStack()
        k._scopes.append(es_exp)
        we1 = [k.sb(f"b3_we1{i}", [P, 8 * 511], F16) for i in range(2)]
        we3 = [k.sb(f"b3_we3{i}", [P, 8 * 511], F16) for i in range(2)]
        we2 = [k.sb(f"b3_we2{i}", [P, 4 * 1023], F16) for i in range(2)]
        for e in range(NE):
            wb = e % 2
            aftw = [w_user.get(e - 2)]
            t_w1 = k.dma("qg", f"q_b3w0_{wb}", we1[wb][:].rearrange("p (a m) -> p a m", m=511),
                         w["We1T"][e].rearrange("(a p) m -> p a m", p=128), after=aftw)
            t_w3 = k.dma("qg", f"q_b3w1_{wb}", we3[wb][:].rearrange("p (a m) -> p a m", m=511),
                         w["We3T"][e].rearrange("(a p) m -> p a m", p=128), after=aftw)
            t_w2a = k.dma("qg", f"q_b3w2_{wb}", we2[wb][0:128, 0:3 * 1023].rearrange("p (a m) -> p a m", m=1023),
                          w["We2T"][e][1:385].rearrange("(a p) m -> p a m", p=128),
                          after=aftw)
            t_w2b = k.dma("qg", f"q_b3w3_{wb}", we2[wb][0:127, 3 * 1023:4 * 1023],
                          w["We2T"][e][385:512], after=aftw)
            t_tb1 = k.dma("qg", f"q_b3w4_{wb}", taub2[wb][0:1, 0:1023], w["We2T"][e][0:1], after=aftw)
            t_tb2 = k.dma("qg", f"q_b3w5_{wb}", taub2[wb][1:2, 0:1023],
                          w["be2"][e].rearrange("(a d) -> a d", a=1), after=aftw)
            t_wall = [t_w2a, t_w2b, t_tb1, t_tb2]
            t_b1r = k.dma("qg", "q_b3b1", vrow[0:1, 0:511],
                          w["be1"][e].rearrange("(a d) -> a d", a=1), after=[bias_user])
            t_b3r = k.dma("qg", "q_b3b3", vrow[0:1, 512:1023],
                          w["be3"][e].rearrange("(a d) -> a d", a=1), after=[t_b1r])
            bc_toks = []
            for m in range(4):
                k.op("pe", lambda h, m=m: h.matmul(
                    psT_[:, m:m + 1], vrow[0:1, m * 128:(m + 1) * 128],
                    c["ones_row_bf"][0:1, 0:1], start=(m == 0), stop=False,
                    skip_group_check=True),
                    after=[t_b1r, t_b3r, guard.get("psT")] + consts)
                bc_toks.append(k.op("pe", lambda h, m=m: h.matmul(
                    psT_[:, 4 + m:5 + m], vrow[0:1, 512 + m * 128:512 + (m + 1) * 128],
                    c["ones_row_bf"][0:1, 0:1], start=False, stop=(m == 3),
                    skip_group_check=True)))
            t_bc = k.op("dv", lambda h: h.tensor_copy(cols[:, 0:8], psT_[:, 0:8]),
                        after=[bc_toks[-1]])
            bias_user = t_bc
            guard["psTau"] = t_bc

            s_toks = []
            pend_tm = None

            def emit_tm(pend):
                (p_sq, p_m, p_kr, p_c0) = pend
                t_tm = k.op("pe", lambda h, m=p_m, kr=p_kr: h.matmul(
                    psT_[0:1, 0:512], ones_h[0:kr, 0:1], sqs[0:kr, 0:512],
                    start=(m == 0), stop=(m == 3), skip_group_check=True),
                    after=[p_sq, guard.get("psTau")])
                guard["sqs"] = t_tm
                if p_m == 3:
                    t_tau = k.op("ac", lambda h, c0=p_c0: h.activation(
                        tauones[0:1, c0:c0 + 512], psT_[0:1, 0:512], AF.Sqrt,
                        bias=c["cpool"][0:1, 2:3]), after=[t_tm] + consts)
                    guard["psTau"] = t_tau

            for tc in range(4):
                c0 = tc * 512
                for m in range(4):
                    kr = 128 if m < 3 else 127
                    b = m % 2
                    mm1, mm3 = [], []
                    for kk in range(NF):
                        aft1 = ([t_w1, guard.get(("U1", b))] if kk == 0 else [])
                        mm1.append(k.op("pe", lambda h, m=m, kr=kr, kk=kk, b=b, c0=c0,
                                        wb=wb: h.matmul(
                                            psU1[b][0:kr, 0:512],
                                            we1[wb][:, kk * 511 + m * 128:kk * 511 + m * 128 + kr],
                                            xfmh[:, kk * T + c0:kk * T + c0 + 512],
                                            start=(kk == 0), stop=(kk == NF - 1),
                                            skip_group_check=True), after=aft1))
                    for kk in range(NF):
                        aft3 = [t_w3, guard.get(("U3", b))] if kk == 0 else []
                        mm3.append(k.op("pe", lambda h, m=m, kr=kr, kk=kk, b=b, c0=c0,
                                        wb=wb: h.matmul(
                                            psU3[b][0:kr, 0:512],
                                            we3[wb][:, kk * 511 + m * 128:kk * 511 + m * 128 + kr],
                                            xfmh[:, kk * T + c0:kk * T + c0 + 512],
                                            start=(kk == 0), stop=(kk == NF - 1),
                                            skip_group_check=True), after=aft3))
                    if pend_tm is not None:
                        emit_tm(pend_tm)
                    t_si = k.op("ac", lambda h, m=m, kr=kr, b=b: h.activation(
                        silu_sb[0:kr, b * 512:b * 512 + 512],
                        psU1[b][0:kr, 0:512],
                        AF.Silu, bias=cols[0:kr, m:m + 1]),
                        after=[mm1[-1], t_bc, s_toks[-2] if len(s_toks) >= 2 else None])
                    guard[("U1", b)] = t_si
                    t_s = k.op("dv", lambda h, m=m, kr=kr, b=b, c0=c0: h.scalar_tensor_tensor(
                        sbf[0:kr, m * T + c0:m * T + c0 + 512],
                        psU3[b][0:kr, 0:512],
                        cols[0:kr, 4 + m:5 + m], silu_sb[0:kr, b * 512:b * 512 + 512],
                        ALU.add, ALU.mult), after=[mm3[-1], t_si, t_bc])
                    guard[("U3", b)] = t_s
                    s_toks.append(t_s)
                    t_sq = k.op("dv", lambda h, m=m, kr=kr, c0=c0: h.tensor_tensor(
                        sqs[0:kr, 0:512],
                        sbf[0:kr, m * T + c0:m * T + c0 + 512],
                        sbf[0:kr, m * T + c0:m * T + c0 + 512],
                        ALU.mult), after=[t_s, guard.get("sqs")])
                    pend_tm = (t_sq, m, kr, c0)
            emit_tm(pend_tm)
            pend_tm = None
            wA_user[e] = ("pe", k.cnt["pe"])
            ready = t_wall + [("dv", k.cnt["dv"]), ("ac", k.cnt["ac"])]

            def mm_emit(tt, aft0, ready=ready):
                half_toks = []
                for half in range(2):
                    ncols = 512 if half == 0 else 511
                    i = 0
                    for kk in range(4):
                        kr = 128 if kk < 3 else 127
                        aft = (ready + [aft0[half]]) if i == 0 else []
                        k.op("pe", lambda h, tt=tt, half=half, kk=kk, kr=kr,
                             i=i, ncols=ncols, wb=wb: h.matmul(
                                 psW2[:, half * 512:half * 512 + ncols],
                                 sbf[0:kr, kk * T + tt * 128:kk * T + (tt + 1) * 128],
                                 we2[wb][0:kr, kk * 1023 + half * 512:kk * 1023 + half * 512 + ncols],
                                 start=(i == 0), stop=False,
                                 skip_group_check=True),
                             after=aft)
                        i += 1
                    tok = k.op("pe", lambda h, tt=tt, half=half, ncols=ncols,
                               wb=wb: h.matmul(
                                   psW2[:, half * 512:half * 512 + ncols],
                                   tauones[0:2, tt * 128:(tt + 1) * 128],
                                   taub2[wb][0:2, half * 512:half * 512 + ncols],
                                   start=False, stop=True,
                                   skip_group_check=True))
                    half_toks.append(tok)
                return half_toks

            nxt = (wteT[:, (e + 1) * 16:(e + 2) * 16] if e + 1 < NE
                   else ("shared" if B3_SH else None))
            lr_last = lres_phaseB(mm_emit, nxt)
            w_user[e] = ("pe", k.cnt["pe"])
            guard["psT"] = guard["psTau"]
        k.join()
        if os.environ.get("B3_DEBUG"):
            for nm, tens in [("d_y0", ytm[0]), ("d_y1", ytm[1]), ("d_wte", wte),
                             ("d_tau", tauones), ("d_sbf", sbf), ("d_we2", we2[0]),
                             ("d_taub2", taub2[0])]:
                dt_ = {"d_y0": F32, "d_y1": F32, "d_wte": F32, "d_tau": F16,
                       "d_sbf": F16, "d_we2": F16, "d_taub2": F16}[nm]
                o = k.nc.dram_tensor(nm, [tens.shape[0], tens.shape[1]], dt_,
                                     kind="ExternalOutput").ap()
                k.dma("qs", "q_dbg", o, tens[:])
            k.join()
        k._scopes.pop()
        es_exp.close()

        if B3_SH:
         with k.scope():
             ws1 = k.sb("b3_ws1", [P, 8 * 1023], F16)
             ws3 = k.sb("b3_ws3", [P, 8 * 1023], F16)
             ws2 = k.sb("b3_ws2", [P, 8 * 1023], F16)
             aftw = [w_user.get(NE - 1)]
             aftwA = [wA_user.get(NE - 1)]
             t_w1 = k.dma("qg", "q_b3s0", ws1[:].rearrange("p (a m) -> p a m", m=1023),
                          w["Ws1T"].rearrange("(a p) m -> p a m", p=128), after=aftwA)
             t_w3 = k.dma("qg", "q_b3s1", ws3[:].rearrange("p (a m) -> p a m", m=1023),
                          w["Ws3T"].rearrange("(a p) m -> p a m", p=128), after=aftwA)
             b_sh = NE % 2
             t_tb1 = k.dma("qg", "q_b3s4", taub2[b_sh][0:1, 0:1023], w["Ws2T"][0:1],
                           after=[w_user.get(NE - 2)])
             t_tb2 = k.dma("qg", "q_b3s5", taub2[b_sh][1:2, 0:1023],
                           row1(w["bs2"]), after=[w_user.get(NE - 2)])
             t_b1r = k.dma("qg", "q_b3b", vrow[0:1, 0:1023], row1(w["bs1"]),
                           after=[bias_user])
             bct = []
             for m in range(8):
                 bct.append(k.op("pe", lambda h, m=m: h.matmul(
                     psT_[:, m:m + 1], vrow[0:1, m * 128:(m + 1) * 128],
                     c["ones_row_bf"][0:1, 0:1], start=(m == 0), stop=(m == 7),
                     skip_group_check=True),
                     after=[t_b1r, guard.get("psT")] + consts))
             t_bc1 = k.op("dv", lambda h: h.tensor_copy(cols[:, 0:8], psT_[:, 0:8]),
                          after=[bct[-1]])
             t_w2a = k.dma("qg", "q_b3s2", ws2[0:128, 0:7 * 1023].rearrange(
                 "p (a m) -> p a m", m=1023),
                 w["Ws2T"][1:897].rearrange("(a p) m -> p a m", p=128), after=aftw)
             t_w2b = k.dma("qg", "q_b3s3", ws2[0:127, 7 * 1023:8 * 1023],
                           w["Ws2T"][897:1024], after=aftw)
             t_wall = [t_w2a, t_w2b, t_tb1, t_tb2]
             t_b3r = k.dma("qg", "q_b3b", vrow[0:1, 0:1023], row1(w["bs3"]),
                           after=[bct[-1]])
             bct = []
             for m in range(8):
                 bct.append(k.op("pe", lambda h, m=m: h.matmul(
                     psT_[:, 8 + m:9 + m], vrow[0:1, m * 128:(m + 1) * 128],
                     c["ones_row_bf"][0:1, 0:1], start=(m == 0), stop=(m == 7),
                     skip_group_check=True), after=[t_b3r, t_bc1]))
             t_bc3 = k.op("dv", lambda h: h.tensor_copy(cols[:, 8:16], psT_[:, 8:16]),
                          after=[bct[-1]])
             guard["psTau"] = t_bc3

             s2b = k.sb("b3_s2b", [P, 4 * T], F16)

             def stile(m):
                 return (sbf, m) if m < 4 else (s2b, m - 4)

             sq_toks, s_toks = [], []
             pend_tm = None

             def emit_tm_sh(pend):
                 (p_sq, p_m, p_kr, p_c0) = pend
                 t_tm = k.op("pe", lambda h, m=p_m, kr=p_kr: h.matmul(
                     psT_[0:1, 0:512], ones_h[0:kr, 0:1], sqs[0:kr, 0:512],
                     start=(m == 0), stop=(m == 7), skip_group_check=True),
                     after=[p_sq, guard.get("psTau")])
                 guard["sqs"] = t_tm
                 if p_m == 7:
                     t_tau = k.op("ac", lambda h, c0=p_c0: h.activation(
                         tauones[0:1, c0:c0 + 512], psT_[0:1, 0:512], AF.Sqrt,
                         bias=c["cpool"][0:1, 2:3]), after=[t_tm] + consts)
                     guard["psTau"] = t_tau

             for tc in range(4):
                 c0 = tc * 512
                 for m in range(8):
                     kr = 128 if m < 7 else 127
                     b = m % 2
                     mm1, mm3 = [], []
                     for kk in range(NF):
                         aft1 = ([t_w1,
                                  guard.get(("U1", b))] if kk == 0 else [])
                         mm1.append(k.op("pe", lambda h, m=m, kr=kr, kk=kk, b=b, c0=c0:
                                         h.matmul(
                                             psU1[b][0:kr, 0:512],
                                             ws1[:, kk * 1023 + m * 128:kk * 1023 + m * 128 + kr],
                                             xfmh[:, kk * T + c0:kk * T + c0 + 512],
                                             start=(kk == 0), stop=(kk == NF - 1),
                                             skip_group_check=True), after=aft1))
                     for kk in range(NF):
                         aft3 = [t_w3, guard.get(("U3", b))] if kk == 0 else []
                         mm3.append(k.op("pe", lambda h, m=m, kr=kr, kk=kk, b=b, c0=c0:
                                         h.matmul(
                                             psU3[b][0:kr, 0:512],
                                             ws3[:, kk * 1023 + m * 128:kk * 1023 + m * 128 + kr],
                                             xfmh[:, kk * T + c0:kk * T + c0 + 512],
                                             start=(kk == 0), stop=(kk == NF - 1),
                                             skip_group_check=True), after=aft3))
                     if pend_tm is not None:
                         emit_tm_sh(pend_tm)
                     t_si = k.op("ac", lambda h, m=m, kr=kr, b=b: h.activation(
                         silu_sb[0:kr, 0:512],
                         psU1[b][0:kr, 0:512],
                         AF.Silu, bias=cols[0:kr, m:m + 1]),
                         after=[mm1[-1], t_bc1, s_toks[-1] if s_toks else None])
                     guard[("U1", b)] = t_si
                     st, sm = stile(m)
                     t_s = k.op("dv", lambda h, m=m, kr=kr, b=b, c0=c0, st=st, sm=sm:
                                h.scalar_tensor_tensor(
                                    st[0:kr, sm * T + c0:sm * T + c0 + 512],
                                    psU3[b][0:kr, 0:512],
                                    cols[0:kr, 8 + m:9 + m], silu_sb[0:kr, 0:512],
                                    ALU.add, ALU.mult), after=[mm3[-1], t_si, t_bc3])
                     guard[("U3", b)] = t_s
                     s_toks.append(t_s)
                     sq_toks.append(k.op("dv", lambda h, m=m, kr=kr, c0=c0, st=st,
                                         sm=sm: h.tensor_tensor(
                         sqs[0:kr, 0:512], st[0:kr, sm * T + c0:sm * T + c0 + 512],
                         st[0:kr, sm * T + c0:sm * T + c0 + 512],
                         ALU.mult), after=[t_s, guard.get("sqs")]))
                     pend_tm = (sq_toks[-1], m, kr, c0)
             emit_tm_sh(pend_tm)
             pend_tm = None
             ready = t_wall + [("dv", k.cnt["dv"]), ("ac", k.cnt["ac"])]

             def mm_emit_sh(tt, aft0, ready=ready):
                 half_toks = []
                 for half in range(2):
                     ncols = 512 if half == 0 else 511
                     i = 0
                     for kk in range(8):
                         kr = 128 if kk < 7 else 127
                         st, sm = stile(kk)
                         aft = (ready + [aft0[half]]) if i == 0 else []
                         k.op("pe", lambda h, tt=tt, half=half, kk=kk, kr=kr,
                              i=i, ncols=ncols, st=st, sm=sm: h.matmul(
                                  psW2[:, half * 512:half * 512 + ncols],
                                  st[0:kr, sm * T + tt * 128:sm * T + (tt + 1) * 128],
                                  ws2[0:kr, kk * 1023 + half * 512:kk * 1023 + half * 512 + ncols],
                                  start=(i == 0), stop=False,
                                  skip_group_check=True),
                              after=aft)
                         i += 1
                     tok = k.op("pe", lambda h, tt=tt, half=half, ncols=ncols:
                                h.matmul(
                                    psW2[:, half * 512:half * 512 + ncols],
                                    tauones[0:2, tt * 128:(tt + 1) * 128],
                                    taub2[b_sh][0:2, half * 512:half * 512 + ncols],
                                    start=False, stop=True,
                                    skip_group_check=True))
                     half_toks.append(tok)
                 return half_toks

             lr_last = lres_phaseB(mm_emit_sh, None)
             k.join()

        t_rs = k.op("dv", lambda h: h.reciprocal(tmpA[:], colS[:]), after=[lr_last])
        for tt in range(16):
            t1 = k.op("dv", lambda h, tt=tt: h.tensor_copy(
                ytm[tt][:, 0:1], colU0[:, tt:tt + 1]), after=[t_rs])
            k.op("dv", lambda h, tt=tt: h.tensor_scalar_mul(
                ytm[tt][:], ytm[tt][:], tmpA[:, tt:tt + 1]), after=[t1])
        k.join()

        with k.scope():
            stg = [k.sb(f"b3_stg{i}", [P, 1024], F32) for i in range(2)]
            xldg = [k.sb(f"b3_xld{i}", [P, 1024], F32) for i in range(2)]
            st_toks = []
            dma_prev = {}
            xread_prev = {}
            xdma = {}
            for tt in range(2):
                xdma[tt] = k.dma("qs", f"q_xre{tt % 2}", xldg[tt % 2][:],
                                 x_in_ap[tt * 128:(tt + 1) * 128, :], after=[])
            prev_qq = None
            prev_o_par = {}
            for tt in range(16):
                b = tt % 2
                cp = tmpB if b == 0 else tmpC
                t_y = k.op("dv", lambda h, tt=tt, b=b: h.tensor_tensor(
                    ytm[tt][:], ytm[tt][:], xldg[b][:], ALU.add),
                    after=[xdma[tt]])
                xread_prev[b] = t_y
                if tt + 2 < 16:
                    xdma[tt + 2] = k.dma("qs", f"q_xre{b}", xldg[b][:],
                                         x_in_ap[(tt + 2) * 128:(tt + 3) * 128, :],
                                         after=[t_y])
                t_m = k.op("dv", lambda h, tt=tt, cp=cp: h.reduce_sum(
                    cp[:, 0:1], ytm[tt][:], axis=AX.X),
                    after=[t_y, prev_o_par.get(b)])
                t_qq = k.op("ac", lambda h, tt=tt, cp=cp: h.activation(
                    zsq[:, 0:1024], ytm[tt][:], AF.Square, accum_out=cp[:, 1:2]),
                    after=[t_y, prev_qq, dma_prev.get(b), prev_o_par.get(b)])
                prev_qq = t_qq
                t_m2 = k.op("dv", lambda h, cp=cp: h.tensor_scalar_mul(
                    cp[:, 0:1], cp[:, 0:1], 1.0 / D), after=[t_m])
                t_v = k.op("ac", lambda h, cp=cp: h.activation(
                    cp[:, 2:3], cp[:, 0:1], AF.Square), after=[t_m2])
                t_v = k.op("dv", lambda h, cp=cp: h.scalar_tensor_tensor(
                    cp[:, 3:4], cp[:, 1:2], 1.0 / D, cp[:, 2:3],
                    ALU.mult, ALU.subtract), after=[t_qq, t_v])
                t_v = k.op("ac", lambda h, cp=cp: h.activation(
                    cp[:, 4:5], cp[:, 3:4], AF.Sqrt, bias=c["ccols"][:, 2:3]),
                    after=[t_v] + consts)
                t_v = k.op("dv", lambda h, cp=cp: h.reciprocal(cp[:, 4:5], cp[:, 4:5]),
                           after=[t_v])
                t_o = k.op("dv", lambda h, tt=tt, b=b, cp=cp: h.tensor_scalar(
                    stg[b][:], ytm[tt][:], cp[:, 0:1], cp[:, 4:5],
                    ALU.subtract, ALU.mult), after=[t_v, dma_prev.get(b)])
                tok = k.dma("qs", f"q_out{b}", x_out_ap[tt * 128:(tt + 1) * 128, :],
                            stg[b][:], after=[t_o])
                dma_prev[b] = tok
                prev_o_par[b] = t_o
                st_toks.append(tok)
        return st_toks




B3_SHAPES = {
    "gwT": ([D, 16], F32), "We1T": ([16, D, 511], F32), "be1": ([16, 511], F32),
    "We3T": ([16, D, 511], F32), "be3": ([16, 511], F32),
    "We2T": ([16, 512, 1023], F32), "be2": ([16, 1023], F32),
    "Ws1T": ([D, 1023], F32), "bs1": ([1023], F32),
    "Ws3T": ([D, 1023], F32), "bs3": ([1023], F32),
    "Ws2T": ([D, 1023], F32), "bs2": ([1023], F32),
}


def build_b3_program():
    k = K()
    nc = k.nc
    x_in = nc.dram_tensor("x_in", [T, D], F32, kind="ExternalInput").ap()
    w = {n: nc.dram_tensor("w_" + n, B3_SHAPES[n][0], B3_SHAPES[n][1],
                           kind="ExternalInput").ap() for n in B3_SHAPES}
    x_out = nc.dram_tensor("x_out", [T, D], F32, kind="ExternalOutput").ap()
    with k.es:
        k.open()
        c = emit_consts(k)
        xfmh = k.sb("b3_xfmh", [P, NF * T], F16)
        st_toks = emit_b3(k, c, x_in, xfmh, w, x_out)
        k.finish(st_toks)
        k.replay()
    return k.nc


def _b3_inputs(inputs, ci, x_cur):
    gf = lambda n: np.asarray(inputs["L1_ffn_" + n], dtype=np.float32)
    return {
        **({"x_in": np.ascontiguousarray(np.asarray(x_cur, dtype=np.float32))}
           if x_cur is not None else {}),
        "w_gwT": np.ascontiguousarray(gf("gate_w").T),
        "w_We1T": np.ascontiguousarray(np.transpose(gf("We1"), (0, 2, 1))),
        "w_be1": np.ascontiguousarray(gf("be1")),
        "w_We3T": np.ascontiguousarray(np.transpose(gf("We3"), (0, 2, 1))),
        "w_be3": np.ascontiguousarray(gf("be3")),
        "w_We2T": np.ascontiguousarray(np.transpose(gf("We2"), (0, 2, 1))),
        "w_be2": np.ascontiguousarray(gf("be2")),
        "w_Ws1T": np.ascontiguousarray(gf("Ws1").T),
        "w_bs1": np.ascontiguousarray(gf("bs1")),
        "w_Ws3T": np.ascontiguousarray(gf("Ws3").T),
        "w_bs3": np.ascontiguousarray(gf("bs3")),
        "w_Ws2T": np.ascontiguousarray(gf("Ws2").T),
        "w_bs2": np.ascontiguousarray(gf("bs2")),
    }



B0_SHAPES = {
    "conv_w": [D, 7], "norm1_w": [D], "norm2_w": [D], "Wdecay": [D, D],
    "bdecay": [D], "Wgate": [D, D], "bgate": [D], "state_scale": [1],
    "Wk": [D, 256], "M_T": [256, 256], "Wmem": [256, D], "mem_scale": [1],
    "Wc": [3 * D, D], "bc": [D], "Wup": [D, 4096], "Wg": [D, 4096],
    "Wdown": [4096, D],
}


import os
B0_STAGE = int(os.environ.get("B0_STAGE", "2"))


def build_b0_program():
    k = K()
    nc = k.nc
    x_in = nc.dram_tensor("x_in", [T, D], F32, kind="ExternalInput").ap()
    w = {n: nc.dram_tensor("w_" + n, B0_SHAPES[n], F32, kind="ExternalInput").ap()
         for n in B0_SHAPES}
    mx_kind = "ExternalOutput" if os.environ.get("B0_DUMP_MX") else "Internal"
    w["mx_scr"] = nc.dram_tensor("mx_scr", [P, NF, T], F32, kind=mx_kind).ap()
    x_out = nc.dram_tensor("x_out", [T, D], F32, kind="ExternalOutput").ap()

    with k.es:
        k.open()
        c = emit_consts(k)
        xfm = [k.sb(f"xfm{f}", [P, T], F32) for f in range(NF)]
        emit_load_stream_fm(k, c, x_in, xfm)
        if B0_STAGE >= 1:
            emit_b0(k, c, xfm, w)
        out_toks = emit_store_stream_fm(k, c, xfm, x_out)
        k.finish(out_toks)
        k.replay()
    return k.nc


def _b0_inputs(inputs, ci, x_cur):
    g = lambda n: np.ascontiguousarray(np.asarray(inputs["L0_attn_" + n], dtype=np.float32))
    return {
        "x_in": np.ascontiguousarray(np.asarray(x_cur, dtype=np.float32)),
        "w_conv_w": g("conv_w"), "w_norm1_w": g("norm1_w"), "w_norm2_w": g("norm2_w"),
        "w_Wdecay": g("Wdecay"), "w_bdecay": g("bdecay"), "w_Wgate": g("Wgate"),
        "w_bgate": g("bgate"),
        "w_state_scale": g("state_scale").reshape(1),
        "w_Wk": g("Wk"), "w_M_T": np.ascontiguousarray(g("M").T), "w_Wmem": g("Wmem"),
        "w_mem_scale": g("mem_scale").reshape(1),
        "w_Wc": g("Wc"), "w_bc": g("bc"), "w_Wup": g("Wup"), "w_Wg": g("Wg"),
        "w_Wdown": g("Wdown"),
    }


def build_full_program():
    k = K()
    nc = k.nc
    x_in = nc.dram_tensor("x_in", [T, D], F32, kind="ExternalInput").ap()
    w0 = {n: nc.dram_tensor("b0w_" + n, B0_SHAPES[n], F32, kind="ExternalInput").ap()
          for n in B0_SHAPES}
    w0["mx_scr"] = nc.dram_tensor("mx_scr", [P, NF, T], F32, kind="Internal").ap()
    w1 = {n: nc.dram_tensor("b1w_" + n, B1_SHAPES[n][0], B1_SHAPES[n][1],
                            kind="ExternalInput").ap() for n in B1_SHAPES}
    w2 = {n: nc.dram_tensor("b2w_" + n, B2_SHAPES[n], F32, kind="ExternalInput").ap()
          for n in B2_SHAPES}
    w3 = {n: nc.dram_tensor("b3w_" + n, B3_SHAPES[n][0], B3_SHAPES[n][1],
                            kind="ExternalInput").ap() for n in B3_SHAPES}
    str_scr = nc.dram_tensor("str_scr", [T, D], F32, kind="Internal").ap()
    x_out = nc.dram_tensor("x_out", [T, D], F32, kind="ExternalOutput").ap()
    with k.es:
        k.open()
        c = emit_consts(k)
        with k.scope():
            xfm = [k.sb(f"xfm{f}", [P, T], F32) for f in range(NF)]
            emit_load_stream_fm(k, c, x_in, xfm)
            with k.scope():
                emit_b0(k, c, xfm, w0)
            with k.scope():
                emit_b1(k, c, xfm, w1)
            with k.scope():
                emit_b2(k, c, xfm, w2)
            seam_toks = emit_store_stream_fm(k, c, xfm, str_scr)
        k.wait("qs", *seam_toks)
        k.join()
        xfmh = k.sb("b3_xfmh", [P, NF * T], F16)
        st_toks = emit_b3(k, c, str_scr, xfmh, w3, x_out)
        k.finish(st_toks)
        k.replay()
    return k.nc


def _full_inputs(inputs, ci):
    m = {"x_in": np.ascontiguousarray(
        np.asarray(inputs["L0_attn_x"][ci], dtype=np.float32))}
    d0 = _b0_inputs(inputs, ci, inputs["L0_attn_x"][ci])
    d1 = _b1_inputs(inputs, ci, None)
    d2 = _b2_inputs(inputs, ci, None)
    d3 = _b3_inputs(inputs, ci, None)
    for dd, pref in [(d0, "b0w_"), (d1, "b1w_"), (d2, "b2w_"), (d3, "b3w_")]:
        for k2, v in dd.items():
            if k2 == "x_in":
                continue
            assert k2.startswith("w_")
            m[pref + k2[2:]] = v
    return m


def kernel(**inputs):
    import os
    if os.environ.get("KMODE") == "unfused":
        x = np.asarray(inputs["L0_attn_x"], dtype=np.float32)
        cur = [np.ascontiguousarray(x[i]) for i in range(NCORES)]
        progs = [
            (build_b0_program, _b0_inputs),
            (build_b1_program, _b1_inputs),
            (build_b2_program, _b2_inputs),
            (build_b3_program, _b3_inputs),
        ]
        for build, mkin in progs:
            nc = build()
            in_maps = [mkin(inputs, i, cur[i]) for i in range(NCORES)]
            res = run_bass_kernel_spmd(nc, in_maps, core_ids=list(range(NCORES)))
            cur = [np.asarray(res.results[i]["x_out"]) for i in range(NCORES)]
        return np.stack(cur, axis=0)
    nc = build_full_program()
    in_maps = [_full_inputs(inputs, i) for i in range(NCORES)]
    res = run_bass_kernel_spmd(nc, in_maps, core_ids=list(range(NCORES)))
    return np.stack([np.asarray(res.results[i]["x_out"]) for i in range(NCORES)],
                    axis=0)
